# version 16
# baseline (speedup 1.0000x reference)
import numpy as np
from contextlib import ExitStack

import concourse.bass as bass
import concourse.mybir as mybir
from concourse.bass_utils import run_bass_kernel_spmd

F32 = mybir.dt.float32
BF16 = mybir.dt.bfloat16
AF = mybir.ActivationFunctionType
OP = mybir.AluOpType
AX = mybir.AxisListType

D = 2048
SQ = 2048
TOWN = 1024
H = 16
FI = 5632
NDC = 16
EPS = 1e-5
INV_D = 1.0 / D

_S128 = float(np.float32(np.sqrt(np.float32(128.0))))
CQ = float(np.float32(128.0 ** -0.25))
_CK = (1.0 / _S128) / CQ
CK1 = float(np.float32(_CK))
CK2 = float(np.float32(_CK - CK1))

ENGS = ("t", "v", "s", "g", "y")


def _rope_tables():
    import jax
    import jax.numpy as jnp
    rot = 64
    with jax.default_device(jax.devices("cpu")[0]):
        inv_freq = 1.0 / 10000.0 ** (jnp.arange(0, rot, 2, dtype=jnp.float32) / rot)
        ang = jnp.arange(SQ, dtype=jnp.float32)[:, None] * inv_freq[None, :]
        cos = np.ascontiguousarray(np.asarray(jnp.cos(ang)).T).astype(np.float32)
        sin = np.ascontiguousarray(np.asarray(jnp.sin(ang)).T).astype(np.float32)
    return cos, sin


def _build():
    nc = bass.Bass("TRN2", target_bir_lowering=False)

    def din(name, shape, dt=F32):
        return nc.dram_tensor(name, shape, dt, kind="ExternalInput").ap()

    x_own = din("x_own", [D, TOWN])
    x_oth = din("x_oth", [D, TOWN])
    w_in = din("w_in", [D, 3 * D])
    w_out = din("w_out", [D, D])
    gate_wt = din("gate_wt", [D, FI])
    up_wt = din("up_wt", [D, FI])
    down_wt = din("down_wt", [FI, D])
    lnw_i = din("lnw", [128, 16])
    lnb_i = din("lnb", [128, 16])
    gs_i = din("gate_s", [128, 44])
    us_i = din("up_s", [128, 44])
    ds_i = din("down_s", [128, 16])
    cosqd_i = din("cosqd", [64, TOWN])
    sinqd_i = din("sinqd", [64, TOWN])
    coskd_i = din("coskd", [64, SQ])
    sinkd_i = din("sinkd", [64, SQ])
    eye_i = din("eye", [128, 128], BF16)
    ones_i = din("ones1", [1, 128], BF16)
    onesc_i = din("onesc", [128, 1], BF16)

    out_fm = nc.dram_tensor("out_fm", [D, TOWN], F32, kind="ExternalOutput").ap()

    q_hi_s = nc.dram_tensor("q_hi_s", [H, 128, TOWN], BF16).ap()
    q_lo_s = nc.dram_tensor("q_lo_s", [H, 128, TOWN], BF16).ap()
    k_hi_s = nc.dram_tensor("k_hi_s", [H, 128, SQ], BF16).ap()
    k_lo_s = nc.dram_tensor("k_lo_s", [H, 128, SQ], BF16).ap()
    v_hi_s = nc.dram_tensor("v_hi_s", [H, 128, SQ], BF16).ap()
    v_lo_s = nc.dram_tensor("v_lo_s", [H, 128, SQ], BF16).ap()
    ao_s = nc.dram_tensor("ao_s", [D, TOWN], F32).ap()

    es = ExitStack()
    with es:
        SEMH = {}
        semnames = (["t", "v", "s", "g", "dC", "dH0", "dH1"]
                    + [f"dW{i}" for i in range(4)] + [f"dX{i}" for i in range(4)]
                    + [f"dE{i}" for i in range(4)] + [f"dY{i}" for i in range(6)])
        for nm in semnames:
            SEMH[nm] = es.enter_context(nc.semaphore("sem_" + nm))
        cnt = {k: 0 for k in SEMH}
        prog = {e: [] for e in ENGS}

        def op(e, fn, waits=(), inc=None, inck=1):
            waits = list(waits)
            if e in ("v", "s", "g"):
                waits.append((e, cnt[e]))
            w = tuple((s_, v_) for (s_, v_) in waits if v_ > 0)
            val = None
            if inc is not None:
                cnt[inc] += inck
                val = cnt[inc]

            def run(eng, fn=fn, w=w, inc=inc, inck=inck):
                for sname, v in w:
                    eng.wait_ge(SEMH[sname], v)
                inst = fn(eng)
                if inc is not None:
                    inst.then_inc(SEMH[inc], inck)
            prog[e].append(run)
            return val

        last_barrier = [()]

        def barrier():
            vals = {k: cnt[k] for k in SEMH}
            for e in ("t", "v", "s", "g"):
                w = [(o, v) for o, v in vals.items() if o != e and v > 0]
                op(e, lambda eng: eng.nop(), waits=w, inc=e)
            def run(eng, vals=dict(vals)):
                for o, v in vals.items():
                    if v > 0:
                        eng.wait_ge(SEMH[o], v)
            prog["y"].append(run)
            last_barrier[0] = tuple((o, v) for o, v in vals.items() if v > 0)

        def dma_in(dst, src, sem, waits=(), q="y"):
            waits = list(waits) + list(last_barrier[0])
            return op(q, lambda e, d=dst, s_=src: e.dma_start(out=d, in_=s_),
                      waits=waits, inc=sem, inck=16)

        def dma_out(dst, src, sem, waits=()):
            waits = list(waits) + list(last_barrier[0])
            return op("g", lambda e, d=dst, s_=src: e.dma_start(out=d, in_=s_),
                      waits=waits, inc=sem, inck=16)

        sb = lambda n, sh, dt: es.enter_context(nc.sbuf_tensor(n, sh, dt))
        pt_ = lambda n, sh, dt: es.enter_context(nc.psum_tensor(n, sh, dt))

        LNW = sb("LNW", [128, 16], F32)
        LNB = sb("LNB", [128, 16], F32)
        GSC = sb("GSC", [128, 44], F32)
        USC = sb("USC", [128, 44], F32)
        DSC = sb("DSC", [128, 16], F32)
        EYE = sb("EYE", [128, 128], BF16)
        ONES = sb("ONES", [1, 128], BF16)
        ONESC = sb("ONESC", [128, 1], BF16)
        ROW = [sb(f"ROW{i}", [1, 1024], F32) for i in range(8)]
        rA, rB, rC, rD, rE, rF, rG, rH = ROW
        RMH = sb("RMH", [1, 2048], BF16)
        RML = sb("RML", [1, 2048], BF16)
        RN = sb("RN", [1, 32], F32)
        NEGB = sb("NEGB", [128, 2], F32)
        Z = sb("Z", [128, 2048], F32)
        SRH2 = sb("SRH2", [128, 512], BF16)

        PT = [pt_(f"PT{i}", [128, 512], F32) for i in range(3)]
        PTB = pt_("PTB", [128, 1024], BF16)
        PR = [pt_(f"PR{i}", [1, 512], F32) for i in range(4)]
        pt_free = [("t", 0)] * 3
        pr_free = [("t", 0)] * 4

        def pt_acquire(i, waits):
            return list(waits) + [pt_free[i]]

        for dst, src in ((LNW, lnw_i), (LNB, lnb_i), (GSC, gs_i), (USC, us_i),
                         (DSC, ds_i), (EYE, eye_i), (ONES, ones_i), (ONESC, onesc_i)):
            dma_in(dst[:], src[:], "dC")
        ld_const = cnt["dC"]

        def row_sqrt(dst, src, waits):
            return op("s", lambda e, d=dst, s_=src: e.activation(d, s_, AF.Sqrt),
                      waits=waits, inc="s")

        def bcast512(dst, rmh_sl, rml_sl, pti, waits):
            w = pt_acquire(pti, waits)
            op("t", lambda e, p=PT[pti], r=rmh_sl: e.matmul(p[:, 0:512], ONES[:], r,
                                                            start=True, stop=False),
               waits=w)
            tev = op("t", lambda e, p=PT[pti], r=rml_sl: e.matmul(p[:, 0:512], ONES[:],
                                                                  r, start=False,
                                                                  stop=True), inc="t")
            vev = op("v", lambda e, d=dst, p=PT[pti]: e.tensor_copy(d, p[:, 0:512]),
                     waits=[("t", tev)], inc="v")
            pt_free[pti] = ("v", vev)
            return vev

        def row_pair(rmh_sl, rml_sl, src, waits):
            h = op("v", lambda e, d=rmh_sl, s_=src: e.tensor_copy(d, s_),
                   waits=waits, inc="v")
            l = op("v", lambda e, d=rml_sl, s_=src, hh=rmh_sl:
                   e.tensor_tensor(d, s_, hh, op=OP.subtract), waits=[("v", h)],
                   inc="v")
            return h, l

        def ln_rows(pr_sum, pr_sq, n_tt, waits, s_dst=None, ms_dst=None):
            evs = []
            for tt in range(n_tt):
                a = op("v", lambda e, tt=tt, p=PR[pr_sum + 2 * tt]:
                       e.tensor_scalar(rA[0:1, tt * 512:(tt + 1) * 512], p[:], INV_D,
                                       None, op0=OP.mult), waits=waits, inc="v")
                b_ = op("v", lambda e, tt=tt, p=PR[pr_sq + 2 * tt]:
                        e.tensor_scalar(rB[0:1, tt * 512:(tt + 1) * 512], p[:], INV_D,
                                        None, op0=OP.mult), inc="v")
                pr_free[pr_sum + 2 * tt] = ("v", a)
                pr_free[pr_sq + 2 * tt] = ("v", b_)
                evs += [a, b_]
            n = n_tt * 512
            c1 = op("v", lambda e, n=n: e.tensor_tensor(rC[0:1, 0:n], rA[0:1, 0:n],
                                                        rA[0:1, 0:n], op=OP.mult),
                    waits=[("v", max(evs))], inc="v")
            c2 = op("v", lambda e, n=n: e.tensor_tensor(rB[0:1, 0:n], rB[0:1, 0:n],
                                                        rC[0:1, 0:n], op=OP.subtract),
                    waits=[("v", c1)], inc="v")
            c3 = op("v", lambda e, n=n: e.tensor_scalar(rC[0:1, 0:n], rB[0:1, 0:n],
                                                        EPS, None, op0=OP.add),
                    waits=[("v", c2)], inc="v")
            c4 = row_sqrt(rD[0:1, 0:n], rC[0:1, 0:n], [("v", c3)])
            c5 = op("v", lambda e, n=n: e.reciprocal(rC[0:1, 0:n], rD[0:1, 0:n]),
                    waits=[("s", c4)], inc="v")
            out = [c5]
            if s_dst is not None:
                out.append(op("v", lambda e, n=n, d=s_dst: e.tensor_copy(
                    d[0:1, 0:n], rC[0:1, 0:n]), waits=[("v", c5)], inc="v"))
            if ms_dst is not None:
                out.append(op("v", lambda e, n=n, d=ms_dst: e.tensor_tensor(
                    d[0:1, 0:n], rA[0:1, 0:n], rC[0:1, 0:n], op=OP.mult),
                    waits=[("v", c5)], inc="v"))
            return max(out)

        p1s = ExitStack()
        CQD = p1s.enter_context(nc.sbuf_tensor("CQD", [64, TOWN], F32))
        SQD = p1s.enter_context(nc.sbuf_tensor("SQD", [64, TOWN], F32))
        CKD = p1s.enter_context(nc.sbuf_tensor("CKD", [64, SQ], F32))
        SKD = p1s.enter_context(nc.sbuf_tensor("SKD", [64, SQ], F32))
        XSH = p1s.enter_context(nc.sbuf_tensor("XSH", [128, 16 * 1024], BF16))
        XSL = p1s.enter_context(nc.sbuf_tensor("XSL", [128, 16 * 1024], BF16))
        WF = p1s.enter_context(nc.sbuf_tensor("WF", [128, 4 * 512], F32))
        WH = p1s.enter_context(nc.sbuf_tensor("WH", [128, 2 * 16 * 256], BF16))
        WL = p1s.enter_context(nc.sbuf_tensor("WL", [128, 2 * 16 * 256], BF16))
        XT = p1s.enter_context(nc.sbuf_tensor("XT", [128, 2048], F32))
        YB = p1s.enter_context(nc.sbuf_tensor("YB", [128, 5 * 512], F32))
        EH = p1s.enter_context(nc.sbuf_tensor("EH", [128, 3 * 512], BF16))
        EL = p1s.enter_context(nc.sbuf_tensor("EL", [128, 3 * 512], BF16))

        dma_in(CQD[:], cosqd_i[:], "dC")
        dma_in(SQD[:], sinqd_i[:], "dC")
        dma_in(CKD[:], coskd_i[:], "dC")
        e4 = dma_in(SKD[:], sinkd_i[:], "dC")
        for TBL in (CKD, SKD):
            a = op("v", lambda e, t=TBL: e.tensor_scalar(Z[0:64, :], t[:], CK2, None,
                                                         op0=OP.mult),
                   waits=[("dC", e4)], inc="v")
            b_ = op("v", lambda e, t=TBL: e.tensor_scalar(t[:], t[:], CK1, None,
                                                          op0=OP.mult),
                    waits=[("v", a)], inc="v")
            op("v", lambda e, t=TBL: e.tensor_tensor(t[:], t[:], Z[0:64, :],
                                                     op=OP.add),
               waits=[("v", b_)], inc="v")
        tbl_ev = cnt["v"]

        yb_state = {"i": 0, "free": [("v", 0)] * 5}

        def yb_slot():
            j = yb_state["i"] % 5
            yb_state["i"] += 1
            return j, YB[:, j * 512:(j + 1) * 512]

        eh_state = {"i": 0, "free": [("dE0", 0)] * 3}

        def eh_slot():
            j = eh_state["i"] % 3
            eh_state["i"] += 1
            return j, EH[:, j * 512:(j + 1) * 512], EL[:, j * 512:(j + 1) * 512]

        def stats_dc(xv, src_waits, dc, tev_hist):
            j = dc % 2
            srh = XSH[:, j * 2048:j * 2048 + 1024]
            sqh = XSH[:, j * 2048 + 1024:(j + 1) * 2048]
            srl = XSL[:, j * 2048:j * 2048 + 1024]
            sql = XSL[:, j * 2048 + 1024:(j + 1) * 2048]
            zq = Z[:, j * 1024:(j + 1) * 1024]
            guard = [("t", tev_hist.get(dc - 2, 0))]
            hh = op("v", lambda e, s_=xv, d=srh: e.tensor_copy(d, s_),
                    waits=list(src_waits) + guard, inc="v")
            ll = op("g", lambda e, s_=xv, d=srl, dh=srh: e.tensor_tensor(
                d, s_, dh, op=OP.subtract), waits=[("v", hh)], inc="g")
            sq = op("v", lambda e, s_=xv, z=zq: e.tensor_tensor(z, s_, s_,
                                                                op=OP.mult),
                    waits=[("g", ll)], inc="v")
            qh = op("v", lambda e, z=zq, d=sqh: e.tensor_copy(d, z),
                    waits=[("v", sq)], inc="v")
            ql = op("g", lambda e, z=zq, d=sql, dh=sqh: e.tensor_tensor(
                d, z, dh, op=OP.subtract), waits=[("v", qh)], inc="g")
            first, last = (dc == 0), (dc == NDC - 1)
            tev = None
            for tt in range(2):
                sl = slice(tt * 512, (tt + 1) * 512)
                w0 = [pr_free[2 * tt]] if first else []
                w1 = [pr_free[2 * tt + 1]] if first else []
                op("t", lambda e, p=PR[2 * tt], f=first, d=srh, sl=sl: e.matmul(
                    p[:], ONESC[:], d[:, sl], start=f, stop=False),
                   waits=[("v", hh)] + w0)
                op("t", lambda e, p=PR[2 * tt], l_=last, d=srl, sl=sl: e.matmul(
                    p[:], ONESC[:], d[:, sl], start=False, stop=l_),
                   waits=[("g", ll)])
                op("t", lambda e, p=PR[2 * tt + 1], f=first, d=sqh, sl=sl: e.matmul(
                    p[:], ONESC[:], d[:, sl], start=f, stop=False),
                   waits=[("v", qh)] + w1)
                tev = op("t", lambda e, p=PR[2 * tt + 1], l_=last, d=sql, sl=sl:
                         e.matmul(p[:], ONESC[:], d[:, sl], start=False, stop=l_),
                         waits=[("g", ql)], inc="t")
            tev_hist[dc] = tev
            return tev

        xt_free = [("t", 0), ("t", 0)]
        for slab in range(2):
            xsrc = x_own if slab == 0 else x_oth
            tev_hist = {}
            last_t = 0
            for dc in range(NDC):
                sl_ = dc % 2
                w = [xt_free[sl_]] if (dc or slab) else [("v", tbl_ev),
                                                         ("dC", cnt["dC"])]
                dl = dma_in(XT[:, sl_ * 1024:(sl_ + 1) * 1024],
                            xsrc[dc * 128:(dc + 1) * 128, :], f"dX{sl_}", waits=w)
                last_t = stats_dc(XT[:, sl_ * 1024:(sl_ + 1) * 1024],
                                  [(f"dX{sl_}", dl)], dc, tev_hist)
                xt_free[sl_] = ("t", last_t)
            sev = ln_rows(0, 1, 2, [("t", last_t)],
                          s_dst=(rG if slab == 0 else None))
            if slab == 0:
                sev = op("v", lambda e: e.tensor_tensor(rH[0:1, :], rA[0:1, :],
                                                        rC[0:1, :], op=OP.mult),
                         waits=[("v", sev)], inc="v")
            d1 = op("v", lambda e: e.tensor_tensor(rD[0:1, :], rC[0:1, :], rC[0:1, :],
                                                   op=OP.mult), waits=[("v", sev)],
                    inc="v")
            d2 = op("v", lambda e: e.tensor_tensor(rD[0:1, :], rD[0:1, :], rB[0:1, :],
                                                   op=OP.mult), waits=[("v", d1)],
                    inc="v")
            d3 = op("v", lambda e: e.tensor_scalar(rD[0:1, :], rD[0:1, :], EPS, None,
                                                   op0=OP.add), waits=[("v", d2)],
                    inc="v")
            d4 = row_sqrt(rE[0:1, :], rD[0:1, :], [("v", d3)])
            d5 = op("v", lambda e: e.reciprocal(rD[0:1, :], rE[0:1, :]),
                    waits=[("s", d4)], inc="v")
            d6 = op("v", lambda e: e.tensor_tensor(rE[0:1, :], rC[0:1, :], rD[0:1, :],
                                                   op=OP.mult), waits=[("v", d5)],
                    inc="v")
            d7 = op("v", lambda e: e.tensor_tensor(rF[0:1, :], rA[0:1, :], rE[0:1, :],
                                                   op=OP.mult), waits=[("v", d6)],
                    inc="v")
            _, l1 = row_pair(RMH[0:1, 0:1024], RML[0:1, 0:1024], rE[0:1, :],
                             [("v", d7)])
            _, l2 = row_pair(RMH[0:1, 1024:2048], RML[0:1, 1024:2048], rF[0:1, :],
                             [("v", d7)])
            bevs = []
            for tt in range(2):
                bevs.append(bcast512(Z[:, tt * 512:(tt + 1) * 512],
                                     RMH[0:1, tt * 512:(tt + 1) * 512],
                                     RML[0:1, tt * 512:(tt + 1) * 512], 0,
                                     [("v", l1)]))
                bevs.append(bcast512(Z[:, 1024 + tt * 512:1024 + (tt + 1) * 512],
                                     RMH[0:1, 1024 + tt * 512:1024 + (tt + 1) * 512],
                                     RML[0:1, 1024 + tt * 512:1024 + (tt + 1) * 512],
                                     1, [("v", l2)]))
            bev = max(bevs)

            xt2_free = [("v", 0), ("v", 0)]
            prev_h, prev_l = 0, 0
            for dc in range(NDC):
                sl_ = dc % 2
                xv = XT[:, sl_ * 1024:(sl_ + 1) * 1024]
                w = [xt2_free[sl_]] + ([("t", last_t)] if dc < 2 else [])
                dl = dma_in(xv, xsrc[dc * 128:(dc + 1) * 128, :], f"dX{sl_}",
                            waits=w)
                t1 = op("v", lambda e, xv=xv, dl=dl, sl_=sl_: e.tensor_tensor(
                    xv, xv, Z[:, 0:1024], op=OP.mult),
                    waits=[(f"dX{sl_}", dl), ("v", bev)], inc="v")
                t2 = op("v", lambda e, xv=xv: e.tensor_tensor(
                    xv, xv, Z[:, 1024:2048], op=OP.subtract),
                    waits=[("v", t1)], inc="v")
                t3 = op("v", lambda e, xv=xv, dc=dc: e.tensor_scalar(
                    xv, xv, LNW[:, dc:dc + 1], LNB[:, dc:dc + 1],
                    op0=OP.mult, op1=OP.add), waits=[("v", t2)], inc="v")
                prev_h = op("v", lambda e, xv=xv, dc=dc: e.tensor_copy(
                    XSH[:, dc * 1024:(dc + 1) * 1024], xv), waits=[("v", t3)],
                    inc="v")
                prev_l = op("g", lambda e, xv=xv, dc=dc: e.tensor_tensor(
                    XSL[:, dc * 1024:(dc + 1) * 1024], xv,
                    XSH[:, dc * 1024:(dc + 1) * 1024], op=OP.subtract),
                    waits=[("v", prev_h)], inc="g")
                xt2_free[sl_] = ("g", prev_l)
            xs_hi_ev, xs_lo_ev = prev_h, prev_l

            wh_half_free = [("t", 0), ("t", 0)]
            wf_free = [("v", 0)] * 4

            def stream_half(csrc0, half, qscale):
                hs = ls = 0
                for dc in range(NDC):
                    slot = dc % 4
                    dsem = f"dW{slot}"
                    dl = dma_in(WF[:, slot * 512:slot * 512 + 256],
                                w_in[dc * 128:(dc + 1) * 128, csrc0:csrc0 + 256],
                                dsem, waits=[wf_free[slot]],
                                q="y" if slot % 2 == 0 else "s")
                    src = WF[:, slot * 512:slot * 512 + 256]
                    dsth = WH[:, half * 4096 + dc * 256:half * 4096 + (dc + 1) * 256]
                    dstl = WL[:, half * 4096 + dc * 256:half * 4096 + (dc + 1) * 256]
                    hw = [wh_half_free[half]] if dc == 0 else []
                    if qscale:
                        sc = op("v", lambda e, s_=src: e.tensor_scalar(
                            s_, s_, CQ, None, op0=OP.mult), waits=[(dsem, dl)],
                            inc="v")
                        hs = op("s", lambda e, s_=src, d=dsth: e.copy(d, s_),
                                waits=[("v", sc)] + hw, inc="s")
                    else:
                        hs = op("s", lambda e, s_=src, d=dsth: e.copy(d, s_),
                                waits=[(dsem, dl)] + hw, inc="s")
                    ls = op("v", lambda e, s_=src, d=dstl, dh=dsth: e.tensor_tensor(
                        d, s_, dh, op=OP.subtract), waits=[("s", hs)], inc="v")
                    wf_free[slot] = ("v", ls)
                return hs, ls

            def rope_evac(pti, fo, tt, base_t, is_q, gev):
                ps = PT[pti]
                j, o32 = yb_slot()
                jw = yb_state["free"][j]
                j2, tmp = yb_slot()
                j2w = yb_state["free"][j2]
                if is_q:
                    CT, ST = CQD, SQD
                    tsl = slice(tt * 512, (tt + 1) * 512)
                else:
                    CT, ST = CKD, SKD
                    tsl = slice(base_t + tt * 512, base_t + (tt + 1) * 512)
                w0 = [("t", gev), jw, j2w]
                op("v", lambda e, o=o32, p=ps, c=CT, ts=tsl: e.tensor_tensor(
                    o[0:32, :], p[0:32, :], c[0:32, ts], op=OP.mult), waits=w0,
                    inc="v")
                a2 = op("v", lambda e, o=tmp, p=ps, s_=ST, ts=tsl: e.tensor_tensor(
                    o[0:32, :], p[32:64, :], s_[0:32, ts], op=OP.mult), inc="v")
                a3 = op("v", lambda e, o=o32, t_=tmp: e.tensor_tensor(
                    o[0:32, :], o[0:32, :], t_[0:32, :], op=OP.subtract),
                    waits=[("v", a2)], inc="v")
                a4 = op("v", lambda e, o=o32, p=ps, c=CT, ts=tsl: e.tensor_tensor(
                    o[32:64, :], p[32:64, :], c[32:64, ts], op=OP.mult), inc="v")
                a5 = op("v", lambda e, o=tmp, p=ps, s_=ST, ts=tsl: e.tensor_tensor(
                    o[32:64, :], p[0:32, :], s_[32:64, ts], op=OP.mult), inc="v")
                a6 = op("v", lambda e, o=o32, t_=tmp: e.tensor_tensor(
                    o[32:64, :], o[32:64, :], t_[32:64, :], op=OP.add),
                    waits=[("v", a5)], inc="v")
                if is_q:
                    pv = op("s", lambda e, o=o32, p=ps: e.copy(o[64:128, :],
                                                               p[64:128, :]),
                            waits=w0, inc="s")
                    pass_ev = [("s", pv)]
                else:
                    b1 = op("s", lambda e, o=o32, p=ps: e.activation(
                        o[64:128, :], p[64:128, :], AF.Copy, scale=CK1),
                        waits=w0, inc="s")
                    b2 = op("s", lambda e, o=tmp, p=ps: e.activation(
                        o[64:128, :], p[64:128, :], AF.Copy, scale=CK2), inc="s")
                    b3 = op("v", lambda e, o=o32, t_=tmp: e.tensor_tensor(
                        o[64:128, :], o[64:128, :], t_[64:128, :], op=OP.add),
                        waits=[("s", b2), ("s", b1)], inc="v")
                    pass_ev = [("v", b3)]
                je, ehs, els = eh_slot()
                ew = eh_state["free"][je]
                hh = op("v", lambda e, d=ehs, o=o32: e.tensor_copy(d, o[:]),
                        waits=[("v", a3), ("v", a6)] + pass_ev + [ew], inc="v")
                ll = op("g", lambda e, d=els, o=o32, dh=ehs: e.tensor_tensor(
                    d, o[:], dh, op=OP.subtract), waits=[("v", hh)], inc="g")
                yb_state["free"][j] = ("g", ll)
                yb_state["free"][j2] = ("g", ll)
                if is_q:
                    dsth = q_hi_s[fo][:, tt * 512:(tt + 1) * 512]
                    dstl = q_lo_s[fo][:, tt * 512:(tt + 1) * 512]
                else:
                    dsth = k_hi_s[fo][:, base_t + tt * 512:base_t + (tt + 1) * 512]
                    dstl = k_lo_s[fo][:, base_t + tt * 512:base_t + (tt + 1) * 512]
                esem = f"dE{je}"
                dma_out(dsth, ehs, esem, waits=[("v", hh)])
                d2_ = dma_out(dstl, els, esem, waits=[("g", ll)])
                eh_state["free"][je] = (esem, d2_)
                return ("v", hh)

            def v_evac(pti, fo, tt, base_t, gev):
                ps = PT[pti]
                je, ehs, els = eh_slot()
                ew = eh_state["free"][je]
                hh = op("v", lambda e, d=ehs, p=ps: e.tensor_copy(d, p[:]),
                        waits=[("t", gev), ew], inc="v")
                ll = op("v", lambda e, d=els, p=ps, dh=ehs: e.tensor_tensor(
                    d, p[:], dh, op=OP.subtract), waits=[("v", hh)], inc="v")
                esem = f"dE{je}"
                dma_out(v_hi_s[fo][:, base_t + tt * 512:base_t + (tt + 1) * 512],
                        ehs, esem, waits=[("v", hh)])
                d2_ = dma_out(v_lo_s[fo][:, base_t + tt * 512:base_t + (tt + 1) * 512],
                              els, esem, waits=[("v", ll)])
                eh_state["free"][je] = (esem, d2_)
                return ("v", ll)

            pti_i = [0]

            def projection(kind):
                col0 = {"q": 0, "k": D, "v": 2 * D}[kind]
                base_t = slab * 1024
                ev = {0: stream_half(col0, 0, kind == "q")}
                for hb in range(8):
                    half = hb % 2
                    if hb + 1 < 8:
                        ev[hb + 1] = stream_half(col0 + (hb + 1) * 256,
                                                 (hb + 1) % 2, kind == "q")
                    hs, ls = ev[hb]
                    last_g = 0
                    for fo_loc in range(2):
                        fo = hb * 2 + fo_loc
                        for tt in range(2):
                            pti = pti_i[0] % 3
                            pti_i[0] += 1
                            w = pt_acquire(pti, [("s", hs), ("v", ls),
                                                 ("v", xs_hi_ev), ("g", xs_lo_ev)])
                            passes = [(WH, XSH), (WH, XSL), (WL, XSH)]
                            gev = None
                            for pi, (LB, RB) in enumerate(passes):
                                for dc in range(NDC):
                                    first = (pi == 0 and dc == 0)
                                    last = (pi == 2 and dc == NDC - 1)
                                    lhsT = LB[:, half * 4096 + dc * 256 + fo_loc * 128:
                                              half * 4096 + dc * 256 + fo_loc * 128
                                              + 128]
                                    rhs = RB[:, dc * 1024 + tt * 512:
                                             dc * 1024 + (tt + 1) * 512]
                                    gev = op("t", lambda e, p=PT[pti], l_=lhsT, r=rhs,
                                             f=first, st=last: e.matmul(
                                                 p[:], l_, r, start=f, stop=st),
                                             waits=w if first else (),
                                             inc="t" if last else None)
                            last_g = gev
                            if kind == "v":
                                pt_free[pti] = v_evac(pti, fo, tt, base_t, gev)
                            else:
                                pt_free[pti] = rope_evac(pti, fo, tt, base_t,
                                                         kind == "q", gev)
                    wh_half_free[half] = ("t", last_g)

            if slab == 0:
                projection("q")
            projection("k")
            projection("v")
            barrier()

        p1s.close()
        barrier()

        p23 = ExitStack()
        CTXH = p23.enter_context(nc.sbuf_tensor("CTXH", [128, 16 * 1024], BF16))
        CTXL = p23.enter_context(nc.sbuf_tensor("CTXL", [128, 16 * 1024], BF16))
        p2s = ExitStack()
        PH = p2s.enter_context(nc.sbuf_tensor("PH", [128, 16 * 512], BF16))
        PL = p2s.enter_context(nc.sbuf_tensor("PL", [128, 16 * 512], BF16))
        KH = [p2s.enter_context(nc.sbuf_tensor(f"KH{i}", [128, SQ], BF16))
              for i in range(2)]
        KL = [p2s.enter_context(nc.sbuf_tensor(f"KL{i}", [128, SQ], BF16))
              for i in range(2)]
        VFH = p2s.enter_context(nc.sbuf_tensor("VFH", [128, SQ], BF16))
        VFL = p2s.enter_context(nc.sbuf_tensor("VFL", [128, SQ], BF16))
        QHB = [p2s.enter_context(nc.sbuf_tensor(f"QHB{i}", [128, TOWN], BF16))
               for i in range(2)]
        QLB = [p2s.enter_context(nc.sbuf_tensor(f"QLB{i}", [128, TOWN], BF16))
               for i in range(2)]
        VTH = [p2s.enter_context(nc.sbuf_tensor(f"VTH{i}", [128, SQ], BF16))
               for i in range(2)]
        VTL = [p2s.enter_context(nc.sbuf_tensor(f"VTL{i}", [128, SQ], BF16))
               for i in range(2)]
        Y2 = p2s.enter_context(nc.sbuf_tensor("Y2", [128, 4 * 512], F32))

        y2_state = {"i": 0, "free": [("v", 0)] * 4}

        def y2_slot():
            j = y2_state["i"] % 4
            y2_state["i"] += 1
            return j, Y2[:, j * 512:(j + 1) * 512]

        head_ld = [0, 0]
        head_cmp = [("t", 0), ("t", 0)]

        def load_head(h, vf_guard):
            p = h % 2
            hsem = f"dH{p}"
            dma_in(KH[p][:], k_hi_s[h][:, :], hsem, waits=[head_cmp[p]])
            dma_in(KL[p][:], k_lo_s[h][:, :], hsem)
            dma_in(VFH[:], v_hi_s[h][:, :], hsem, waits=[vf_guard])
            dma_in(VFL[:], v_lo_s[h][:, :], hsem)
            dma_in(QHB[p][:], q_hi_s[h][:, :], hsem)
            head_ld[p] = dma_in(QLB[p][:], q_lo_s[h][:, :], hsem)

        load_head(0, ("v", 0))
        ptb_free = ("v", 0)
        vt_ev = {}
        negb_ev = {}

        def prelude(h):
            nonlocal ptb_free
            par = h % 2
            ldw = [(f"dH{par}", head_ld[par])]
            for src_, dst in ((VFH, VTH[par]), (VFL, VTL[par])):
                for rnd in range(2):
                    tev = None
                    for tc8 in range(8):
                        tc = rnd * 8 + tc8
                        tev = op("t", lambda e, s_=src_, tc=tc, tc8=tc8: e.transpose(
                            PTB[:, tc8 * 128:(tc8 + 1) * 128],
                            s_[:, tc * 128:(tc + 1) * 128], EYE[:]),
                            waits=(ldw + [ptb_free]) if tc8 == 0 else (),
                            inc="t" if tc8 == 7 else None)
                    vv = op("v", lambda e, d=dst, rnd=rnd: e.tensor_copy(
                        d[:, rnd * 1024:(rnd + 1) * 1024], PTB[:]),
                        waits=[("t", tev)], inc="v")
                    ptb_free = ("v", vv)
            vt_ev[h] = cnt["v"]
            o = par * 12
            prev_m3 = 0
            nrm_evs = []
            for i, (src_, n) in enumerate(((KH[par], 4), (QHB[par], 2))):
                for j in range(n):
                    jj, ys = y2_slot()
                    yw = y2_state["free"][jj]
                    m1 = op("v", lambda e, s_=src_, j=j, ys=ys: e.tensor_tensor(
                        ys, s_[:, j * 512:(j + 1) * 512],
                        s_[:, j * 512:(j + 1) * 512], op=OP.mult),
                        waits=[yw] + ldw, inc="v")
                    m2 = op("v", lambda e, ys=ys: e.tensor_copy(SRH2[:], ys),
                            waits=[("v", m1), ("t", prev_m3)], inc="v")
                    y2_state["free"][jj] = ("v", m2)
                    m3 = op("t", lambda e: e.matmul(PR[3][:], ONESC[:], SRH2[:],
                                                    start=True, stop=True),
                            waits=[("v", m2), pr_free[3]], inc="t")
                    prev_m3 = m3
                    idx = o + i * 4 + j
                    m4 = op("v", lambda e, idx=idx: e.reduce_max(
                        RN[0:1, idx:idx + 1], PR[3][:], axis=AX.X),
                        waits=[("t", m3)], inc="v")
                    pr_free[3] = ("v", m4)
                    nrm_evs.append(m4)
            c1 = op("v", lambda e, o=o: e.tensor_max(RN[0:1, o:o + 1],
                                                     RN[0:1, o:o + 1],
                                                     RN[0:1, o + 1:o + 2]),
                    waits=[("v", max(nrm_evs))], inc="v")
            op("v", lambda e, o=o: e.tensor_max(RN[0:1, o + 2:o + 3],
                                                RN[0:1, o + 2:o + 3],
                                                RN[0:1, o + 3:o + 4]), inc="v")
            c3 = op("v", lambda e, o=o: e.tensor_max(RN[0:1, o:o + 1],
                                                     RN[0:1, o:o + 1],
                                                     RN[0:1, o + 2:o + 3]), inc="v")
            c4 = op("v", lambda e, o=o: e.tensor_max(RN[0:1, o + 4:o + 5],
                                                     RN[0:1, o + 4:o + 5],
                                                     RN[0:1, o + 5:o + 6]), inc="v")
            c5 = op("v", lambda e, o=o: e.tensor_tensor(
                RN[0:1, o + 8:o + 9], RN[0:1, o:o + 1], RN[0:1, o + 4:o + 5],
                op=OP.add), waits=[("v", c3), ("v", c4)], inc="v")
            c6 = op("v", lambda e, o=o: e.tensor_scalar(
                RN[0:1, o + 9:o + 10], RN[0:1, o + 8:o + 9], -0.51, -0.5,
                op0=OP.mult, op1=OP.add), waits=[("v", c5)], inc="v")
            c7 = op("v", lambda e, o=o, par=par: e.tensor_copy(
                RMH[0:1, 1024 + par:1025 + par], RN[0:1, o + 9:o + 10]),
                waits=[("v", c6)], inc="v")
            c8 = op("t", lambda e, par=par: e.matmul(
                PT[0][:, 0:1], ONES[:], RMH[0:1, 1024 + par:1025 + par],
                start=True, stop=True), waits=pt_acquire(0, [("v", c7)]), inc="t")
            c9 = op("v", lambda e, par=par: e.tensor_copy(NEGB[:, par:par + 1],
                                                          PT[0][:, 0:1]),
                    waits=[("t", c8)], inc="v")
            pt_free[0] = ("v", c9)
            negb_ev[h] = c9

        prelude(0)
        for h in range(H):
            par = h % 2
            if h + 1 < H:
                load_head(h + 1, ("v", vt_ev[h]))
            head_last = []
            for qt in range(2):
                if qt == 1 and h + 1 < H:
                    prelude(h + 1)
                ctxw = pt_acquire(2, [])
                prw0 = [pr_free[0]]
                sum_gev = ctx_gev = None
                ll_ev = {}

                def consume(kc):
                    pti = kc % 2
                    eh = op("s", lambda e, p=PT[pti], kc=kc, par=par: e.activation(
                        PH[:, kc * 512:(kc + 1) * 512], p[:], AF.Exp,
                        bias=NEGB[:, par:par + 1]), waits=[("t", sc_ev[kc])],
                        inc="s")
                    jj, ys = y2_slot()
                    yw = y2_state["free"][jj]
                    e32 = op("s", lambda e, p=PT[pti], ys=ys, par=par: e.activation(
                        ys, p[:], AF.Exp, bias=NEGB[:, par:par + 1]), waits=[yw],
                        inc="s")
                    pt_free[pti] = ("s", e32)
                    ll_ev[kc] = op("v", lambda e, ys=ys, kc=kc: e.tensor_tensor(
                        PL[:, kc * 512:(kc + 1) * 512], ys,
                        PH[:, kc * 512:(kc + 1) * 512], op=OP.subtract),
                        waits=[("s", e32)], inc="v")
                    y2_state["free"][jj] = ("v", ll_ev[kc])
                    op("t", lambda e, kc=kc, par=par: e.matmul(
                        PT[2][:], VTH[par][:, kc * 128:(kc + 1) * 128],
                        PH[:, kc * 512:(kc + 1) * 512], start=(kc == 0),
                        stop=False), waits=[("s", eh)] + (ctxw if kc == 0 else []))
                    op("t", lambda e, kc=kc, par=par: e.matmul(
                        PT[2][:], VTL[par][:, kc * 128:(kc + 1) * 128],
                        PH[:, kc * 512:(kc + 1) * 512], start=False, stop=False))
                    op("t", lambda e, kc=kc: e.matmul(
                        PR[0][:], ONESC[:], PH[:, kc * 512:(kc + 1) * 512],
                        start=(kc == 0), stop=False),
                        waits=(prw0 if kc == 0 else []))

                sc_ev = {}
                for kc in range(16):
                    pti = kc % 2
                    w = pt_acquire(pti, [("v", vt_ev[h]), ("v", negb_ev[h])]
                                   if kc < 2 else [])
                    op("t", lambda e, p=PT[pti], k_=KH[par], q_=QHB[par], kc=kc,
                       qt=qt: e.matmul(p[:], k_[:, kc * 128:(kc + 1) * 128],
                                       q_[:, qt * 512:(qt + 1) * 512], start=True,
                                       stop=False), waits=w)
                    op("t", lambda e, p=PT[pti], k_=KH[par], q_=QLB[par], kc=kc,
                       qt=qt: e.matmul(p[:], k_[:, kc * 128:(kc + 1) * 128],
                                       q_[:, qt * 512:(qt + 1) * 512], start=False,
                                       stop=False))
                    sc_ev[kc] = op("t", lambda e, p=PT[pti], k_=KL[par],
                                   q_=QHB[par], kc=kc, qt=qt: e.matmul(
                                       p[:], k_[:, kc * 128:(kc + 1) * 128],
                                       q_[:, qt * 512:(qt + 1) * 512], start=False,
                                       stop=True), inc="t")
                    if kc >= 1:
                        consume(kc - 1)
                consume(15)
                for kc in range(16):
                    ctx_gev = op("t", lambda e, kc=kc, par=par: e.matmul(
                        PT[2][:], VTH[par][:, kc * 128:(kc + 1) * 128],
                        PL[:, kc * 512:(kc + 1) * 512], start=False,
                        stop=(kc == 15)),
                        waits=[("v", ll_ev[15])] if kc == 0 else (),
                        inc="t" if kc == 15 else None) or ctx_gev
                    sum_gev = op("t", lambda e, kc=kc: e.matmul(
                        PR[0][:], ONESC[:], PL[:, kc * 512:(kc + 1) * 512],
                        start=False, stop=(kc == 15)),
                        inc="t" if kc == 15 else None) or sum_gev
                jj, c32 = y2_slot()
                yw = y2_state["free"][jj]
                ce = op("v", lambda e, c=c32: e.tensor_copy(c, PT[2][:]),
                        waits=[("t", ctx_gev), yw], inc="v")
                pt_free[2] = ("v", ce)
                s1 = op("v", lambda e: e.tensor_copy(rA[0:1, 0:512], PR[0][:]),
                        waits=[("t", sum_gev)], inc="v")
                pr_free[0] = ("v", s1)
                s2 = op("v", lambda e: e.reciprocal(rA[0:1, 512:1024],
                                                    rA[0:1, 0:512]),
                        waits=[("v", s1)], inc="v")
                _, rl = row_pair(RMH[0:1, 0:512], RML[0:1, 0:512],
                                 rA[0:1, 512:1024], [("v", s2)])
                bv = bcast512(Z[:, 0:512], RMH[0:1, 0:512], RML[0:1, 0:512], 0,
                              [("v", rl)])
                e1_ = op("v", lambda e, c=c32: e.tensor_tensor(
                    c, c, Z[:, 0:512], op=OP.mult), waits=[("v", bv)], inc="v")
                ch = op("v", lambda e, c=c32, h=h, qt=qt: e.tensor_copy(
                    CTXH[:, h * 1024 + qt * 512:h * 1024 + (qt + 1) * 512], c),
                    waits=[("v", e1_)], inc="v")
                cl = op("g", lambda e, c=c32, h=h, qt=qt: e.tensor_tensor(
                    CTXL[:, h * 1024 + qt * 512:h * 1024 + (qt + 1) * 512], c,
                    CTXH[:, h * 1024 + qt * 512:h * 1024 + (qt + 1) * 512],
                    op=OP.subtract), waits=[("v", ch)], inc="g")
                y2_state["free"][jj] = ("g", cl)
                head_last.append(cl)
            head_cmp[par] = ("g", max(head_last))

        barrier()
        p2s.close()
        barrier()

        p3s = ExitStack()
        WF3 = p3s.enter_context(nc.sbuf_tensor("WF3", [128, 4 * 512], F32))
        WH3 = p3s.enter_context(nc.sbuf_tensor("WH3", [128, 2 * 16 * 256], BF16))
        WL3 = p3s.enter_context(nc.sbuf_tensor("WL3", [128, 2 * 16 * 256], BF16))
        XT3 = p3s.enter_context(nc.sbuf_tensor("XT3", [128, 1024], F32))
        Y3 = p3s.enter_context(nc.sbuf_tensor("Y3", [128, 6 * 512], F32))
        SRH3 = p3s.enter_context(nc.sbuf_tensor("SRH3", [128, 3 * 512], BF16))
        SRL3 = p3s.enter_context(nc.sbuf_tensor("SRL3", [128, 3 * 512], BF16))
        SQH3 = p3s.enter_context(nc.sbuf_tensor("SQH3", [128, 3 * 512], BF16))
        SQL3 = p3s.enter_context(nc.sbuf_tensor("SQL3", [128, 3 * 512], BF16))

        y3_state = {"i": 0, "free": [("v", 0)] * 6}

        def y3_slot():
            j = y3_state["i"] % 6
            y3_state["i"] += 1
            return j, Y3[:, j * 512:(j + 1) * 512]

        class DeferStats:

            def __init__(self, SH, SL, QH_, QL_, slot_fn, state):
                self.SH, self.SL, self.QH, self.QL = SH, SL, QH_, QL_
                self.slot_fn, self.state = slot_fn, state
                self.pend = []
                self.idx = 0
                self.tev = {}

            def prep(self, src32, src_waits, pr_sum, pr_sq, first, last):
                i = self.idx
                self.idx += 1
                sl = slice((i % 3) * 512, (i % 3) * 512 + 512)
                guard = [("t", self.tev.get(i - 3, 0))]
                hh = op("v", lambda e, s_=src32, SH=self.SH, sl=sl:
                        e.tensor_copy(SH[:, sl], s_),
                        waits=list(src_waits) + guard, inc="v")
                ll = op("g", lambda e, s_=src32, SH=self.SH, SL=self.SL, sl=sl:
                        e.tensor_tensor(SL[:, sl], s_, SH[:, sl], op=OP.subtract),
                        waits=[("v", hh)], inc="g")
                jj, ys = self.slot_fn()
                yw = self.state["free"][jj]
                sq = op("v", lambda e, s_=src32, ys=ys: e.tensor_tensor(
                    ys, s_, s_, op=OP.mult), waits=[("g", ll), yw], inc="v")
                qh = op("v", lambda e, ys=ys, QH_=self.QH, sl=sl:
                        e.tensor_copy(QH_[:, sl], ys), waits=[("v", sq)], inc="v")
                ql = op("g", lambda e, ys=ys, QH_=self.QH, QL_=self.QL, sl=sl:
                        e.tensor_tensor(QL_[:, sl], ys, QH_[:, sl],
                                        op=OP.subtract), waits=[("v", qh)], inc="g")
                self.state["free"][jj] = ("g", ql)
                self.pend.append((i, sl, pr_sum, pr_sq, first, last, hh, ll, qh, ql))
                if len(self.pend) > 2:
                    self.flush_one()

            def flush_one(self):
                i, sl, pr_sum, pr_sq, first, last, hh, ll, qh, ql = self.pend.pop(0)
                w0 = [pr_free[pr_sum]] if first else []
                w1 = [pr_free[pr_sq]] if first else []
                op("t", lambda e, p=PR[pr_sum], f=first, SH=self.SH, sl=sl:
                   e.matmul(p[:], ONESC[:], SH[:, sl], start=f, stop=False),
                   waits=[("v", hh)] + w0)
                op("t", lambda e, p=PR[pr_sum], l_=last, SL=self.SL, sl=sl:
                   e.matmul(p[:], ONESC[:], SL[:, sl], start=False, stop=l_),
                   waits=[("g", ll)])
                op("t", lambda e, p=PR[pr_sq], f=first, QH_=self.QH, sl=sl:
                   e.matmul(p[:], ONESC[:], QH_[:, sl], start=f, stop=False),
                   waits=[("v", qh)] + w1)
                tev = op("t", lambda e, p=PR[pr_sq], l_=last, QL_=self.QL, sl=sl:
                         e.matmul(p[:], ONESC[:], QL_[:, sl], start=False, stop=l_),
                         waits=[("g", ql)], inc="t")
                self.tev[i] = tev
                return tev

            def flush(self):
                tev = 0
                while self.pend:
                    tev = self.flush_one()
                return tev

        _, l1 = row_pair(RMH[0:1, 0:1024], RML[0:1, 0:1024], rG[0:1, :], [])
        _, l2 = row_pair(RMH[0:1, 1024:2048], RML[0:1, 1024:2048], rH[0:1, :], [])
        bev3s = []
        for tt in range(2):
            bev3s.append(bcast512(Z[:, tt * 512:(tt + 1) * 512],
                                  RMH[0:1, tt * 512:(tt + 1) * 512],
                                  RML[0:1, tt * 512:(tt + 1) * 512], 0, [("v", l1)]))
            bev3s.append(bcast512(Z[:, 1024 + tt * 512:1024 + (tt + 1) * 512],
                                  RMH[0:1, 1024 + tt * 512:1024 + (tt + 1) * 512],
                                  RML[0:1, 1024 + tt * 512:1024 + (tt + 1) * 512],
                                  1, [("v", l2)]))
        bev3 = max(bev3s)

        wh3_half_free = [("t", 0), ("t", 0)]
        wf3_free = [("v", 0)] * 4
        xt3_free = ("v", 0)
        pti3 = [0]
        last3 = 0

        def stream3(hb):
            half = hb % 2
            hs = ls = 0
            for dc in range(NDC):
                slot = dc % 4
                dsem = f"dW{slot}"
                dl = dma_in(WF3[:, slot * 512:slot * 512 + 256],
                            w_out[dc * 128:(dc + 1) * 128, hb * 256:hb * 256 + 256],
                            dsem, waits=[wf3_free[slot]],
                            q="y" if slot % 2 == 0 else "s")
                src = WF3[:, slot * 512:slot * 512 + 256]
                dsth = WH3[:, half * 4096 + dc * 256:half * 4096 + (dc + 1) * 256]
                dstl = WL3[:, half * 4096 + dc * 256:half * 4096 + (dc + 1) * 256]
                hw = [wh3_half_free[half]] if dc == 0 else []
                hs = op("s", lambda e, s_=src, d=dsth: e.copy(d, s_),
                        waits=[(dsem, dl)] + hw, inc="s")
                ls = op("v", lambda e, s_=src, d=dstl, dh=dsth: e.tensor_tensor(
                    d, s_, dh, op=OP.subtract), waits=[("s", hs)], inc="v")
                wf3_free[slot] = ("v", ls)
            return hs, ls

        ds3 = DeferStats(SRH3, SRL3, SQH3, SQL3, y3_slot, y3_state)
        ev3 = {0: stream3(0)}
        for hb in range(8):
            half = hb % 2
            if hb + 1 < 8:
                ev3[hb + 1] = stream3(hb + 1)
            hs, ls = ev3[hb]
            last_g = 0
            for fo_loc in range(2):
                fo = hb * 2 + fo_loc
                dlx = dma_in(XT3[:], x_own[fo * 128:(fo + 1) * 128, :], "dX0",
                             waits=[xt3_free])
                ao_evs = []
                for tt in range(2):
                    pti = pti3[0] % 3
                    pti3[0] += 1
                    w = pt_acquire(pti, [("s", hs), ("v", ls)])
                    passes = [(WH3, CTXH), (WH3, CTXL), (WL3, CTXH)]
                    gev = None
                    for pi, (LB, RB) in enumerate(passes):
                        for dc in range(NDC):
                            first = (pi == 0 and dc == 0)
                            last = (pi == 2 and dc == NDC - 1)
                            lhsT = LB[:, half * 4096 + dc * 256 + fo_loc * 128:
                                      half * 4096 + dc * 256 + fo_loc * 128 + 128]
                            rhs = RB[:, dc * 1024 + tt * 512:dc * 1024 +
                                     (tt + 1) * 512]
                            gev = op("t", lambda e, p=PT[pti], l_=lhsT, r=rhs,
                                     f=first, st=last: e.matmul(p[:], l_, r,
                                                                start=f, stop=st),
                                     waits=w if first else (),
                                     inc="t" if last else None)
                    last_g = gev
                    jj, ys = y3_slot()
                    yw = y3_state["free"][jj]
                    a1 = op("v", lambda e, ys=ys, tt=tt: e.tensor_tensor(
                        ys, XT3[:, tt * 512:(tt + 1) * 512],
                        Z[:, tt * 512:(tt + 1) * 512], op=OP.mult),
                        waits=[("dX0", dlx), yw, ("v", bev3)], inc="v")
                    a2 = op("v", lambda e, ys=ys, tt=tt: e.tensor_tensor(
                        ys, ys, Z[:, 1024 + tt * 512:1024 + (tt + 1) * 512],
                        op=OP.subtract), waits=[("v", a1)], inc="v")
                    a3 = op("v", lambda e, ys=ys, p=PT[pti]: e.tensor_tensor(
                        ys, ys, p[:], op=OP.add), waits=[("v", a2), ("t", gev)],
                        inc="v")
                    pt_free[pti] = ("v", a3)
                    ds3.prep(ys, [("v", a3)], 2 * tt, 2 * tt + 1,
                             first=(fo == 0), last=(fo == 15))
                    ysem = f"dY{jj}"
                    de = dma_out(ao_s[fo * 128:(fo + 1) * 128,
                                      tt * 512:(tt + 1) * 512], ys, ysem,
                                 waits=[("v", cnt["v"]), ("g", cnt["g"])])
                    y3_state["free"][jj] = (ysem, de)
                    ao_evs.append(a1)
                xt3_free = ("v", max(ao_evs))
            wh3_half_free[half] = ("t", last_g)
        last3 = ds3.flush()
        ln_rows(0, 1, 2, [("t", last3)])
        op("v", lambda e: e.tensor_copy(rG[0:1, :], rC[0:1, :]), inc="v")
        op("v", lambda e: e.tensor_tensor(rH[0:1, :], rA[0:1, :], rC[0:1, :],
                                          op=OP.mult), inc="v")
        barrier()
        p3s.close()
        p23.close()
        barrier()

        p4s = ExitStack()
        YSH = p4s.enter_context(nc.sbuf_tensor("YSH", [128, 16 * 512], BF16))
        YSL = p4s.enter_context(nc.sbuf_tensor("YSL", [128, 16 * 512], BF16))
        HH = p4s.enter_context(nc.sbuf_tensor("HH", [128, 8 * 512], BF16))
        HL = p4s.enter_context(nc.sbuf_tensor("HL", [128, 8 * 512], BF16))
        DA = p4s.enter_context(nc.sbuf_tensor("DA", [128, 16 * 512], F32))
        WF4 = p4s.enter_context(nc.sbuf_tensor("WF4", [128, 4 * 512], F32))
        WH4 = p4s.enter_context(nc.sbuf_tensor("WH4", [128, 3 * 16 * 256], BF16))
        XT4 = p4s.enter_context(nc.sbuf_tensor("XT4", [128, 4 * 512], F32))
        Y4 = p4s.enter_context(nc.sbuf_tensor("Y4", [128, 6 * 512], F32))
        SRH4 = p4s.enter_context(nc.sbuf_tensor("SRH4", [128, 3 * 512], BF16))
        SRL4 = p4s.enter_context(nc.sbuf_tensor("SRL4", [128, 3 * 512], BF16))
        SQH4 = p4s.enter_context(nc.sbuf_tensor("SQH4", [128, 3 * 512], BF16))
        SQL4 = p4s.enter_context(nc.sbuf_tensor("SQL4", [128, 3 * 512], BF16))

        y4_state = {"i": 0, "free": [("v", 0)] * 6}

        def y4_slot():
            j = y4_state["i"] % 6
            y4_state["i"] += 1
            return j, Y4[:, j * 512:(j + 1) * 512]

        xt4_state = {"i": 0, "free": [("v", 0)] * 4}

        def xt4_slot():
            j = xt4_state["i"] % 4
            xt4_state["i"] += 1
            return j, XT4[:, j * 512:(j + 1) * 512]

        def make_y(fo, tt, bw):
            ja, at = xt4_slot()
            aw = xt4_state["free"][ja]
            jx, xt = xt4_slot()
            xw = xt4_state["free"][jx]
            d1 = dma_in(at, ao_s[fo * 128:(fo + 1) * 128, tt * 512:(tt + 1) * 512],
                        f"dX{ja}", waits=[aw], q="y")
            d2 = dma_in(xt, x_own[fo * 128:(fo + 1) * 128, tt * 512:(tt + 1) * 512],
                        f"dX{jx}", waits=[xw], q="s")
            jy, ys = y4_slot()
            yw = y4_state["free"][jy]
            a1 = op("v", lambda e, ys=ys, at=at: e.tensor_tensor(
                ys, at, Z[:, 1024:1536], op=OP.mult),
                waits=[(f"dX{ja}", d1), yw, bw], inc="v")
            a2 = op("v", lambda e, ys=ys: e.tensor_tensor(
                ys, ys, Z[:, 1536:2048], op=OP.subtract), waits=[("v", a1)], inc="v")
            a3 = op("v", lambda e, ys=ys, xt=xt: e.tensor_tensor(
                ys, ys, xt, op=OP.add), waits=[("v", a2), (f"dX{jx}", d2)], inc="v")
            xt4_state["free"][ja] = ("v", a1)
            xt4_state["free"][jx] = ("v", a3)
            return jy, ys, a3

        QPART = [(0, 8), (8, 8), (16, 8), (24, 8), (32, 8), (40, 4)]

        for tt in range(2):
            _, l1 = row_pair(RMH[0:1, 0:512], RML[0:1, 0:512],
                             rG[0:1, tt * 512:(tt + 1) * 512], [])
            _, l2 = row_pair(RMH[0:1, 512:1024], RML[0:1, 512:1024],
                             rH[0:1, tt * 512:(tt + 1) * 512], [])
            bcast512(Z[:, 1024:1536], RMH[0:1, 0:512], RML[0:1, 0:512], 0,
                     [("v", l1)])
            b2v = bcast512(Z[:, 1536:2048], RMH[0:1, 512:1024], RML[0:1, 512:1024],
                           1, [("v", l2)])
            bw = ("v", b2v)
            ds4 = DeferStats(SRH4, SRL4, SQH4, SQL4, y4_slot, y4_state)
            for fo in range(NDC):
                jy, ys, yv = make_y(fo, tt, bw)
                ds4.prep(ys, [("v", yv)], 0, 1, first=(fo == 0), last=(fo == 15))
                y4_state["free"][jy] = ("g", cnt["g"])
            last4 = ds4.flush()
            ln_rows(0, 1, 1, [("t", last4)])
            m3e = op("v", lambda e: e.tensor_tensor(rF[0:1, 0:512], rA[0:1, 0:512],
                                                    rC[0:1, 0:512], op=OP.mult),
                     inc="v")
            _, l1 = row_pair(RMH[0:1, 0:512], RML[0:1, 0:512], rC[0:1, 0:512],
                             [("v", m3e)])
            _, l2 = row_pair(RMH[0:1, 512:1024], RML[0:1, 512:1024], rF[0:1, 0:512],
                             [("v", m3e)])
            bcast512(Z[:, 0:512], RMH[0:1, 0:512], RML[0:1, 0:512], 0, [("v", l1)])
            b4 = bcast512(Z[:, 512:1024], RMH[0:1, 512:1024], RML[0:1, 512:1024], 1,
                          [("v", l2)])
            ysh_ev = ysl_ev = 0
            for fo in range(NDC):
                jy, ys, yv = make_y(fo, tt, bw)
                a1 = op("v", lambda e, ys=ys: e.tensor_tensor(
                    ys, ys, Z[:, 0:512], op=OP.mult),
                    waits=[("v", yv), ("v", b4)], inc="v")
                a2 = op("v", lambda e, ys=ys: e.tensor_tensor(
                    ys, ys, Z[:, 512:1024], op=OP.subtract), waits=[("v", a1)],
                    inc="v")
                ysh_ev = op("v", lambda e, ys=ys, fo=fo: e.tensor_copy(
                    YSH[:, fo * 512:(fo + 1) * 512], ys), waits=[("v", a2)],
                    inc="v")
                ysl_ev = op("g", lambda e, ys=ys, fo=fo: e.tensor_tensor(
                    YSL[:, fo * 512:(fo + 1) * 512], ys,
                    YSH[:, fo * 512:(fo + 1) * 512], op=OP.subtract),
                    waits=[("v", ysh_ev)], inc="g")
                y4_state["free"][jy] = ("g", ysl_ev)

            wf4_free = [("v", 0)] * 4
            wh4_half_free = [("t", 0), ("t", 0), ("t", 0)]
            pti4 = [0]
            da_ev = {}
            hfree = ("t", 0)

            def stream_half4(wsrc, row0, c0, ntiles, half):
                hs = 0
                for i in range(ntiles):
                    slot = i % 4
                    dsem = f"dW{slot}"
                    dl = dma_in(WF4[:, slot * 512:slot * 512 + 256],
                                wsrc[(row0 + i) * 128:(row0 + i + 1) * 128,
                                     c0:c0 + 256], dsem, waits=[wf4_free[slot]],
                                q="y" if slot % 2 == 0 else "s")
                    src = WF4[:, slot * 512:slot * 512 + 256]
                    dsth = WH4[:, half * 4096 + i * 256:half * 4096 + (i + 1) * 256]
                    hw = [wh4_half_free[half]] if i == 0 else []
                    hs = op("s", lambda e, s_=src, d=dsth: e.copy(d, s_),
                            waits=[(dsem, dl)] + hw, inc="s")
                    wf4_free[slot] = ("s", hs)
                return hs

            tasks = []
            for (ic0, nic) in QPART:
                for ib in range(nic // 2):
                    tasks.append(("g", ic0 + ib * 2))
                    tasks.append(("u", ic0 + ib * 2))
                for hob in range(8):
                    tasks.append(("d", ic0, nic, hob))

            def stream_task(k):
                t = tasks[k]
                half = k % 3
                if t[0] == "g":
                    return stream_half4(gate_wt, 0, t[1] * 128, NDC, half)
                if t[0] == "u":
                    return stream_half4(up_wt, 0, t[1] * 128, NDC, half)
                return stream_half4(down_wt, t[1], t[3] * 256, t[2], half)

            sev_t = {0: stream_task(0), 1: stream_task(1)}
            gate_res = None
            h_ev = {}
            hseen = {}
            for k, t in enumerate(tasks):
                if k + 2 < len(tasks):
                    sev_t[k + 2] = stream_task(k + 2)
                hs = sev_t.pop(k)
                half = k % 3
                if t[0] in ("g", "u"):
                    icb = t[1]
                    scl, fn = (GSC, AF.Silu) if t[0] == "g" else (USC, AF.Copy)
                    g_evs = []
                    gev = None
                    for io in range(2):
                        ic = icb + io
                        pti = pti4[0] % 3
                        pti4[0] += 1
                        w = pt_acquire(pti, [("s", hs), ("v", ysh_ev),
                                             ("g", ysl_ev)])
                        gev = None
                        for pi, RB in enumerate((YSH, YSL)):
                            for dc in range(NDC):
                                first = (pi == 0 and dc == 0)
                                last = (pi == 1 and dc == NDC - 1)
                                lhsT = WH4[:, half * 4096 + dc * 256 + io * 128:
                                           half * 4096 + dc * 256 + io * 128 + 128]
                                rhs = RB[:, dc * 512:(dc + 1) * 512]
                                gev = op("t", lambda e, p=PT[pti], l_=lhsT, r=rhs,
                                         f=first, st=last: e.matmul(
                                             p[:], l_, r, start=f, stop=st),
                                         waits=w if first else (),
                                         inc="t" if last else None)
                        jy, ys = y4_slot()
                        yw = y4_state["free"][jy]
                        se = op("s", lambda e, ys=ys, p=PT[pti], scl=scl, ic=ic,
                                fn=fn: e.activation(ys, p[:], fn,
                                                    scale=scl[:, ic:ic + 1]),
                                waits=[("t", gev), yw], inc="s")
                        pt_free[pti] = ("s", se)
                        g_evs.append((jy, ys, se))
                    wh4_half_free[half] = ("t", gev)
                    if t[0] == "g":
                        gate_res = g_evs
                    else:
                        ic0q = icb - (icb % 12) if icb < 36 else 36
                        for q0, qn in QPART:
                            if q0 <= icb < q0 + qn:
                                ic0q = q0
                        for io in range(2):
                            jg, gys, ge = gate_res[io]
                            ju, uys, ue = g_evs[io]
                            pos = icb + io - ic0q
                            m1 = op("v", lambda e, g_=gys, u_=uys:
                                    e.tensor_tensor(g_, g_, u_, op=OP.mult),
                                    waits=[("s", ge), ("s", ue)], inc="v")
                            h_ = op("v", lambda e, g_=gys, pos=pos:
                                    e.tensor_copy(
                                        HH[:, pos * 512:(pos + 1) * 512], g_),
                                    waits=[("v", m1), hfree], inc="v")
                            l_ = op("g", lambda e, g_=gys, pos=pos:
                                    e.tensor_tensor(
                                        HL[:, pos * 512:(pos + 1) * 512], g_,
                                        HH[:, pos * 512:(pos + 1) * 512],
                                        op=OP.subtract),
                                    waits=[("v", h_)], inc="g")
                            y4_state["free"][jg] = ("g", l_)
                            y4_state["free"][ju] = ("v", m1)
                            hseen[ic0q] = hseen.get(ic0q, 0) + 1
                            if hseen[ic0q] == dict(QPART)[ic0q]:
                                h_ev[ic0q] = (cnt["v"], cnt["g"])
                else:
                    _, ic0, nic, hob = t
                    h_hi_ev, h_lo_ev = h_ev[ic0]
                    gevd = None
                    for ho_loc in range(2):
                        ho = hob * 2 + ho_loc
                        pti = pti4[0] % 3
                        pti4[0] += 1
                        w = pt_acquire(pti, [("s", hs), ("v", h_hi_ev),
                                             ("g", h_lo_ev)])
                        gevd = None
                        for pi, RB in enumerate((HH, HL)):
                            for i in range(nic):
                                first = (pi == 0 and i == 0)
                                last = (pi == 1 and i == nic - 1)
                                lhsT = WH4[:, half * 4096 + i * 256 + ho_loc * 128:
                                           half * 4096 + i * 256 + ho_loc * 128
                                           + 128]
                                rhs = RB[:, i * 512:(i + 1) * 512]
                                gevd = op("t", lambda e, p=PT[pti], l_=lhsT, r=rhs,
                                          f=first, st=last: e.matmul(
                                              p[:], l_, r, start=f, stop=st),
                                          waits=w if first else (),
                                          inc="t" if last else None)
                        dsl = DA[:, ho * 512:(ho + 1) * 512]
                        daw = da_ev.get(ho, ("v", 0))
                        if ic0 == 0:
                            de = op("v", lambda e, d=dsl, p=PT[pti]: e.tensor_copy(
                                d, p[:]), waits=[("t", gevd), daw], inc="v")
                        else:
                            de = op("v", lambda e, d=dsl, p=PT[pti]:
                                    e.tensor_tensor(d, d, p[:], op=OP.add),
                                    waits=[("t", gevd), daw], inc="v")
                        if ic0 == QPART[-1][0]:
                            de = op("v", lambda e, d=dsl, ho=ho: e.tensor_scalar(
                                d, d, DSC[:, ho:ho + 1], None, op0=OP.mult),
                                waits=[("v", de)], inc="v")
                        da_ev[ho] = ("v", de)
                        pt_free[pti] = ("v", de)
                    wh4_half_free[half] = ("t", gevd)
                    if hob == 7:
                        hfree = ("t", gevd)

            ds4b = DeferStats(SRH4, SRL4, SQH4, SQL4, y4_slot, y4_state)
            for ho in range(NDC):
                ds4b.prep(DA[:, ho * 512:(ho + 1) * 512], [da_ev[ho]], 0, 1,
                          first=(ho == 0), last=(ho == 15))
            last4 = ds4b.flush()
            ln_rows(0, 1, 1, [("t", last4)])
            m4e = op("v", lambda e: e.tensor_tensor(rF[0:1, 0:512], rA[0:1, 0:512],
                                                    rC[0:1, 0:512], op=OP.mult),
                     inc="v")
            _, l1 = row_pair(RMH[0:1, 0:512], RML[0:1, 0:512], rC[0:1, 0:512],
                             [("v", m4e)])
            _, l2 = row_pair(RMH[0:1, 512:1024], RML[0:1, 512:1024], rF[0:1, 0:512],
                             [("v", m4e)])
            bcast512(Z[:, 0:512], RMH[0:1, 0:512], RML[0:1, 0:512], 0, [("v", l1)])
            b4 = bcast512(Z[:, 512:1024], RMH[0:1, 512:1024], RML[0:1, 512:1024], 1,
                          [("v", l2)])
            for fo in range(NDC):
                jy, ys, yv = make_y(fo, tt, bw)
                jd, dt_ = y4_slot()
                dw = y4_state["free"][jd]
                f1 = op("v", lambda e, d=dt_, fo=fo: e.tensor_tensor(
                    d, DA[:, fo * 512:(fo + 1) * 512], Z[:, 0:512], op=OP.mult),
                    waits=[dw, ("v", b4), da_ev[fo]], inc="v")
                f2 = op("v", lambda e, d=dt_: e.tensor_tensor(
                    d, d, Z[:, 512:1024], op=OP.subtract), waits=[("v", f1)],
                    inc="v")
                f3 = op("v", lambda e, d=dt_, ys=ys: e.tensor_tensor(
                    d, d, ys, op=OP.add), waits=[("v", f2), ("v", yv)], inc="v")
                ysem = f"dY{jd}"
                de = dma_out(out_fm[fo * 128:(fo + 1) * 128,
                                    tt * 512:(tt + 1) * 512], dt_, ysem,
                             waits=[("v", f3)])
                y4_state["free"][jd] = (ysem, de)
                y4_state["free"][jy] = ("v", f3)
            barrier()

        p4s.close()
        fin = [(nm, cnt[nm]) for nm in SEMH if nm.startswith(("dE", "dY"))
               and cnt[nm] > 0]
        def _final(e, fin=tuple(fin)):
            for nm, v in fin:
                e.wait_ge(SEMH[nm], v)
        prog["y"].append(_final)

        block = es.enter_context(nc.Block())
        block.tensor(lambda e: [f(e) for f in prog["t"]])
        block.vector(lambda e: [f(e) for f in prog["v"]])
        block.scalar(lambda e: [f(e) for f in prog["s"]])
        block.gpsimd(lambda e: [f(e) for f in prog["g"]])
        block.sync(lambda e: [f(e) for f in prog["y"]])

    return nc


def kernel(L0_attn_X, L0_attn_ln_w, L0_attn_ln_b, L0_attn_W_in, L0_attn_W_out,
           L0_ffn_gate_w, L0_ffn_up_w, L0_ffn_down_w, L0_ffn_gate_s, L0_ffn_up_s,
           L0_ffn_down_s):
    X = np.asarray(L0_attn_X, dtype=np.float32)
    W_in = np.asarray(L0_attn_W_in, dtype=np.float32)
    W_out = np.ascontiguousarray(np.asarray(L0_attn_W_out, dtype=np.float32))
    gate_w = np.asarray(L0_ffn_gate_w, dtype=np.float32)
    up_w = np.asarray(L0_ffn_up_w, dtype=np.float32)
    down_w = np.asarray(L0_ffn_down_w, dtype=np.float32)

    perm64 = np.concatenate([np.arange(0, 64, 2), np.arange(1, 64, 2),
                             np.arange(64, 128)])
    colp = np.empty(3 * D, dtype=np.int64)
    for h in range(H):
        colp[h * 128:(h + 1) * 128] = h * 128 + perm64
        colp[D + h * 128:D + (h + 1) * 128] = D + h * 128 + perm64
    colp[2 * D:] = np.arange(2 * D, 3 * D)
    w_in_p = np.ascontiguousarray(W_in[:, colp])
    gate_wt = np.ascontiguousarray(gate_w.T)
    up_wt = np.ascontiguousarray(up_w.T)
    down_wt = np.ascontiguousarray(down_w.T)
    lnw = np.ascontiguousarray(np.asarray(L0_attn_ln_w, np.float32).reshape(16, 128).T)
    lnb = np.ascontiguousarray(np.asarray(L0_attn_ln_b, np.float32).reshape(16, 128).T)
    gs = np.ascontiguousarray(np.asarray(L0_ffn_gate_s, np.float32).reshape(44, 128).T)
    us = np.ascontiguousarray(np.asarray(L0_ffn_up_s, np.float32).reshape(44, 128).T)
    ds = np.ascontiguousarray(np.asarray(L0_ffn_down_s, np.float32).reshape(16, 128).T)
    cos32, sin32 = _rope_tables()
    import ml_dtypes
    eye = np.eye(128, dtype=ml_dtypes.bfloat16)
    ones1 = np.ones((1, 128), dtype=ml_dtypes.bfloat16)
    onesc = np.ones((128, 1), dtype=ml_dtypes.bfloat16)

    Xfm = [np.ascontiguousarray(X[b].T) for b in range(4)]

    nc = _build()
    in_maps = []
    for c in range(8):
        b, q0 = c // 2, (c % 2) * 1024
        o0 = 1024 - q0
        cq_ = cos32[:, q0:q0 + 1024]
        sq_ = sin32[:, q0:q0 + 1024]
        ck_ = np.concatenate([cos32[:, q0:q0 + 1024], cos32[:, o0:o0 + 1024]], axis=1)
        sk_ = np.concatenate([sin32[:, q0:q0 + 1024], sin32[:, o0:o0 + 1024]], axis=1)
        in_maps.append({
            "x_own": np.ascontiguousarray(Xfm[b][:, q0:q0 + 1024]),
            "x_oth": np.ascontiguousarray(Xfm[b][:, o0:o0 + 1024]),
            "w_in": w_in_p, "w_out": W_out,
            "gate_wt": gate_wt, "up_wt": up_wt, "down_wt": down_wt,
            "lnw": lnw, "lnb": lnb, "gate_s": gs, "up_s": us, "down_s": ds,
            "cosqd": np.ascontiguousarray(np.concatenate([cq_, cq_], axis=0)),
            "sinqd": np.ascontiguousarray(np.concatenate([sq_, sq_], axis=0)),
            "coskd": np.ascontiguousarray(np.concatenate([ck_, ck_], axis=0)),
            "sinkd": np.ascontiguousarray(np.concatenate([sk_, sk_], axis=0)),
            "eye": eye, "ones1": ones1, "onesc": onesc,
        })
    res = run_bass_kernel_spmd(nc, in_maps, core_ids=list(range(8)))
    out = np.empty((4, 2048, 2048), dtype=np.float32)
    for c in range(8):
        b, q0 = c // 2, (c % 2) * 1024
        out[b, q0:q0 + 1024, :] = np.asarray(res.results[c]["out_fm"]).T
    return out


# version 18
# speedup vs baseline: 1.3163x; 1.3163x over previous
import numpy as np
from contextlib import ExitStack

import concourse.bass as bass
import concourse.mybir as mybir
from concourse.bass_utils import run_bass_kernel_spmd

F32 = mybir.dt.float32
BF16 = mybir.dt.bfloat16
AF = mybir.ActivationFunctionType
OP = mybir.AluOpType
AX = mybir.AxisListType

D = 2048
SQ = 2048
TOWN = 1024
H = 16
FI = 5632
NDC = 16
EPS = 1e-5
INV_D = 1.0 / D

_S128 = float(np.float32(np.sqrt(np.float32(128.0))))
CQ = float(np.float32(128.0 ** -0.25))
_CK = (1.0 / _S128) / CQ
CK1 = float(np.float32(_CK))
CK2 = float(np.float32(_CK - CK1))

ENGS = ("t", "v", "s", "g", "y")


def _rope_tables():
    import jax
    import jax.numpy as jnp
    rot = 64
    with jax.default_device(jax.devices("cpu")[0]):
        inv_freq = 1.0 / 10000.0 ** (jnp.arange(0, rot, 2, dtype=jnp.float32) / rot)
        ang = jnp.arange(SQ, dtype=jnp.float32)[:, None] * inv_freq[None, :]
        cos = np.ascontiguousarray(np.asarray(jnp.cos(ang)).T).astype(np.float32)
        sin = np.ascontiguousarray(np.asarray(jnp.sin(ang)).T).astype(np.float32)
    return cos, sin


def _build():
    nc = bass.Bass("TRN2", target_bir_lowering=False)

    def din(name, shape, dt=F32):
        return nc.dram_tensor(name, shape, dt, kind="ExternalInput").ap()

    x_own = din("x_own", [D, TOWN])
    x_oth = din("x_oth", [D, TOWN])
    w_in = din("w_in", [D, 3 * D])
    w_out = din("w_out", [D, D])
    gate_wt = din("gate_wt", [D, FI])
    up_wt = din("up_wt", [D, FI])
    down_wt = din("down_wt", [FI, D])
    lnw_i = din("lnw", [128, 16])
    lnb_i = din("lnb", [128, 16])
    gs_i = din("gate_s", [128, 44])
    us_i = din("up_s", [128, 44])
    ds_i = din("down_s", [128, 16])
    cosqd_i = din("cosqd", [64, TOWN])
    sinqd_i = din("sinqd", [64, TOWN])
    coskd_i = din("coskd", [64, SQ])
    sinkd_i = din("sinkd", [64, SQ])
    eye_i = din("eye", [128, 128], BF16)
    ones_i = din("ones1", [1, 128], BF16)
    onesc_i = din("onesc", [128, 1], BF16)

    out_fm = nc.dram_tensor("out_fm", [D, TOWN], F32, kind="ExternalOutput").ap()

    q_hi_s = nc.dram_tensor("q_hi_s", [H, 128, TOWN], BF16).ap()
    q_lo_s = nc.dram_tensor("q_lo_s", [H, 128, TOWN], BF16).ap()
    k_hi_s = nc.dram_tensor("k_hi_s", [H, 128, SQ], BF16).ap()
    k_lo_s = nc.dram_tensor("k_lo_s", [H, 128, SQ], BF16).ap()
    v_hi_s = nc.dram_tensor("v_hi_s", [H, 128, SQ], BF16).ap()
    v_lo_s = nc.dram_tensor("v_lo_s", [H, 128, SQ], BF16).ap()
    ao_s = nc.dram_tensor("ao_s", [D, TOWN], F32).ap()

    es = ExitStack()
    with es:
        SEMH = {}
        semnames = (["t", "v", "s", "g", "dC", "dH0", "dH1"]
                    + [f"dW{i}" for i in range(4)] + [f"dX{i}" for i in range(4)]
                    + [f"dE{i}" for i in range(4)] + [f"dY{i}" for i in range(6)])
        for nm in semnames:
            SEMH[nm] = es.enter_context(nc.semaphore("sem_" + nm))
        cnt = {k: 0 for k in SEMH}
        prog = {e: [] for e in ENGS}

        def op(e, fn, waits=(), inc=None, inck=1):
            waits = list(waits)
            if e in ("v", "s", "g"):
                waits.append((e, cnt[e]))
            w = tuple((s_, v_) for (s_, v_) in waits if v_ > 0)
            val = None
            if inc is not None:
                cnt[inc] += inck
                val = cnt[inc]

            def run(eng, fn=fn, w=w, inc=inc, inck=inck):
                for sname, v in w:
                    eng.wait_ge(SEMH[sname], v)
                inst = fn(eng)
                if inc is not None:
                    inst.then_inc(SEMH[inc], inck)
            prog[e].append(run)
            return val

        last_barrier = [()]

        def barrier():
            vals = {k: cnt[k] for k in SEMH}
            for e in ("t", "v", "s", "g"):
                w = [(o, v) for o, v in vals.items() if o != e and v > 0]
                op(e, lambda eng: eng.nop(), waits=w, inc=e)
            def run(eng, vals=dict(vals)):
                for o, v in vals.items():
                    if v > 0:
                        eng.wait_ge(SEMH[o], v)
            prog["y"].append(run)
            last_barrier[0] = tuple((o, v) for o, v in vals.items() if v > 0)

        def dma_in(dst, src, sem, waits=(), q="y"):
            waits = list(waits) + list(last_barrier[0])
            return op(q, lambda e, d=dst, s_=src: e.dma_start(out=d, in_=s_),
                      waits=waits, inc=sem, inck=16)

        def dma_out(dst, src, sem, waits=()):
            waits = list(waits) + list(last_barrier[0])
            return op("g", lambda e, d=dst, s_=src: e.dma_start(out=d, in_=s_),
                      waits=waits, inc=sem, inck=16)

        sb = lambda n, sh, dt: es.enter_context(nc.sbuf_tensor(n, sh, dt))
        pt_ = lambda n, sh, dt: es.enter_context(nc.psum_tensor(n, sh, dt))

        LNW = sb("LNW", [128, 16], F32)
        LNB = sb("LNB", [128, 16], F32)
        GSC = sb("GSC", [128, 44], F32)
        USC = sb("USC", [128, 44], F32)
        DSC = sb("DSC", [128, 16], F32)
        EYE = sb("EYE", [128, 128], BF16)
        ONES = sb("ONES", [1, 128], BF16)
        ONESC = sb("ONESC", [128, 1], BF16)
        ROW = [sb(f"ROW{i}", [1, 1024], F32) for i in range(8)]
        rA, rB, rC, rD, rE, rF, rG, rH = ROW
        RMH = sb("RMH", [1, 2048], BF16)
        RML = sb("RML", [1, 2048], BF16)
        RN = sb("RN", [1, 32], F32)
        NEGB = sb("NEGB", [128, 2], F32)
        Z = sb("Z", [128, 2048], F32)
        SRH2 = sb("SRH2", [128, 512], BF16)

        PT = [pt_(f"PT{i}", [128, 512], F32) for i in range(3)]
        PTB = pt_("PTB", [128, 1024], BF16)
        PR = [pt_(f"PR{i}", [1, 512], F32) for i in range(4)]
        pt_free = [("t", 0)] * 3
        pr_free = [("t", 0)] * 4

        def pt_acquire(i, waits):
            return list(waits) + [pt_free[i]]

        for dst, src in ((LNW, lnw_i), (LNB, lnb_i), (GSC, gs_i), (USC, us_i),
                         (DSC, ds_i), (EYE, eye_i), (ONES, ones_i), (ONESC, onesc_i)):
            dma_in(dst[:], src[:], "dC")
        ld_const = cnt["dC"]

        def row_sqrt(dst, src, waits):
            return op("s", lambda e, d=dst, s_=src: e.activation(d, s_, AF.Sqrt),
                      waits=waits, inc="s")

        def bcast512(dst, rmh_sl, rml_sl, pti, waits):
            w = pt_acquire(pti, waits)
            op("t", lambda e, p=PT[pti], r=rmh_sl: e.matmul(p[:, 0:512], ONES[:], r,
                                                            start=True, stop=False),
               waits=w)
            tev = op("t", lambda e, p=PT[pti], r=rml_sl: e.matmul(p[:, 0:512], ONES[:],
                                                                  r, start=False,
                                                                  stop=True), inc="t")
            vev = op("v", lambda e, d=dst, p=PT[pti]: e.tensor_copy(d, p[:, 0:512]),
                     waits=[("t", tev)], inc="v")
            pt_free[pti] = ("v", vev)
            return vev

        def row_pair(rmh_sl, rml_sl, src, waits):
            h = op("v", lambda e, d=rmh_sl, s_=src: e.tensor_copy(d, s_),
                   waits=waits, inc="v")
            l = op("v", lambda e, d=rml_sl, s_=src, hh=rmh_sl:
                   e.tensor_tensor(d, s_, hh, op=OP.subtract), waits=[("v", h)],
                   inc="v")
            return h, l

        def ln_rows(pr_sum, pr_sq, n_tt, waits, s_dst=None, ms_dst=None):
            evs = []
            for tt in range(n_tt):
                a = op("v", lambda e, tt=tt, p=PR[pr_sum + 2 * tt]:
                       e.tensor_scalar(rA[0:1, tt * 512:(tt + 1) * 512], p[:], INV_D,
                                       None, op0=OP.mult), waits=waits, inc="v")
                b_ = op("v", lambda e, tt=tt, p=PR[pr_sq + 2 * tt]:
                        e.tensor_scalar(rB[0:1, tt * 512:(tt + 1) * 512], p[:], INV_D,
                                        None, op0=OP.mult), inc="v")
                pr_free[pr_sum + 2 * tt] = ("v", a)
                pr_free[pr_sq + 2 * tt] = ("v", b_)
                evs += [a, b_]
            n = n_tt * 512
            c1 = op("v", lambda e, n=n: e.tensor_tensor(rC[0:1, 0:n], rA[0:1, 0:n],
                                                        rA[0:1, 0:n], op=OP.mult),
                    waits=[("v", max(evs))], inc="v")
            c2 = op("v", lambda e, n=n: e.tensor_tensor(rB[0:1, 0:n], rB[0:1, 0:n],
                                                        rC[0:1, 0:n], op=OP.subtract),
                    waits=[("v", c1)], inc="v")
            c3 = op("v", lambda e, n=n: e.tensor_scalar(rC[0:1, 0:n], rB[0:1, 0:n],
                                                        EPS, None, op0=OP.add),
                    waits=[("v", c2)], inc="v")
            c4 = row_sqrt(rD[0:1, 0:n], rC[0:1, 0:n], [("v", c3)])
            c5 = op("v", lambda e, n=n: e.reciprocal(rC[0:1, 0:n], rD[0:1, 0:n]),
                    waits=[("s", c4)], inc="v")
            out = [c5]
            if s_dst is not None:
                out.append(op("v", lambda e, n=n, d=s_dst: e.tensor_copy(
                    d[0:1, 0:n], rC[0:1, 0:n]), waits=[("v", c5)], inc="v"))
            if ms_dst is not None:
                out.append(op("v", lambda e, n=n, d=ms_dst: e.tensor_tensor(
                    d[0:1, 0:n], rA[0:1, 0:n], rC[0:1, 0:n], op=OP.mult),
                    waits=[("v", c5)], inc="v"))
            return max(out)

        p1s = ExitStack()
        CQD = p1s.enter_context(nc.sbuf_tensor("CQD", [64, TOWN], F32))
        SQD = p1s.enter_context(nc.sbuf_tensor("SQD", [64, TOWN], F32))
        CKD = p1s.enter_context(nc.sbuf_tensor("CKD", [64, SQ], F32))
        SKD = p1s.enter_context(nc.sbuf_tensor("SKD", [64, SQ], F32))
        XSH = p1s.enter_context(nc.sbuf_tensor("XSH", [128, 16 * 1024], BF16))
        XSL = p1s.enter_context(nc.sbuf_tensor("XSL", [128, 16 * 1024], BF16))
        WF = p1s.enter_context(nc.sbuf_tensor("WF", [128, 4 * 512], F32))
        WH = p1s.enter_context(nc.sbuf_tensor("WH", [128, 2 * 16 * 256], BF16))
        WL = p1s.enter_context(nc.sbuf_tensor("WL", [128, 2 * 16 * 256], BF16))
        XT = p1s.enter_context(nc.sbuf_tensor("XT", [128, 2048], F32))
        YB = p1s.enter_context(nc.sbuf_tensor("YB", [128, 5 * 512], F32))
        EH = p1s.enter_context(nc.sbuf_tensor("EH", [128, 3 * 512], BF16))
        EL = p1s.enter_context(nc.sbuf_tensor("EL", [128, 3 * 512], BF16))

        dma_in(CQD[:], cosqd_i[:], "dC")
        dma_in(SQD[:], sinqd_i[:], "dC")
        dma_in(CKD[:], coskd_i[:], "dC")
        e4 = dma_in(SKD[:], sinkd_i[:], "dC")
        for TBL in (CKD, SKD):
            a = op("v", lambda e, t=TBL: e.tensor_scalar(Z[0:64, :], t[:], CK2, None,
                                                         op0=OP.mult),
                   waits=[("dC", e4)], inc="v")
            b_ = op("v", lambda e, t=TBL: e.tensor_scalar(t[:], t[:], CK1, None,
                                                          op0=OP.mult),
                    waits=[("v", a)], inc="v")
            op("v", lambda e, t=TBL: e.tensor_tensor(t[:], t[:], Z[0:64, :],
                                                     op=OP.add),
               waits=[("v", b_)], inc="v")
        tbl_ev = cnt["v"]

        yb_state = {"i": 0, "free": [("v", 0)] * 5}

        def yb_slot():
            j = yb_state["i"] % 5
            yb_state["i"] += 1
            return j, YB[:, j * 512:(j + 1) * 512]

        eh_state = {"i": 0, "free": [("dE0", 0)] * 3}

        def eh_slot():
            j = eh_state["i"] % 3
            eh_state["i"] += 1
            return j, EH[:, j * 512:(j + 1) * 512], EL[:, j * 512:(j + 1) * 512]

        def stats_dc(xv, src_waits, dc, tev_hist):
            j = dc % 2
            srh = XSH[:, j * 2048:j * 2048 + 1024]
            sqh = XSH[:, j * 2048 + 1024:(j + 1) * 2048]
            srl = XSL[:, j * 2048:j * 2048 + 1024]
            sql = XSL[:, j * 2048 + 1024:(j + 1) * 2048]
            zq = Z[:, j * 1024:(j + 1) * 1024]
            guard = [("t", tev_hist.get(dc - 2, 0))]
            hh = op("v", lambda e, s_=xv, d=srh: e.tensor_copy(d, s_),
                    waits=list(src_waits) + guard, inc="v")
            ll = op("g", lambda e, s_=xv, d=srl, dh=srh: e.tensor_tensor(
                d, s_, dh, op=OP.subtract), waits=[("v", hh)], inc="g")
            sq = op("v", lambda e, s_=xv, z=zq: e.tensor_tensor(z, s_, s_,
                                                                op=OP.mult),
                    waits=[("g", ll)], inc="v")
            qh = op("v", lambda e, z=zq, d=sqh: e.tensor_copy(d, z),
                    waits=[("v", sq)], inc="v")
            ql = op("g", lambda e, z=zq, d=sql, dh=sqh: e.tensor_tensor(
                d, z, dh, op=OP.subtract), waits=[("v", qh)], inc="g")
            first, last = (dc == 0), (dc == NDC - 1)
            tev = None
            for tt in range(2):
                sl = slice(tt * 512, (tt + 1) * 512)
                w0 = [pr_free[2 * tt]] if first else []
                w1 = [pr_free[2 * tt + 1]] if first else []
                op("t", lambda e, p=PR[2 * tt], f=first, d=srh, sl=sl: e.matmul(
                    p[:], ONESC[:], d[:, sl], start=f, stop=False),
                   waits=[("v", hh)] + w0)
                op("t", lambda e, p=PR[2 * tt], l_=last, d=srl, sl=sl: e.matmul(
                    p[:], ONESC[:], d[:, sl], start=False, stop=l_),
                   waits=[("g", ll)])
                op("t", lambda e, p=PR[2 * tt + 1], f=first, d=sqh, sl=sl: e.matmul(
                    p[:], ONESC[:], d[:, sl], start=f, stop=False),
                   waits=[("v", qh)] + w1)
                tev = op("t", lambda e, p=PR[2 * tt + 1], l_=last, d=sql, sl=sl:
                         e.matmul(p[:], ONESC[:], d[:, sl], start=False, stop=l_),
                         waits=[("g", ql)], inc="t")
            tev_hist[dc] = tev
            return tev

        xt_free = [("t", 0), ("t", 0)]
        for slab in range(2):
            xsrc = x_own if slab == 0 else x_oth
            tev_hist = {}
            last_t = 0
            for dc in range(NDC):
                sl_ = dc % 2
                w = [xt_free[sl_]] if (dc or slab) else [("v", tbl_ev),
                                                         ("dC", cnt["dC"])]
                dl = dma_in(XT[:, sl_ * 1024:(sl_ + 1) * 1024],
                            xsrc[dc * 128:(dc + 1) * 128, :], f"dX{sl_}", waits=w)
                last_t = stats_dc(XT[:, sl_ * 1024:(sl_ + 1) * 1024],
                                  [(f"dX{sl_}", dl)], dc, tev_hist)
                xt_free[sl_] = ("t", last_t)
            sev = ln_rows(0, 1, 2, [("t", last_t)],
                          s_dst=(rG if slab == 0 else None))
            if slab == 0:
                sev = op("v", lambda e: e.tensor_tensor(rH[0:1, :], rA[0:1, :],
                                                        rC[0:1, :], op=OP.mult),
                         waits=[("v", sev)], inc="v")
            d1 = op("v", lambda e: e.tensor_tensor(rD[0:1, :], rC[0:1, :], rC[0:1, :],
                                                   op=OP.mult), waits=[("v", sev)],
                    inc="v")
            d2 = op("v", lambda e: e.tensor_tensor(rD[0:1, :], rD[0:1, :], rB[0:1, :],
                                                   op=OP.mult), waits=[("v", d1)],
                    inc="v")
            d3 = op("v", lambda e: e.tensor_scalar(rD[0:1, :], rD[0:1, :], EPS, None,
                                                   op0=OP.add), waits=[("v", d2)],
                    inc="v")
            d4 = row_sqrt(rE[0:1, :], rD[0:1, :], [("v", d3)])
            d5 = op("v", lambda e: e.reciprocal(rD[0:1, :], rE[0:1, :]),
                    waits=[("s", d4)], inc="v")
            d6 = op("v", lambda e: e.tensor_tensor(rE[0:1, :], rC[0:1, :], rD[0:1, :],
                                                   op=OP.mult), waits=[("v", d5)],
                    inc="v")
            d7 = op("v", lambda e: e.tensor_tensor(rF[0:1, :], rA[0:1, :], rE[0:1, :],
                                                   op=OP.mult), waits=[("v", d6)],
                    inc="v")
            _, l1 = row_pair(RMH[0:1, 0:1024], RML[0:1, 0:1024], rE[0:1, :],
                             [("v", d7)])
            _, l2 = row_pair(RMH[0:1, 1024:2048], RML[0:1, 1024:2048], rF[0:1, :],
                             [("v", d7)])
            bevs = []
            for tt in range(2):
                bevs.append(bcast512(Z[:, tt * 512:(tt + 1) * 512],
                                     RMH[0:1, tt * 512:(tt + 1) * 512],
                                     RML[0:1, tt * 512:(tt + 1) * 512], 0,
                                     [("v", l1)]))
                bevs.append(bcast512(Z[:, 1024 + tt * 512:1024 + (tt + 1) * 512],
                                     RMH[0:1, 1024 + tt * 512:1024 + (tt + 1) * 512],
                                     RML[0:1, 1024 + tt * 512:1024 + (tt + 1) * 512],
                                     1, [("v", l2)]))
            bev = max(bevs)

            xt2_free = [("v", 0), ("v", 0)]
            prev_h, prev_l = 0, 0
            for dc in range(NDC):
                sl_ = dc % 2
                xv = XT[:, sl_ * 1024:(sl_ + 1) * 1024]
                w = [xt2_free[sl_]] + ([("t", last_t)] if dc < 2 else [])
                dl = dma_in(xv, xsrc[dc * 128:(dc + 1) * 128, :], f"dX{sl_}",
                            waits=w)
                t1 = op("v", lambda e, xv=xv, dl=dl, sl_=sl_: e.tensor_tensor(
                    xv, xv, Z[:, 0:1024], op=OP.mult),
                    waits=[(f"dX{sl_}", dl), ("v", bev)], inc="v")
                t2 = op("v", lambda e, xv=xv: e.tensor_tensor(
                    xv, xv, Z[:, 1024:2048], op=OP.subtract),
                    waits=[("v", t1)], inc="v")
                t3 = op("v", lambda e, xv=xv, dc=dc: e.tensor_scalar(
                    xv, xv, LNW[:, dc:dc + 1], LNB[:, dc:dc + 1],
                    op0=OP.mult, op1=OP.add), waits=[("v", t2)], inc="v")
                prev_h = op("v", lambda e, xv=xv, dc=dc: e.tensor_copy(
                    XSH[:, dc * 1024:(dc + 1) * 1024], xv), waits=[("v", t3)],
                    inc="v")
                prev_l = op("g", lambda e, xv=xv, dc=dc: e.tensor_tensor(
                    XSL[:, dc * 1024:(dc + 1) * 1024], xv,
                    XSH[:, dc * 1024:(dc + 1) * 1024], op=OP.subtract),
                    waits=[("v", prev_h)], inc="g")
                xt2_free[sl_] = ("g", prev_l)
            xs_hi_ev, xs_lo_ev = prev_h, prev_l

            wh_half_free = [("t", 0), ("t", 0)]
            wf_free = [("v", 0)] * 4

            def stream_half(csrc0, half, qscale):
                hs = ls = 0
                for dc in range(NDC):
                    slot = dc % 4
                    dsem = f"dW{slot}"
                    dl = dma_in(WF[:, slot * 512:slot * 512 + 256],
                                w_in[dc * 128:(dc + 1) * 128, csrc0:csrc0 + 256],
                                dsem, waits=[wf_free[slot]])
                    src = WF[:, slot * 512:slot * 512 + 256]
                    dsth = WH[:, half * 4096 + dc * 256:half * 4096 + (dc + 1) * 256]
                    dstl = WL[:, half * 4096 + dc * 256:half * 4096 + (dc + 1) * 256]
                    hw = [wh_half_free[half]] if dc == 0 else []
                    if qscale:
                        sc = op("v", lambda e, s_=src: e.tensor_scalar(
                            s_, s_, CQ, None, op0=OP.mult), waits=[(dsem, dl)],
                            inc="v")
                        hs = op("s", lambda e, s_=src, d=dsth: e.copy(d, s_),
                                waits=[("v", sc)] + hw, inc="s")
                    else:
                        hs = op("s", lambda e, s_=src, d=dsth: e.copy(d, s_),
                                waits=[(dsem, dl)] + hw, inc="s")
                    ls = op("v", lambda e, s_=src, d=dstl, dh=dsth: e.tensor_tensor(
                        d, s_, dh, op=OP.subtract), waits=[("s", hs)], inc="v")
                    wf_free[slot] = ("v", ls)
                return hs, ls

            def rope_evac(pti, fo, tt, base_t, is_q, gev):
                ps = PT[pti]
                j, o32 = yb_slot()
                jw = yb_state["free"][j]
                j2, tmp = yb_slot()
                j2w = yb_state["free"][j2]
                if is_q:
                    CT, ST = CQD, SQD
                    tsl = slice(tt * 512, (tt + 1) * 512)
                else:
                    CT, ST = CKD, SKD
                    tsl = slice(base_t + tt * 512, base_t + (tt + 1) * 512)
                w0 = [("t", gev), jw, j2w]
                op("v", lambda e, o=o32, p=ps, c=CT, ts=tsl: e.tensor_tensor(
                    o[0:32, :], p[0:32, :], c[0:32, ts], op=OP.mult), waits=w0,
                    inc="v")
                a2 = op("v", lambda e, o=tmp, p=ps, s_=ST, ts=tsl: e.tensor_tensor(
                    o[0:32, :], p[32:64, :], s_[0:32, ts], op=OP.mult), inc="v")
                a3 = op("v", lambda e, o=o32, t_=tmp: e.tensor_tensor(
                    o[0:32, :], o[0:32, :], t_[0:32, :], op=OP.subtract),
                    waits=[("v", a2)], inc="v")
                a4 = op("v", lambda e, o=o32, p=ps, c=CT, ts=tsl: e.tensor_tensor(
                    o[32:64, :], p[32:64, :], c[32:64, ts], op=OP.mult), inc="v")
                a5 = op("v", lambda e, o=tmp, p=ps, s_=ST, ts=tsl: e.tensor_tensor(
                    o[32:64, :], p[0:32, :], s_[32:64, ts], op=OP.mult), inc="v")
                a6 = op("v", lambda e, o=o32, t_=tmp: e.tensor_tensor(
                    o[32:64, :], o[32:64, :], t_[32:64, :], op=OP.add),
                    waits=[("v", a5)], inc="v")
                if is_q:
                    pv = op("s", lambda e, o=o32, p=ps: e.copy(o[64:128, :],
                                                               p[64:128, :]),
                            waits=w0, inc="s")
                    pass_ev = [("s", pv)]
                else:
                    b1 = op("s", lambda e, o=o32, p=ps: e.activation(
                        o[64:128, :], p[64:128, :], AF.Copy, scale=CK1),
                        waits=w0, inc="s")
                    b2 = op("s", lambda e, o=tmp, p=ps: e.activation(
                        o[64:128, :], p[64:128, :], AF.Copy, scale=CK2), inc="s")
                    b3 = op("v", lambda e, o=o32, t_=tmp: e.tensor_tensor(
                        o[64:128, :], o[64:128, :], t_[64:128, :], op=OP.add),
                        waits=[("s", b2), ("s", b1)], inc="v")
                    pass_ev = [("v", b3)]
                je, ehs, els = eh_slot()
                ew = eh_state["free"][je]
                hh = op("v", lambda e, d=ehs, o=o32: e.tensor_copy(d, o[:]),
                        waits=[("v", a3), ("v", a6)] + pass_ev + [ew], inc="v")
                ll = op("g", lambda e, d=els, o=o32, dh=ehs: e.tensor_tensor(
                    d, o[:], dh, op=OP.subtract), waits=[("v", hh)], inc="g")
                yb_state["free"][j] = ("g", ll)
                yb_state["free"][j2] = ("g", ll)
                if is_q:
                    dsth = q_hi_s[fo][:, tt * 512:(tt + 1) * 512]
                    dstl = q_lo_s[fo][:, tt * 512:(tt + 1) * 512]
                else:
                    dsth = k_hi_s[fo][:, base_t + tt * 512:base_t + (tt + 1) * 512]
                    dstl = k_lo_s[fo][:, base_t + tt * 512:base_t + (tt + 1) * 512]
                esem = f"dE{je}"
                dma_out(dsth, ehs, esem, waits=[("v", hh)])
                d2_ = dma_out(dstl, els, esem, waits=[("g", ll)])
                eh_state["free"][je] = (esem, d2_)
                return ("v", hh)

            def v_evac(pti, fo, tt, base_t, gev):
                ps = PT[pti]
                je, ehs, els = eh_slot()
                ew = eh_state["free"][je]
                hh = op("v", lambda e, d=ehs, p=ps: e.tensor_copy(d, p[:]),
                        waits=[("t", gev), ew], inc="v")
                ll = op("v", lambda e, d=els, p=ps, dh=ehs: e.tensor_tensor(
                    d, p[:], dh, op=OP.subtract), waits=[("v", hh)], inc="v")
                esem = f"dE{je}"
                dma_out(v_hi_s[fo][:, base_t + tt * 512:base_t + (tt + 1) * 512],
                        ehs, esem, waits=[("v", hh)])
                d2_ = dma_out(v_lo_s[fo][:, base_t + tt * 512:base_t + (tt + 1) * 512],
                              els, esem, waits=[("v", ll)])
                eh_state["free"][je] = (esem, d2_)
                return ("v", ll)

            pti_i = [0]

            def projection(kind):
                col0 = {"q": 0, "k": D, "v": 2 * D}[kind]
                base_t = slab * 1024
                ev = {0: stream_half(col0, 0, kind == "q")}
                for hb in range(8):
                    half = hb % 2
                    if hb + 1 < 8:
                        ev[hb + 1] = stream_half(col0 + (hb + 1) * 256,
                                                 (hb + 1) % 2, kind == "q")
                    hs, ls = ev[hb]
                    last_g = 0
                    for fo_loc in range(2):
                        fo = hb * 2 + fo_loc
                        for tt in range(2):
                            pti = pti_i[0] % 3
                            pti_i[0] += 1
                            w = pt_acquire(pti, [("s", hs), ("v", ls),
                                                 ("v", xs_hi_ev), ("g", xs_lo_ev)])
                            passes = [(WH, XSH), (WH, XSL), (WL, XSH)]
                            gev = None
                            for pi, (LB, RB) in enumerate(passes):
                                for dc in range(NDC):
                                    first = (pi == 0 and dc == 0)
                                    last = (pi == 2 and dc == NDC - 1)
                                    lhsT = LB[:, half * 4096 + dc * 256 + fo_loc * 128:
                                              half * 4096 + dc * 256 + fo_loc * 128
                                              + 128]
                                    rhs = RB[:, dc * 1024 + tt * 512:
                                             dc * 1024 + (tt + 1) * 512]
                                    gev = op("t", lambda e, p=PT[pti], l_=lhsT, r=rhs,
                                             f=first, st=last: e.matmul(
                                                 p[:], l_, r, start=f, stop=st),
                                             waits=w if first else (),
                                             inc="t" if last else None)
                            last_g = gev
                            if kind == "v":
                                pt_free[pti] = v_evac(pti, fo, tt, base_t, gev)
                            else:
                                pt_free[pti] = rope_evac(pti, fo, tt, base_t,
                                                         kind == "q", gev)
                    wh_half_free[half] = ("t", last_g)

            if slab == 0:
                projection("q")
            projection("k")
            projection("v")
            barrier()

        p1s.close()
        barrier()

        p23 = ExitStack()
        CTXH = p23.enter_context(nc.sbuf_tensor("CTXH", [128, 16 * 1024], BF16))
        CTXL = p23.enter_context(nc.sbuf_tensor("CTXL", [128, 16 * 1024], BF16))
        p2s = ExitStack()
        PH = p2s.enter_context(nc.sbuf_tensor("PH", [128, 16 * 512], BF16))
        PL = p2s.enter_context(nc.sbuf_tensor("PL", [128, 16 * 512], BF16))
        KH = [p2s.enter_context(nc.sbuf_tensor(f"KH{i}", [128, SQ], BF16))
              for i in range(2)]
        KL = [p2s.enter_context(nc.sbuf_tensor(f"KL{i}", [128, SQ], BF16))
              for i in range(2)]
        VFH = p2s.enter_context(nc.sbuf_tensor("VFH", [128, SQ], BF16))
        VFL = p2s.enter_context(nc.sbuf_tensor("VFL", [128, SQ], BF16))
        QHB = [p2s.enter_context(nc.sbuf_tensor(f"QHB{i}", [128, TOWN], BF16))
               for i in range(2)]
        QLB = [p2s.enter_context(nc.sbuf_tensor(f"QLB{i}", [128, TOWN], BF16))
               for i in range(2)]
        VTH = [p2s.enter_context(nc.sbuf_tensor(f"VTH{i}", [128, SQ], BF16))
               for i in range(2)]
        VTL = [p2s.enter_context(nc.sbuf_tensor(f"VTL{i}", [128, SQ], BF16))
               for i in range(2)]
        Y2 = p2s.enter_context(nc.sbuf_tensor("Y2", [128, 4 * 512], F32))

        y2_state = {"i": 0, "free": [("v", 0)] * 4}

        def y2_slot():
            j = y2_state["i"] % 4
            y2_state["i"] += 1
            return j, Y2[:, j * 512:(j + 1) * 512]

        head_ld = [0, 0]
        head_cmp = [("t", 0), ("t", 0)]

        def load_head(h, vf_guard):
            p = h % 2
            hsem = f"dH{p}"
            dma_in(KH[p][:], k_hi_s[h][:, :], hsem, waits=[head_cmp[p]])
            dma_in(KL[p][:], k_lo_s[h][:, :], hsem)
            dma_in(VFH[:], v_hi_s[h][:, :], hsem, waits=[vf_guard])
            dma_in(VFL[:], v_lo_s[h][:, :], hsem)
            dma_in(QHB[p][:], q_hi_s[h][:, :], hsem)
            head_ld[p] = dma_in(QLB[p][:], q_lo_s[h][:, :], hsem)

        load_head(0, ("v", 0))
        ptb_free = ("v", 0)
        vt_ev = {}
        negb_ev = {}

        def prelude(h):
            nonlocal ptb_free
            par = h % 2
            ldw = [(f"dH{par}", head_ld[par])]
            for src_, dst in ((VFH, VTH[par]), (VFL, VTL[par])):
                for rnd in range(2):
                    tev = None
                    for tc8 in range(8):
                        tc = rnd * 8 + tc8
                        tev = op("t", lambda e, s_=src_, tc=tc, tc8=tc8: e.transpose(
                            PTB[:, tc8 * 128:(tc8 + 1) * 128],
                            s_[:, tc * 128:(tc + 1) * 128], EYE[:]),
                            waits=(ldw + [ptb_free]) if tc8 == 0 else (),
                            inc="t" if tc8 == 7 else None)
                    vv = op("v", lambda e, d=dst, rnd=rnd: e.tensor_copy(
                        d[:, rnd * 1024:(rnd + 1) * 1024], PTB[:]),
                        waits=[("t", tev)], inc="v")
                    ptb_free = ("v", vv)
            vt_ev[h] = cnt["v"]
            o = par * 12
            prev_m3 = 0
            nrm_evs = []
            for i, (src_, n) in enumerate(((KH[par], 4), (QHB[par], 2))):
                for j in range(n):
                    jj, ys = y2_slot()
                    yw = y2_state["free"][jj]
                    m1 = op("v", lambda e, s_=src_, j=j, ys=ys: e.tensor_tensor(
                        ys, s_[:, j * 512:(j + 1) * 512],
                        s_[:, j * 512:(j + 1) * 512], op=OP.mult),
                        waits=[yw] + ldw, inc="v")
                    m2 = op("v", lambda e, ys=ys: e.tensor_copy(SRH2[:], ys),
                            waits=[("v", m1), ("t", prev_m3)], inc="v")
                    y2_state["free"][jj] = ("v", m2)
                    m3 = op("t", lambda e: e.matmul(PR[3][:], ONESC[:], SRH2[:],
                                                    start=True, stop=True),
                            waits=[("v", m2), pr_free[3]], inc="t")
                    prev_m3 = m3
                    idx = o + i * 4 + j
                    m4 = op("v", lambda e, idx=idx: e.reduce_max(
                        RN[0:1, idx:idx + 1], PR[3][:], axis=AX.X),
                        waits=[("t", m3)], inc="v")
                    pr_free[3] = ("v", m4)
                    nrm_evs.append(m4)
            c1 = op("v", lambda e, o=o: e.tensor_max(RN[0:1, o:o + 1],
                                                     RN[0:1, o:o + 1],
                                                     RN[0:1, o + 1:o + 2]),
                    waits=[("v", max(nrm_evs))], inc="v")
            op("v", lambda e, o=o: e.tensor_max(RN[0:1, o + 2:o + 3],
                                                RN[0:1, o + 2:o + 3],
                                                RN[0:1, o + 3:o + 4]), inc="v")
            c3 = op("v", lambda e, o=o: e.tensor_max(RN[0:1, o:o + 1],
                                                     RN[0:1, o:o + 1],
                                                     RN[0:1, o + 2:o + 3]), inc="v")
            c4 = op("v", lambda e, o=o: e.tensor_max(RN[0:1, o + 4:o + 5],
                                                     RN[0:1, o + 4:o + 5],
                                                     RN[0:1, o + 5:o + 6]), inc="v")
            c5 = op("v", lambda e, o=o: e.tensor_tensor(
                RN[0:1, o + 8:o + 9], RN[0:1, o:o + 1], RN[0:1, o + 4:o + 5],
                op=OP.add), waits=[("v", c3), ("v", c4)], inc="v")
            c6 = op("v", lambda e, o=o: e.tensor_scalar(
                RN[0:1, o + 9:o + 10], RN[0:1, o + 8:o + 9], -0.51, -0.5,
                op0=OP.mult, op1=OP.add), waits=[("v", c5)], inc="v")
            c7 = op("v", lambda e, o=o, par=par: e.tensor_copy(
                RMH[0:1, 1024 + par:1025 + par], RN[0:1, o + 9:o + 10]),
                waits=[("v", c6)], inc="v")
            c8 = op("t", lambda e, par=par: e.matmul(
                PT[0][:, 0:1], ONES[:], RMH[0:1, 1024 + par:1025 + par],
                start=True, stop=True), waits=pt_acquire(0, [("v", c7)]), inc="t")
            c9 = op("v", lambda e, par=par: e.tensor_copy(NEGB[:, par:par + 1],
                                                          PT[0][:, 0:1]),
                    waits=[("t", c8)], inc="v")
            pt_free[0] = ("v", c9)
            negb_ev[h] = c9

        prelude(0)
        for h in range(H):
            par = h % 2
            if h + 1 < H:
                load_head(h + 1, ("v", vt_ev[h]))
            head_last = []
            for qt in range(2):
                if qt == 1 and h + 1 < H:
                    prelude(h + 1)
                ctxw = pt_acquire(2, [])
                prw0 = [pr_free[0]]
                sum_gev = ctx_gev = None
                ll_ev = {}

                def consume(kc):
                    pti = kc % 2
                    eh = op("s", lambda e, p=PT[pti], kc=kc, par=par: e.activation(
                        PH[:, kc * 512:(kc + 1) * 512], p[:], AF.Exp,
                        bias=NEGB[:, par:par + 1]), waits=[("t", sc_ev[kc])],
                        inc="s")
                    jj, ys = y2_slot()
                    yw = y2_state["free"][jj]
                    e32 = op("s", lambda e, p=PT[pti], ys=ys, par=par: e.activation(
                        ys, p[:], AF.Exp, bias=NEGB[:, par:par + 1]), waits=[yw],
                        inc="s")
                    pt_free[pti] = ("s", e32)
                    ll_ev[kc] = op("v", lambda e, ys=ys, kc=kc: e.tensor_tensor(
                        PL[:, kc * 512:(kc + 1) * 512], ys,
                        PH[:, kc * 512:(kc + 1) * 512], op=OP.subtract),
                        waits=[("s", e32)], inc="v")
                    y2_state["free"][jj] = ("v", ll_ev[kc])
                    op("t", lambda e, kc=kc, par=par: e.matmul(
                        PT[2][:], VTH[par][:, kc * 128:(kc + 1) * 128],
                        PH[:, kc * 512:(kc + 1) * 512], start=(kc == 0),
                        stop=False), waits=[("s", eh)] + (ctxw if kc == 0 else []))
                    op("t", lambda e, kc=kc, par=par: e.matmul(
                        PT[2][:], VTL[par][:, kc * 128:(kc + 1) * 128],
                        PH[:, kc * 512:(kc + 1) * 512], start=False, stop=False))
                    op("t", lambda e, kc=kc: e.matmul(
                        PR[0][:], ONESC[:], PH[:, kc * 512:(kc + 1) * 512],
                        start=(kc == 0), stop=False),
                        waits=(prw0 if kc == 0 else []))

                sc_ev = {}
                for kc in range(16):
                    pti = kc % 2
                    w = pt_acquire(pti, [("v", vt_ev[h]), ("v", negb_ev[h])]
                                   if kc < 2 else [])
                    op("t", lambda e, p=PT[pti], k_=KH[par], q_=QHB[par], kc=kc,
                       qt=qt: e.matmul(p[:], k_[:, kc * 128:(kc + 1) * 128],
                                       q_[:, qt * 512:(qt + 1) * 512], start=True,
                                       stop=False), waits=w)
                    op("t", lambda e, p=PT[pti], k_=KH[par], q_=QLB[par], kc=kc,
                       qt=qt: e.matmul(p[:], k_[:, kc * 128:(kc + 1) * 128],
                                       q_[:, qt * 512:(qt + 1) * 512], start=False,
                                       stop=False))
                    sc_ev[kc] = op("t", lambda e, p=PT[pti], k_=KL[par],
                                   q_=QHB[par], kc=kc, qt=qt: e.matmul(
                                       p[:], k_[:, kc * 128:(kc + 1) * 128],
                                       q_[:, qt * 512:(qt + 1) * 512], start=False,
                                       stop=True), inc="t")
                    if kc >= 1:
                        consume(kc - 1)
                consume(15)
                for kc in range(16):
                    ctx_gev = op("t", lambda e, kc=kc, par=par: e.matmul(
                        PT[2][:], VTH[par][:, kc * 128:(kc + 1) * 128],
                        PL[:, kc * 512:(kc + 1) * 512], start=False,
                        stop=(kc == 15)),
                        waits=[("v", ll_ev[15])] if kc == 0 else (),
                        inc="t" if kc == 15 else None) or ctx_gev
                    sum_gev = op("t", lambda e, kc=kc: e.matmul(
                        PR[0][:], ONESC[:], PL[:, kc * 512:(kc + 1) * 512],
                        start=False, stop=(kc == 15)),
                        inc="t" if kc == 15 else None) or sum_gev
                jj, c32 = y2_slot()
                yw = y2_state["free"][jj]
                ce = op("v", lambda e, c=c32: e.tensor_copy(c, PT[2][:]),
                        waits=[("t", ctx_gev), yw], inc="v")
                pt_free[2] = ("v", ce)
                s1 = op("v", lambda e: e.tensor_copy(rA[0:1, 0:512], PR[0][:]),
                        waits=[("t", sum_gev)], inc="v")
                pr_free[0] = ("v", s1)
                s2 = op("v", lambda e: e.reciprocal(rA[0:1, 512:1024],
                                                    rA[0:1, 0:512]),
                        waits=[("v", s1)], inc="v")
                _, rl = row_pair(RMH[0:1, 0:512], RML[0:1, 0:512],
                                 rA[0:1, 512:1024], [("v", s2)])
                bv = bcast512(Z[:, 0:512], RMH[0:1, 0:512], RML[0:1, 0:512], 0,
                              [("v", rl)])
                e1_ = op("v", lambda e, c=c32: e.tensor_tensor(
                    c, c, Z[:, 0:512], op=OP.mult), waits=[("v", bv)], inc="v")
                ch = op("v", lambda e, c=c32, h=h, qt=qt: e.tensor_copy(
                    CTXH[:, h * 1024 + qt * 512:h * 1024 + (qt + 1) * 512], c),
                    waits=[("v", e1_)], inc="v")
                cl = op("g", lambda e, c=c32, h=h, qt=qt: e.tensor_tensor(
                    CTXL[:, h * 1024 + qt * 512:h * 1024 + (qt + 1) * 512], c,
                    CTXH[:, h * 1024 + qt * 512:h * 1024 + (qt + 1) * 512],
                    op=OP.subtract), waits=[("v", ch)], inc="g")
                y2_state["free"][jj] = ("g", cl)
                head_last.append(cl)
            head_cmp[par] = ("g", max(head_last))

        barrier()
        p2s.close()
        barrier()

        p3s = ExitStack()
        WF3 = p3s.enter_context(nc.sbuf_tensor("WF3", [128, 4 * 512], F32))
        WH3 = p3s.enter_context(nc.sbuf_tensor("WH3", [128, 2 * 16 * 256], BF16))
        WL3 = p3s.enter_context(nc.sbuf_tensor("WL3", [128, 2 * 16 * 256], BF16))
        XT3 = p3s.enter_context(nc.sbuf_tensor("XT3", [128, 1024], F32))
        Y3 = p3s.enter_context(nc.sbuf_tensor("Y3", [128, 6 * 512], F32))
        SRH3 = p3s.enter_context(nc.sbuf_tensor("SRH3", [128, 3 * 512], BF16))
        SRL3 = p3s.enter_context(nc.sbuf_tensor("SRL3", [128, 3 * 512], BF16))
        SQH3 = p3s.enter_context(nc.sbuf_tensor("SQH3", [128, 3 * 512], BF16))
        SQL3 = p3s.enter_context(nc.sbuf_tensor("SQL3", [128, 3 * 512], BF16))

        y3_state = {"i": 0, "free": [("v", 0)] * 6}

        def y3_slot():
            j = y3_state["i"] % 6
            y3_state["i"] += 1
            return j, Y3[:, j * 512:(j + 1) * 512]

        class DeferStats:

            def __init__(self, SH, SL, QH_, QL_, slot_fn, state):
                self.SH, self.SL, self.QH, self.QL = SH, SL, QH_, QL_
                self.slot_fn, self.state = slot_fn, state
                self.pend = []
                self.idx = 0
                self.tev = {}

            def prep(self, src32, src_waits, pr_sum, pr_sq, first, last):
                i = self.idx
                self.idx += 1
                sl = slice((i % 3) * 512, (i % 3) * 512 + 512)
                guard = [("t", self.tev.get(i - 3, 0))]
                hh = op("v", lambda e, s_=src32, SH=self.SH, sl=sl:
                        e.tensor_copy(SH[:, sl], s_),
                        waits=list(src_waits) + guard, inc="v")
                ll = op("g", lambda e, s_=src32, SH=self.SH, SL=self.SL, sl=sl:
                        e.tensor_tensor(SL[:, sl], s_, SH[:, sl], op=OP.subtract),
                        waits=[("v", hh)], inc="g")
                jj, ys = self.slot_fn()
                yw = self.state["free"][jj]
                sq = op("v", lambda e, s_=src32, ys=ys: e.tensor_tensor(
                    ys, s_, s_, op=OP.mult), waits=[("g", ll), yw], inc="v")
                qh = op("v", lambda e, ys=ys, QH_=self.QH, sl=sl:
                        e.tensor_copy(QH_[:, sl], ys), waits=[("v", sq)], inc="v")
                ql = op("g", lambda e, ys=ys, QH_=self.QH, QL_=self.QL, sl=sl:
                        e.tensor_tensor(QL_[:, sl], ys, QH_[:, sl],
                                        op=OP.subtract), waits=[("v", qh)], inc="g")
                self.state["free"][jj] = ("g", ql)
                self.pend.append((i, sl, pr_sum, pr_sq, first, last, hh, ll, qh, ql))
                if len(self.pend) > 2:
                    self.flush_one()

            def flush_one(self):
                i, sl, pr_sum, pr_sq, first, last, hh, ll, qh, ql = self.pend.pop(0)
                w0 = [pr_free[pr_sum]] if first else []
                w1 = [pr_free[pr_sq]] if first else []
                op("t", lambda e, p=PR[pr_sum], f=first, SH=self.SH, sl=sl:
                   e.matmul(p[:], ONESC[:], SH[:, sl], start=f, stop=False),
                   waits=[("v", hh)] + w0)
                op("t", lambda e, p=PR[pr_sum], l_=last, SL=self.SL, sl=sl:
                   e.matmul(p[:], ONESC[:], SL[:, sl], start=False, stop=l_),
                   waits=[("g", ll)])
                op("t", lambda e, p=PR[pr_sq], f=first, QH_=self.QH, sl=sl:
                   e.matmul(p[:], ONESC[:], QH_[:, sl], start=f, stop=False),
                   waits=[("v", qh)] + w1)
                tev = op("t", lambda e, p=PR[pr_sq], l_=last, QL_=self.QL, sl=sl:
                         e.matmul(p[:], ONESC[:], QL_[:, sl], start=False, stop=l_),
                         waits=[("g", ql)], inc="t")
                self.tev[i] = tev
                return tev

            def flush(self):
                tev = 0
                while self.pend:
                    tev = self.flush_one()
                return tev

        _, l1 = row_pair(RMH[0:1, 0:1024], RML[0:1, 0:1024], rG[0:1, :], [])
        _, l2 = row_pair(RMH[0:1, 1024:2048], RML[0:1, 1024:2048], rH[0:1, :], [])
        bev3s = []
        for tt in range(2):
            bev3s.append(bcast512(Z[:, tt * 512:(tt + 1) * 512],
                                  RMH[0:1, tt * 512:(tt + 1) * 512],
                                  RML[0:1, tt * 512:(tt + 1) * 512], 0, [("v", l1)]))
            bev3s.append(bcast512(Z[:, 1024 + tt * 512:1024 + (tt + 1) * 512],
                                  RMH[0:1, 1024 + tt * 512:1024 + (tt + 1) * 512],
                                  RML[0:1, 1024 + tt * 512:1024 + (tt + 1) * 512],
                                  1, [("v", l2)]))
        bev3 = max(bev3s)

        wh3_half_free = [("t", 0), ("t", 0)]
        wf3_free = [("v", 0)] * 4
        xt3_free = ("v", 0)
        pti3 = [0]
        last3 = 0

        def stream3(hb):
            half = hb % 2
            hs = ls = 0
            for dc in range(NDC):
                slot = dc % 4
                dsem = f"dW{slot}"
                dl = dma_in(WF3[:, slot * 512:slot * 512 + 256],
                            w_out[dc * 128:(dc + 1) * 128, hb * 256:hb * 256 + 256],
                            dsem, waits=[wf3_free[slot]])
                src = WF3[:, slot * 512:slot * 512 + 256]
                dsth = WH3[:, half * 4096 + dc * 256:half * 4096 + (dc + 1) * 256]
                dstl = WL3[:, half * 4096 + dc * 256:half * 4096 + (dc + 1) * 256]
                hw = [wh3_half_free[half]] if dc == 0 else []
                hs = op("s", lambda e, s_=src, d=dsth: e.copy(d, s_),
                        waits=[(dsem, dl)] + hw, inc="s")
                ls = op("v", lambda e, s_=src, d=dstl, dh=dsth: e.tensor_tensor(
                    d, s_, dh, op=OP.subtract), waits=[("s", hs)], inc="v")
                wf3_free[slot] = ("v", ls)
            return hs, ls

        ds3 = DeferStats(SRH3, SRL3, SQH3, SQL3, y3_slot, y3_state)
        ev3 = {0: stream3(0)}
        for hb in range(8):
            half = hb % 2
            if hb + 1 < 8:
                ev3[hb + 1] = stream3(hb + 1)
            hs, ls = ev3[hb]
            last_g = 0
            for fo_loc in range(2):
                fo = hb * 2 + fo_loc
                dlx = dma_in(XT3[:], x_own[fo * 128:(fo + 1) * 128, :], "dX0",
                             waits=[xt3_free])
                ao_evs = []
                for tt in range(2):
                    pti = pti3[0] % 3
                    pti3[0] += 1
                    w = pt_acquire(pti, [("s", hs), ("v", ls)])
                    passes = [(WH3, CTXH), (WH3, CTXL), (WL3, CTXH)]
                    gev = None
                    for pi, (LB, RB) in enumerate(passes):
                        for dc in range(NDC):
                            first = (pi == 0 and dc == 0)
                            last = (pi == 2 and dc == NDC - 1)
                            lhsT = LB[:, half * 4096 + dc * 256 + fo_loc * 128:
                                      half * 4096 + dc * 256 + fo_loc * 128 + 128]
                            rhs = RB[:, dc * 1024 + tt * 512:dc * 1024 +
                                     (tt + 1) * 512]
                            gev = op("t", lambda e, p=PT[pti], l_=lhsT, r=rhs,
                                     f=first, st=last: e.matmul(p[:], l_, r,
                                                                start=f, stop=st),
                                     waits=w if first else (),
                                     inc="t" if last else None)
                    last_g = gev
                    jj, ys = y3_slot()
                    yw = y3_state["free"][jj]
                    a1 = op("v", lambda e, ys=ys, tt=tt: e.tensor_tensor(
                        ys, XT3[:, tt * 512:(tt + 1) * 512],
                        Z[:, tt * 512:(tt + 1) * 512], op=OP.mult),
                        waits=[("dX0", dlx), yw, ("v", bev3)], inc="v")
                    a2 = op("v", lambda e, ys=ys, tt=tt: e.tensor_tensor(
                        ys, ys, Z[:, 1024 + tt * 512:1024 + (tt + 1) * 512],
                        op=OP.subtract), waits=[("v", a1)], inc="v")
                    a3 = op("v", lambda e, ys=ys, p=PT[pti]: e.tensor_tensor(
                        ys, ys, p[:], op=OP.add), waits=[("v", a2), ("t", gev)],
                        inc="v")
                    pt_free[pti] = ("v", a3)
                    ds3.prep(ys, [("v", a3)], 2 * tt, 2 * tt + 1,
                             first=(fo == 0), last=(fo == 15))
                    ysem = f"dY{jj}"
                    de = dma_out(ao_s[fo * 128:(fo + 1) * 128,
                                      tt * 512:(tt + 1) * 512], ys, ysem,
                                 waits=[("v", cnt["v"]), ("g", cnt["g"])])
                    y3_state["free"][jj] = (ysem, de)
                    ao_evs.append(a1)
                xt3_free = ("v", max(ao_evs))
            wh3_half_free[half] = ("t", last_g)
        last3 = ds3.flush()
        ln_rows(0, 1, 2, [("t", last3)])
        op("v", lambda e: e.tensor_copy(rG[0:1, :], rC[0:1, :]), inc="v")
        op("v", lambda e: e.tensor_tensor(rH[0:1, :], rA[0:1, :], rC[0:1, :],
                                          op=OP.mult), inc="v")
        barrier()
        p3s.close()
        p23.close()
        barrier()

        p4s = ExitStack()
        YSH = p4s.enter_context(nc.sbuf_tensor("YSH", [128, 16 * 512], BF16))
        YSL = p4s.enter_context(nc.sbuf_tensor("YSL", [128, 16 * 512], BF16))
        HH = p4s.enter_context(nc.sbuf_tensor("HH", [128, 8 * 512], BF16))
        HL = p4s.enter_context(nc.sbuf_tensor("HL", [128, 8 * 512], BF16))
        DA = p4s.enter_context(nc.sbuf_tensor("DA", [128, 16 * 512], F32))
        WF4 = p4s.enter_context(nc.sbuf_tensor("WF4", [128, 4 * 512], F32))
        WH4 = p4s.enter_context(nc.sbuf_tensor("WH4", [128, 3 * 16 * 256], BF16))
        XT4 = p4s.enter_context(nc.sbuf_tensor("XT4", [128, 4 * 512], F32))
        Y4 = p4s.enter_context(nc.sbuf_tensor("Y4", [128, 6 * 512], F32))
        SRH4 = p4s.enter_context(nc.sbuf_tensor("SRH4", [128, 3 * 512], BF16))
        SRL4 = p4s.enter_context(nc.sbuf_tensor("SRL4", [128, 3 * 512], BF16))
        SQH4 = p4s.enter_context(nc.sbuf_tensor("SQH4", [128, 3 * 512], BF16))
        SQL4 = p4s.enter_context(nc.sbuf_tensor("SQL4", [128, 3 * 512], BF16))

        y4_state = {"i": 0, "free": [("v", 0)] * 6}

        def y4_slot():
            j = y4_state["i"] % 6
            y4_state["i"] += 1
            return j, Y4[:, j * 512:(j + 1) * 512]

        xt4_state = {"i": 0, "free": [("v", 0)] * 4}

        def xt4_slot():
            j = xt4_state["i"] % 4
            xt4_state["i"] += 1
            return j, XT4[:, j * 512:(j + 1) * 512]

        def make_y(fo, tt, bw):
            ja, at = xt4_slot()
            aw = xt4_state["free"][ja]
            jx, xt = xt4_slot()
            xw = xt4_state["free"][jx]
            d1 = dma_in(at, ao_s[fo * 128:(fo + 1) * 128, tt * 512:(tt + 1) * 512],
                        f"dX{ja}", waits=[aw], q="y")
            d2 = dma_in(xt, x_own[fo * 128:(fo + 1) * 128, tt * 512:(tt + 1) * 512],
                        f"dX{jx}", waits=[xw], q="g")
            jy, ys = y4_slot()
            yw = y4_state["free"][jy]
            a1 = op("v", lambda e, ys=ys, at=at: e.tensor_tensor(
                ys, at, Z[:, 1024:1536], op=OP.mult),
                waits=[(f"dX{ja}", d1), yw, bw], inc="v")
            a2 = op("v", lambda e, ys=ys: e.tensor_tensor(
                ys, ys, Z[:, 1536:2048], op=OP.subtract), waits=[("v", a1)], inc="v")
            a3 = op("v", lambda e, ys=ys, xt=xt: e.tensor_tensor(
                ys, ys, xt, op=OP.add), waits=[("v", a2), (f"dX{jx}", d2)], inc="v")
            xt4_state["free"][ja] = ("v", a1)
            xt4_state["free"][jx] = ("v", a3)
            return jy, ys, a3

        QPART = [(0, 8), (8, 8), (16, 8), (24, 8), (32, 8), (40, 4)]

        for tt in range(2):
            _, l1 = row_pair(RMH[0:1, 0:512], RML[0:1, 0:512],
                             rG[0:1, tt * 512:(tt + 1) * 512], [])
            _, l2 = row_pair(RMH[0:1, 512:1024], RML[0:1, 512:1024],
                             rH[0:1, tt * 512:(tt + 1) * 512], [])
            bcast512(Z[:, 1024:1536], RMH[0:1, 0:512], RML[0:1, 0:512], 0,
                     [("v", l1)])
            b2v = bcast512(Z[:, 1536:2048], RMH[0:1, 512:1024], RML[0:1, 512:1024],
                           1, [("v", l2)])
            bw = ("v", b2v)
            ds4 = DeferStats(SRH4, SRL4, SQH4, SQL4, y4_slot, y4_state)
            for fo in range(NDC):
                jy, ys, yv = make_y(fo, tt, bw)
                ds4.prep(ys, [("v", yv)], 0, 1, first=(fo == 0), last=(fo == 15))
                y4_state["free"][jy] = ("g", cnt["g"])
            last4 = ds4.flush()
            ln_rows(0, 1, 1, [("t", last4)])
            m3e = op("v", lambda e: e.tensor_tensor(rF[0:1, 0:512], rA[0:1, 0:512],
                                                    rC[0:1, 0:512], op=OP.mult),
                     inc="v")
            _, l1 = row_pair(RMH[0:1, 0:512], RML[0:1, 0:512], rC[0:1, 0:512],
                             [("v", m3e)])
            _, l2 = row_pair(RMH[0:1, 512:1024], RML[0:1, 512:1024], rF[0:1, 0:512],
                             [("v", m3e)])
            bcast512(Z[:, 0:512], RMH[0:1, 0:512], RML[0:1, 0:512], 0, [("v", l1)])
            b4 = bcast512(Z[:, 512:1024], RMH[0:1, 512:1024], RML[0:1, 512:1024], 1,
                          [("v", l2)])
            ysh_ev = ysl_ev = 0
            for fo in range(NDC):
                jy, ys, yv = make_y(fo, tt, bw)
                a1 = op("v", lambda e, ys=ys: e.tensor_tensor(
                    ys, ys, Z[:, 0:512], op=OP.mult),
                    waits=[("v", yv), ("v", b4)], inc="v")
                a2 = op("v", lambda e, ys=ys: e.tensor_tensor(
                    ys, ys, Z[:, 512:1024], op=OP.subtract), waits=[("v", a1)],
                    inc="v")
                ysh_ev = op("v", lambda e, ys=ys, fo=fo: e.tensor_copy(
                    YSH[:, fo * 512:(fo + 1) * 512], ys), waits=[("v", a2)],
                    inc="v")
                ysl_ev = op("g", lambda e, ys=ys, fo=fo: e.tensor_tensor(
                    YSL[:, fo * 512:(fo + 1) * 512], ys,
                    YSH[:, fo * 512:(fo + 1) * 512], op=OP.subtract),
                    waits=[("v", ysh_ev)], inc="g")
                y4_state["free"][jy] = ("g", ysl_ev)

            wf4_free = [("v", 0)] * 4
            wh4_half_free = [("t", 0), ("t", 0), ("t", 0)]
            pti4 = [0]
            da_ev = {}
            hfree = ("t", 0)

            def stream_half4(wsrc, row0, c0, ntiles, half):
                hs = 0
                for i in range(ntiles):
                    slot = i % 4
                    dsem = f"dW{slot}"
                    dl = dma_in(WF4[:, slot * 512:slot * 512 + 256],
                                wsrc[(row0 + i) * 128:(row0 + i + 1) * 128,
                                     c0:c0 + 256], dsem, waits=[wf4_free[slot]],
                                q="y" if slot % 2 == 0 else "g")
                    src = WF4[:, slot * 512:slot * 512 + 256]
                    dsth = WH4[:, half * 4096 + i * 256:half * 4096 + (i + 1) * 256]
                    hw = [wh4_half_free[half]] if i == 0 else []
                    hs = op("s", lambda e, s_=src, d=dsth: e.copy(d, s_),
                            waits=[(dsem, dl)] + hw, inc="s")
                    wf4_free[slot] = ("s", hs)
                return hs

            tasks = []
            for (ic0, nic) in QPART:
                for ib in range(nic // 2):
                    tasks.append(("g", ic0 + ib * 2))
                    tasks.append(("u", ic0 + ib * 2))
                for hob in range(8):
                    tasks.append(("d", ic0, nic, hob))

            def stream_task(k):
                t = tasks[k]
                half = k % 3
                if t[0] == "g":
                    return stream_half4(gate_wt, 0, t[1] * 128, NDC, half)
                if t[0] == "u":
                    return stream_half4(up_wt, 0, t[1] * 128, NDC, half)
                return stream_half4(down_wt, t[1], t[3] * 256, t[2], half)

            sev_t = {0: stream_task(0), 1: stream_task(1)}
            gate_res = None
            h_ev = {}
            hseen = {}
            for k, t in enumerate(tasks):
                if k + 2 < len(tasks):
                    sev_t[k + 2] = stream_task(k + 2)
                hs = sev_t.pop(k)
                half = k % 3
                if t[0] in ("g", "u"):
                    icb = t[1]
                    scl, fn = (GSC, AF.Silu) if t[0] == "g" else (USC, AF.Copy)
                    g_evs = []
                    gev = None
                    for io in range(2):
                        ic = icb + io
                        pti = pti4[0] % 3
                        pti4[0] += 1
                        w = pt_acquire(pti, [("s", hs), ("v", ysh_ev),
                                             ("g", ysl_ev)])
                        gev = None
                        for pi, RB in enumerate((YSH, YSL)):
                            for dc in range(NDC):
                                first = (pi == 0 and dc == 0)
                                last = (pi == 1 and dc == NDC - 1)
                                lhsT = WH4[:, half * 4096 + dc * 256 + io * 128:
                                           half * 4096 + dc * 256 + io * 128 + 128]
                                rhs = RB[:, dc * 512:(dc + 1) * 512]
                                gev = op("t", lambda e, p=PT[pti], l_=lhsT, r=rhs,
                                         f=first, st=last: e.matmul(
                                             p[:], l_, r, start=f, stop=st),
                                         waits=w if first else (),
                                         inc="t" if last else None)
                        jy, ys = y4_slot()
                        yw = y4_state["free"][jy]
                        se = op("s", lambda e, ys=ys, p=PT[pti], scl=scl, ic=ic,
                                fn=fn: e.activation(ys, p[:], fn,
                                                    scale=scl[:, ic:ic + 1]),
                                waits=[("t", gev), yw], inc="s")
                        pt_free[pti] = ("s", se)
                        g_evs.append((jy, ys, se))
                    wh4_half_free[half] = ("t", gev)
                    if t[0] == "g":
                        gate_res = g_evs
                    else:
                        ic0q = icb - (icb % 12) if icb < 36 else 36
                        for q0, qn in QPART:
                            if q0 <= icb < q0 + qn:
                                ic0q = q0
                        for io in range(2):
                            jg, gys, ge = gate_res[io]
                            ju, uys, ue = g_evs[io]
                            pos = icb + io - ic0q
                            m1 = op("v", lambda e, g_=gys, u_=uys:
                                    e.tensor_tensor(g_, g_, u_, op=OP.mult),
                                    waits=[("s", ge), ("s", ue)], inc="v")
                            h_ = op("v", lambda e, g_=gys, pos=pos:
                                    e.tensor_copy(
                                        HH[:, pos * 512:(pos + 1) * 512], g_),
                                    waits=[("v", m1), hfree], inc="v")
                            l_ = op("g", lambda e, g_=gys, pos=pos:
                                    e.tensor_tensor(
                                        HL[:, pos * 512:(pos + 1) * 512], g_,
                                        HH[:, pos * 512:(pos + 1) * 512],
                                        op=OP.subtract),
                                    waits=[("v", h_)], inc="g")
                            y4_state["free"][jg] = ("g", l_)
                            y4_state["free"][ju] = ("v", m1)
                            hseen[ic0q] = hseen.get(ic0q, 0) + 1
                            if hseen[ic0q] == dict(QPART)[ic0q]:
                                h_ev[ic0q] = (cnt["v"], cnt["g"])
                else:
                    _, ic0, nic, hob = t
                    h_hi_ev, h_lo_ev = h_ev[ic0]
                    gevd = None
                    for ho_loc in range(2):
                        ho = hob * 2 + ho_loc
                        pti = pti4[0] % 3
                        pti4[0] += 1
                        w = pt_acquire(pti, [("s", hs), ("v", h_hi_ev),
                                             ("g", h_lo_ev)])
                        gevd = None
                        for pi, RB in enumerate((HH, HL)):
                            for i in range(nic):
                                first = (pi == 0 and i == 0)
                                last = (pi == 1 and i == nic - 1)
                                lhsT = WH4[:, half * 4096 + i * 256 + ho_loc * 128:
                                           half * 4096 + i * 256 + ho_loc * 128
                                           + 128]
                                rhs = RB[:, i * 512:(i + 1) * 512]
                                gevd = op("t", lambda e, p=PT[pti], l_=lhsT, r=rhs,
                                          f=first, st=last: e.matmul(
                                              p[:], l_, r, start=f, stop=st),
                                          waits=w if first else (),
                                          inc="t" if last else None)
                        dsl = DA[:, ho * 512:(ho + 1) * 512]
                        daw = da_ev.get(ho, ("v", 0))
                        if ic0 == 0:
                            de = op("v", lambda e, d=dsl, p=PT[pti]: e.tensor_copy(
                                d, p[:]), waits=[("t", gevd), daw], inc="v")
                        else:
                            de = op("v", lambda e, d=dsl, p=PT[pti]:
                                    e.tensor_tensor(d, d, p[:], op=OP.add),
                                    waits=[("t", gevd), daw], inc="v")
                        if ic0 == QPART[-1][0]:
                            de = op("v", lambda e, d=dsl, ho=ho: e.tensor_scalar(
                                d, d, DSC[:, ho:ho + 1], None, op0=OP.mult),
                                waits=[("v", de)], inc="v")
                        da_ev[ho] = ("v", de)
                        pt_free[pti] = ("v", de)
                    wh4_half_free[half] = ("t", gevd)
                    if hob == 7:
                        hfree = ("t", gevd)

            ds4b = DeferStats(SRH4, SRL4, SQH4, SQL4, y4_slot, y4_state)
            for ho in range(NDC):
                ds4b.prep(DA[:, ho * 512:(ho + 1) * 512], [da_ev[ho]], 0, 1,
                          first=(ho == 0), last=(ho == 15))
            last4 = ds4b.flush()
            ln_rows(0, 1, 1, [("t", last4)])
            m4e = op("v", lambda e: e.tensor_tensor(rF[0:1, 0:512], rA[0:1, 0:512],
                                                    rC[0:1, 0:512], op=OP.mult),
                     inc="v")
            _, l1 = row_pair(RMH[0:1, 0:512], RML[0:1, 0:512], rC[0:1, 0:512],
                             [("v", m4e)])
            _, l2 = row_pair(RMH[0:1, 512:1024], RML[0:1, 512:1024], rF[0:1, 0:512],
                             [("v", m4e)])
            bcast512(Z[:, 0:512], RMH[0:1, 0:512], RML[0:1, 0:512], 0, [("v", l1)])
            b4 = bcast512(Z[:, 512:1024], RMH[0:1, 512:1024], RML[0:1, 512:1024], 1,
                          [("v", l2)])
            for fo in range(NDC):
                jy, ys, yv = make_y(fo, tt, bw)
                jd, dt_ = y4_slot()
                dw = y4_state["free"][jd]
                f1 = op("v", lambda e, d=dt_, fo=fo: e.tensor_tensor(
                    d, DA[:, fo * 512:(fo + 1) * 512], Z[:, 0:512], op=OP.mult),
                    waits=[dw, ("v", b4), da_ev[fo]], inc="v")
                f2 = op("v", lambda e, d=dt_: e.tensor_tensor(
                    d, d, Z[:, 512:1024], op=OP.subtract), waits=[("v", f1)],
                    inc="v")
                f3 = op("v", lambda e, d=dt_, ys=ys: e.tensor_tensor(
                    d, d, ys, op=OP.add), waits=[("v", f2), ("v", yv)], inc="v")
                ysem = f"dY{jd}"
                de = dma_out(out_fm[fo * 128:(fo + 1) * 128,
                                    tt * 512:(tt + 1) * 512], dt_, ysem,
                             waits=[("v", f3)])
                y4_state["free"][jd] = (ysem, de)
                y4_state["free"][jy] = ("v", f3)
            barrier()

        p4s.close()
        fin = [(nm, cnt[nm]) for nm in SEMH if nm.startswith(("dE", "dY"))
               and cnt[nm] > 0]
        def _final(e, fin=tuple(fin)):
            for nm, v in fin:
                e.wait_ge(SEMH[nm], v)
        prog["y"].append(_final)

        block = es.enter_context(nc.Block())
        block.tensor(lambda e: [f(e) for f in prog["t"]])
        block.vector(lambda e: [f(e) for f in prog["v"]])
        block.scalar(lambda e: [f(e) for f in prog["s"]])
        block.gpsimd(lambda e: [f(e) for f in prog["g"]])
        block.sync(lambda e: [f(e) for f in prog["y"]])

    return nc


def kernel(L0_attn_X, L0_attn_ln_w, L0_attn_ln_b, L0_attn_W_in, L0_attn_W_out,
           L0_ffn_gate_w, L0_ffn_up_w, L0_ffn_down_w, L0_ffn_gate_s, L0_ffn_up_s,
           L0_ffn_down_s):
    X = np.asarray(L0_attn_X, dtype=np.float32)
    W_in = np.asarray(L0_attn_W_in, dtype=np.float32)
    W_out = np.ascontiguousarray(np.asarray(L0_attn_W_out, dtype=np.float32))
    gate_w = np.asarray(L0_ffn_gate_w, dtype=np.float32)
    up_w = np.asarray(L0_ffn_up_w, dtype=np.float32)
    down_w = np.asarray(L0_ffn_down_w, dtype=np.float32)

    perm64 = np.concatenate([np.arange(0, 64, 2), np.arange(1, 64, 2),
                             np.arange(64, 128)])
    colp = np.empty(3 * D, dtype=np.int64)
    for h in range(H):
        colp[h * 128:(h + 1) * 128] = h * 128 + perm64
        colp[D + h * 128:D + (h + 1) * 128] = D + h * 128 + perm64
    colp[2 * D:] = np.arange(2 * D, 3 * D)
    w_in_p = np.ascontiguousarray(W_in[:, colp])
    gate_wt = np.ascontiguousarray(gate_w.T)
    up_wt = np.ascontiguousarray(up_w.T)
    down_wt = np.ascontiguousarray(down_w.T)
    lnw = np.ascontiguousarray(np.asarray(L0_attn_ln_w, np.float32).reshape(16, 128).T)
    lnb = np.ascontiguousarray(np.asarray(L0_attn_ln_b, np.float32).reshape(16, 128).T)
    gs = np.ascontiguousarray(np.asarray(L0_ffn_gate_s, np.float32).reshape(44, 128).T)
    us = np.ascontiguousarray(np.asarray(L0_ffn_up_s, np.float32).reshape(44, 128).T)
    ds = np.ascontiguousarray(np.asarray(L0_ffn_down_s, np.float32).reshape(16, 128).T)
    cos32, sin32 = _rope_tables()
    import ml_dtypes
    eye = np.eye(128, dtype=ml_dtypes.bfloat16)
    ones1 = np.ones((1, 128), dtype=ml_dtypes.bfloat16)
    onesc = np.ones((128, 1), dtype=ml_dtypes.bfloat16)

    Xfm = [np.ascontiguousarray(X[b].T) for b in range(4)]

    nc = _build()
    in_maps = []
    for c in range(8):
        b, q0 = c // 2, (c % 2) * 1024
        o0 = 1024 - q0
        cq_ = cos32[:, q0:q0 + 1024]
        sq_ = sin32[:, q0:q0 + 1024]
        ck_ = np.concatenate([cos32[:, q0:q0 + 1024], cos32[:, o0:o0 + 1024]], axis=1)
        sk_ = np.concatenate([sin32[:, q0:q0 + 1024], sin32[:, o0:o0 + 1024]], axis=1)
        in_maps.append({
            "x_own": np.ascontiguousarray(Xfm[b][:, q0:q0 + 1024]),
            "x_oth": np.ascontiguousarray(Xfm[b][:, o0:o0 + 1024]),
            "w_in": w_in_p, "w_out": W_out,
            "gate_wt": gate_wt, "up_wt": up_wt, "down_wt": down_wt,
            "lnw": lnw, "lnb": lnb, "gate_s": gs, "up_s": us, "down_s": ds,
            "cosqd": np.ascontiguousarray(np.concatenate([cq_, cq_], axis=0)),
            "sinqd": np.ascontiguousarray(np.concatenate([sq_, sq_], axis=0)),
            "coskd": np.ascontiguousarray(np.concatenate([ck_, ck_], axis=0)),
            "sinkd": np.ascontiguousarray(np.concatenate([sk_, sk_], axis=0)),
            "eye": eye, "ones1": ones1, "onesc": onesc,
        })
    res = run_bass_kernel_spmd(nc, in_maps, core_ids=list(range(8)))
    out = np.empty((4, 2048, 2048), dtype=np.float32)
    for c in range(8):
        b, q0 = c // 2, (c % 2) * 1024
        out[b, q0:q0 + 1024, :] = np.asarray(res.results[c]["out_fm"]).T
    return out


# version 19 (final)
# speedup vs baseline: 1.6272x; 1.2362x over previous
import numpy as np
from contextlib import ExitStack

import concourse.bass as bass
import concourse.mybir as mybir
from concourse.bass_utils import run_bass_kernel_spmd

F32 = mybir.dt.float32
BF16 = mybir.dt.bfloat16
AF = mybir.ActivationFunctionType
OP = mybir.AluOpType
AX = mybir.AxisListType

D = 2048
SQ = 2048
TOWN = 1024
H = 16
FI = 5632
NDC = 16
EPS = 1e-5
INV_D = 1.0 / D

_S128 = float(np.float32(np.sqrt(np.float32(128.0))))
CQ = float(np.float32(128.0 ** -0.25))
_CK = (1.0 / _S128) / CQ
CK1 = float(np.float32(_CK))
CK2 = float(np.float32(_CK - CK1))

ENGS = ("t", "v", "s", "g", "y")


def _rope_tables():
    import jax
    import jax.numpy as jnp
    rot = 64
    with jax.default_device(jax.devices("cpu")[0]):
        inv_freq = 1.0 / 10000.0 ** (jnp.arange(0, rot, 2, dtype=jnp.float32) / rot)
        ang = jnp.arange(SQ, dtype=jnp.float32)[:, None] * inv_freq[None, :]
        cos = np.ascontiguousarray(np.asarray(jnp.cos(ang)).T).astype(np.float32)
        sin = np.ascontiguousarray(np.asarray(jnp.sin(ang)).T).astype(np.float32)
    return cos, sin


def _build():
    nc = bass.Bass("TRN2", target_bir_lowering=False)

    def din(name, shape, dt=F32):
        return nc.dram_tensor(name, shape, dt, kind="ExternalInput").ap()

    x_own = din("x_own", [D, TOWN])
    x_oth = din("x_oth", [D, TOWN])
    w_in = din("w_in", [D, 3 * D])
    w_out = din("w_out", [D, D])
    gate_wt = din("gate_wt", [D, FI])
    up_wt = din("up_wt", [D, FI])
    down_wt = din("down_wt", [FI, D])
    lnw_i = din("lnw", [128, 16])
    lnb_i = din("lnb", [128, 16])
    gs_i = din("gate_s", [128, 44])
    us_i = din("up_s", [128, 44])
    ds_i = din("down_s", [128, 16])
    cosqd_i = din("cosqd", [64, TOWN])
    sinqd_i = din("sinqd", [64, TOWN])
    coskd_i = din("coskd", [64, SQ])
    sinkd_i = din("sinkd", [64, SQ])
    eye_i = din("eye", [128, 128], BF16)
    ones_i = din("ones1", [1, 128], BF16)
    onesc_i = din("onesc", [128, 1], BF16)

    out_fm = nc.dram_tensor("out_fm", [D, TOWN], F32, kind="ExternalOutput").ap()

    q_hi_s = nc.dram_tensor("q_hi_s", [H, 128, TOWN], BF16).ap()
    q_lo_s = nc.dram_tensor("q_lo_s", [H, 128, TOWN], BF16).ap()
    k_hi_s = nc.dram_tensor("k_hi_s", [H, 128, SQ], BF16).ap()
    k_lo_s = nc.dram_tensor("k_lo_s", [H, 128, SQ], BF16).ap()
    v_hi_s = nc.dram_tensor("v_hi_s", [H, 128, SQ], BF16).ap()
    v_lo_s = nc.dram_tensor("v_lo_s", [H, 128, SQ], BF16).ap()
    ao_s = nc.dram_tensor("ao_s", [D, TOWN], F32).ap()

    es = ExitStack()
    with es:
        SEMH = {}
        semnames = (["t", "v", "s", "g", "dC", "dH0", "dH1"]
                    + [f"dW{i}" for i in range(4)] + [f"dX{i}" for i in range(4)]
                    + [f"dE{i}" for i in range(4)] + [f"dY{i}" for i in range(6)])
        for nm in semnames:
            SEMH[nm] = es.enter_context(nc.semaphore("sem_" + nm))
        cnt = {k: 0 for k in SEMH}
        prog = {e: [] for e in ENGS}

        def op(e, fn, waits=(), inc=None, inck=1):
            waits = list(waits)
            if e in ("v", "s", "g"):
                waits.append((e, cnt[e]))
            w = tuple((s_, v_) for (s_, v_) in waits if v_ > 0)
            val = None
            if inc is not None:
                cnt[inc] += inck
                val = cnt[inc]

            def run(eng, fn=fn, w=w, inc=inc, inck=inck):
                for sname, v in w:
                    eng.wait_ge(SEMH[sname], v)
                inst = fn(eng)
                if inc is not None:
                    inst.then_inc(SEMH[inc], inck)
            prog[e].append(run)
            return val

        last_barrier = [()]

        def barrier():
            vals = {k: cnt[k] for k in SEMH}
            for e in ("t", "v", "s", "g"):
                w = [(o, v) for o, v in vals.items() if o != e and v > 0]
                op(e, lambda eng: eng.nop(), waits=w, inc=e)
            def run(eng, vals=dict(vals)):
                for o, v in vals.items():
                    if v > 0:
                        eng.wait_ge(SEMH[o], v)
            prog["y"].append(run)
            last_barrier[0] = tuple((o, v) for o, v in vals.items() if v > 0)

        def dma_in(dst, src, sem, waits=(), q="y"):
            waits = list(waits) + list(last_barrier[0])
            return op(q, lambda e, d=dst, s_=src: e.dma_start(out=d, in_=s_),
                      waits=waits, inc=sem, inck=16)

        def dma_out(dst, src, sem, waits=()):
            waits = list(waits) + list(last_barrier[0])
            return op("g", lambda e, d=dst, s_=src: e.dma_start(out=d, in_=s_),
                      waits=waits, inc=sem, inck=16)

        sb = lambda n, sh, dt: es.enter_context(nc.sbuf_tensor(n, sh, dt))
        pt_ = lambda n, sh, dt: es.enter_context(nc.psum_tensor(n, sh, dt))

        LNW = sb("LNW", [128, 16], F32)
        LNB = sb("LNB", [128, 16], F32)
        GSC = sb("GSC", [128, 44], F32)
        USC = sb("USC", [128, 44], F32)
        DSC = sb("DSC", [128, 16], F32)
        EYE = sb("EYE", [128, 128], BF16)
        ONES = sb("ONES", [1, 128], BF16)
        ONESC = sb("ONESC", [128, 1], BF16)
        ROW = [sb(f"ROW{i}", [1, 1024], F32) for i in range(8)]
        rA, rB, rC, rD, rE, rF, rG, rH = ROW
        RMH = sb("RMH", [1, 2048], BF16)
        RML = sb("RML", [1, 2048], BF16)
        RN = sb("RN", [1, 32], F32)
        NEGB = sb("NEGB", [128, 2], F32)
        Z = sb("Z", [128, 2048], F32)
        SRH2 = sb("SRH2", [128, 512], BF16)

        PT = [pt_(f"PT{i}", [128, 512], F32) for i in range(3)]
        PTB = pt_("PTB", [128, 1024], BF16)
        PR = [pt_(f"PR{i}", [1, 512], F32) for i in range(4)]
        pt_free = [("t", 0)] * 3
        pr_free = [("t", 0)] * 4

        def pt_acquire(i, waits):
            return list(waits) + [pt_free[i]]

        for dst, src in ((LNW, lnw_i), (LNB, lnb_i), (GSC, gs_i), (USC, us_i),
                         (DSC, ds_i), (EYE, eye_i), (ONES, ones_i), (ONESC, onesc_i)):
            dma_in(dst[:], src[:], "dC")
        ld_const = cnt["dC"]

        def row_sqrt(dst, src, waits):
            return op("s", lambda e, d=dst, s_=src: e.activation(d, s_, AF.Sqrt),
                      waits=waits, inc="s")

        def bcast512(dst, rmh_sl, rml_sl, pti, waits):
            w = pt_acquire(pti, waits)
            op("t", lambda e, p=PT[pti], r=rmh_sl: e.matmul(p[:, 0:512], ONES[:], r,
                                                            start=True, stop=False),
               waits=w)
            tev = op("t", lambda e, p=PT[pti], r=rml_sl: e.matmul(p[:, 0:512], ONES[:],
                                                                  r, start=False,
                                                                  stop=True), inc="t")
            vev = op("v", lambda e, d=dst, p=PT[pti]: e.tensor_copy(d, p[:, 0:512]),
                     waits=[("t", tev)], inc="v")
            pt_free[pti] = ("v", vev)
            return vev

        def row_pair(rmh_sl, rml_sl, src, waits):
            h = op("v", lambda e, d=rmh_sl, s_=src: e.tensor_copy(d, s_),
                   waits=waits, inc="v")
            l = op("v", lambda e, d=rml_sl, s_=src, hh=rmh_sl:
                   e.tensor_tensor(d, s_, hh, op=OP.subtract), waits=[("v", h)],
                   inc="v")
            return h, l

        def ln_rows(pr_sum, pr_sq, n_tt, waits, s_dst=None, ms_dst=None):
            evs = []
            for tt in range(n_tt):
                a = op("v", lambda e, tt=tt, p=PR[pr_sum + 2 * tt]:
                       e.tensor_scalar(rA[0:1, tt * 512:(tt + 1) * 512], p[:], INV_D,
                                       None, op0=OP.mult), waits=waits, inc="v")
                b_ = op("v", lambda e, tt=tt, p=PR[pr_sq + 2 * tt]:
                        e.tensor_scalar(rB[0:1, tt * 512:(tt + 1) * 512], p[:], INV_D,
                                        None, op0=OP.mult), inc="v")
                pr_free[pr_sum + 2 * tt] = ("v", a)
                pr_free[pr_sq + 2 * tt] = ("v", b_)
                evs += [a, b_]
            n = n_tt * 512
            c1 = op("v", lambda e, n=n: e.tensor_tensor(rC[0:1, 0:n], rA[0:1, 0:n],
                                                        rA[0:1, 0:n], op=OP.mult),
                    waits=[("v", max(evs))], inc="v")
            c2 = op("v", lambda e, n=n: e.tensor_tensor(rB[0:1, 0:n], rB[0:1, 0:n],
                                                        rC[0:1, 0:n], op=OP.subtract),
                    waits=[("v", c1)], inc="v")
            c3 = op("v", lambda e, n=n: e.tensor_scalar(rC[0:1, 0:n], rB[0:1, 0:n],
                                                        EPS, None, op0=OP.add),
                    waits=[("v", c2)], inc="v")
            c4 = row_sqrt(rD[0:1, 0:n], rC[0:1, 0:n], [("v", c3)])
            c5 = op("v", lambda e, n=n: e.reciprocal(rC[0:1, 0:n], rD[0:1, 0:n]),
                    waits=[("s", c4)], inc="v")
            out = [c5]
            if s_dst is not None:
                out.append(op("v", lambda e, n=n, d=s_dst: e.tensor_copy(
                    d[0:1, 0:n], rC[0:1, 0:n]), waits=[("v", c5)], inc="v"))
            if ms_dst is not None:
                out.append(op("v", lambda e, n=n, d=ms_dst: e.tensor_tensor(
                    d[0:1, 0:n], rA[0:1, 0:n], rC[0:1, 0:n], op=OP.mult),
                    waits=[("v", c5)], inc="v"))
            return max(out)

        p1s = ExitStack()
        CQD = p1s.enter_context(nc.sbuf_tensor("CQD", [64, TOWN], F32))
        SQD = p1s.enter_context(nc.sbuf_tensor("SQD", [64, TOWN], F32))
        CKD = p1s.enter_context(nc.sbuf_tensor("CKD", [64, SQ], F32))
        SKD = p1s.enter_context(nc.sbuf_tensor("SKD", [64, SQ], F32))
        XSH = p1s.enter_context(nc.sbuf_tensor("XSH", [128, 16 * 1024], BF16))
        XSL = p1s.enter_context(nc.sbuf_tensor("XSL", [128, 16 * 1024], BF16))
        WF = p1s.enter_context(nc.sbuf_tensor("WF", [128, 4 * 512], F32))
        WH = p1s.enter_context(nc.sbuf_tensor("WH", [128, 2 * 16 * 256], BF16))
        WL = p1s.enter_context(nc.sbuf_tensor("WL", [128, 2 * 16 * 256], BF16))
        XT = p1s.enter_context(nc.sbuf_tensor("XT", [128, 2048], F32))
        YB = p1s.enter_context(nc.sbuf_tensor("YB", [128, 5 * 512], F32))
        EH = p1s.enter_context(nc.sbuf_tensor("EH", [128, 3 * 512], BF16))
        EL = p1s.enter_context(nc.sbuf_tensor("EL", [128, 3 * 512], BF16))

        dma_in(CQD[:], cosqd_i[:], "dC")
        dma_in(SQD[:], sinqd_i[:], "dC")
        dma_in(CKD[:], coskd_i[:], "dC")
        e4 = dma_in(SKD[:], sinkd_i[:], "dC")
        for TBL in (CKD, SKD):
            a = op("v", lambda e, t=TBL: e.tensor_scalar(Z[0:64, :], t[:], CK2, None,
                                                         op0=OP.mult),
                   waits=[("dC", e4)], inc="v")
            b_ = op("v", lambda e, t=TBL: e.tensor_scalar(t[:], t[:], CK1, None,
                                                          op0=OP.mult),
                    waits=[("v", a)], inc="v")
            op("v", lambda e, t=TBL: e.tensor_tensor(t[:], t[:], Z[0:64, :],
                                                     op=OP.add),
               waits=[("v", b_)], inc="v")
        tbl_ev = cnt["v"]

        yb_state = {"i": 0, "free": [("v", 0)] * 5}

        def yb_slot():
            j = yb_state["i"] % 5
            yb_state["i"] += 1
            return j, YB[:, j * 512:(j + 1) * 512]

        eh_state = {"i": 0, "free": [("dE0", 0)] * 3}

        def eh_slot():
            j = eh_state["i"] % 3
            eh_state["i"] += 1
            return j, EH[:, j * 512:(j + 1) * 512], EL[:, j * 512:(j + 1) * 512]

        def stats_dc(xv, src_waits, dc, tev_hist):
            j = dc % 2
            srh = XSH[:, j * 2048:j * 2048 + 1024]
            sqh = XSH[:, j * 2048 + 1024:(j + 1) * 2048]
            srl = XSL[:, j * 2048:j * 2048 + 1024]
            sql = XSL[:, j * 2048 + 1024:(j + 1) * 2048]
            zq = Z[:, j * 1024:(j + 1) * 1024]
            guard = [("t", tev_hist.get(dc - 2, 0))]
            hh = op("v", lambda e, s_=xv, d=srh: e.tensor_copy(d, s_),
                    waits=list(src_waits) + guard, inc="v")
            ll = op("g", lambda e, s_=xv, d=srl, dh=srh: e.tensor_tensor(
                d, s_, dh, op=OP.subtract), waits=[("v", hh)], inc="g")
            sq = op("v", lambda e, s_=xv, z=zq: e.tensor_tensor(z, s_, s_,
                                                                op=OP.mult),
                    waits=[("g", ll)], inc="v")
            qh = op("v", lambda e, z=zq, d=sqh: e.tensor_copy(d, z),
                    waits=[("v", sq)], inc="v")
            ql = op("g", lambda e, z=zq, d=sql, dh=sqh: e.tensor_tensor(
                d, z, dh, op=OP.subtract), waits=[("v", qh)], inc="g")
            first, last = (dc == 0), (dc == NDC - 1)
            tev = None
            for tt in range(2):
                sl = slice(tt * 512, (tt + 1) * 512)
                w0 = [pr_free[2 * tt]] if first else []
                w1 = [pr_free[2 * tt + 1]] if first else []
                op("t", lambda e, p=PR[2 * tt], f=first, d=srh, sl=sl: e.matmul(
                    p[:], ONESC[:], d[:, sl], start=f, stop=False),
                   waits=[("v", hh)] + w0)
                op("t", lambda e, p=PR[2 * tt], l_=last, d=srl, sl=sl: e.matmul(
                    p[:], ONESC[:], d[:, sl], start=False, stop=l_),
                   waits=[("g", ll)])
                op("t", lambda e, p=PR[2 * tt + 1], f=first, d=sqh, sl=sl: e.matmul(
                    p[:], ONESC[:], d[:, sl], start=f, stop=False),
                   waits=[("v", qh)] + w1)
                tev = op("t", lambda e, p=PR[2 * tt + 1], l_=last, d=sql, sl=sl:
                         e.matmul(p[:], ONESC[:], d[:, sl], start=False, stop=l_),
                         waits=[("g", ql)], inc="t")
            tev_hist[dc] = tev
            return tev

        xt_free = [("t", 0), ("t", 0)]
        for slab in range(2):
            xsrc = x_own if slab == 0 else x_oth
            tev_hist = {}
            last_t = 0
            for dc in range(NDC):
                sl_ = dc % 2
                w = [xt_free[sl_]] if (dc or slab) else [("v", tbl_ev),
                                                         ("dC", cnt["dC"])]
                dl = dma_in(XT[:, sl_ * 1024:(sl_ + 1) * 1024],
                            xsrc[dc * 128:(dc + 1) * 128, :], f"dX{sl_}", waits=w)
                last_t = stats_dc(XT[:, sl_ * 1024:(sl_ + 1) * 1024],
                                  [(f"dX{sl_}", dl)], dc, tev_hist)
                xt_free[sl_] = ("t", last_t)
            sev = ln_rows(0, 1, 2, [("t", last_t)],
                          s_dst=(rG if slab == 0 else None))
            if slab == 0:
                sev = op("v", lambda e: e.tensor_tensor(rH[0:1, :], rA[0:1, :],
                                                        rC[0:1, :], op=OP.mult),
                         waits=[("v", sev)], inc="v")
            d1 = op("v", lambda e: e.tensor_tensor(rD[0:1, :], rC[0:1, :], rC[0:1, :],
                                                   op=OP.mult), waits=[("v", sev)],
                    inc="v")
            d2 = op("v", lambda e: e.tensor_tensor(rD[0:1, :], rD[0:1, :], rB[0:1, :],
                                                   op=OP.mult), waits=[("v", d1)],
                    inc="v")
            d3 = op("v", lambda e: e.tensor_scalar(rD[0:1, :], rD[0:1, :], EPS, None,
                                                   op0=OP.add), waits=[("v", d2)],
                    inc="v")
            d4 = row_sqrt(rE[0:1, :], rD[0:1, :], [("v", d3)])
            d5 = op("v", lambda e: e.reciprocal(rD[0:1, :], rE[0:1, :]),
                    waits=[("s", d4)], inc="v")
            d6 = op("v", lambda e: e.tensor_tensor(rE[0:1, :], rC[0:1, :], rD[0:1, :],
                                                   op=OP.mult), waits=[("v", d5)],
                    inc="v")
            d7 = op("v", lambda e: e.tensor_tensor(rF[0:1, :], rA[0:1, :], rE[0:1, :],
                                                   op=OP.mult), waits=[("v", d6)],
                    inc="v")
            _, l1 = row_pair(RMH[0:1, 0:1024], RML[0:1, 0:1024], rE[0:1, :],
                             [("v", d7)])
            _, l2 = row_pair(RMH[0:1, 1024:2048], RML[0:1, 1024:2048], rF[0:1, :],
                             [("v", d7)])
            bevs = []
            for tt in range(2):
                bevs.append(bcast512(Z[:, tt * 512:(tt + 1) * 512],
                                     RMH[0:1, tt * 512:(tt + 1) * 512],
                                     RML[0:1, tt * 512:(tt + 1) * 512], 0,
                                     [("v", l1)]))
                bevs.append(bcast512(Z[:, 1024 + tt * 512:1024 + (tt + 1) * 512],
                                     RMH[0:1, 1024 + tt * 512:1024 + (tt + 1) * 512],
                                     RML[0:1, 1024 + tt * 512:1024 + (tt + 1) * 512],
                                     1, [("v", l2)]))
            bev = max(bevs)

            xt2_free = [("v", 0), ("v", 0)]
            prev_h, prev_l = 0, 0
            for dc in range(NDC):
                sl_ = dc % 2
                xv = XT[:, sl_ * 1024:(sl_ + 1) * 1024]
                w = [xt2_free[sl_]] + ([("t", last_t)] if dc < 2 else [])
                dl = dma_in(xv, xsrc[dc * 128:(dc + 1) * 128, :], f"dX{sl_}",
                            waits=w)
                t1 = op("v", lambda e, xv=xv, dl=dl, sl_=sl_: e.tensor_tensor(
                    xv, xv, Z[:, 0:1024], op=OP.mult),
                    waits=[(f"dX{sl_}", dl), ("v", bev)], inc="v")
                t2 = op("v", lambda e, xv=xv: e.tensor_tensor(
                    xv, xv, Z[:, 1024:2048], op=OP.subtract),
                    waits=[("v", t1)], inc="v")
                t3 = op("v", lambda e, xv=xv, dc=dc: e.tensor_scalar(
                    xv, xv, LNW[:, dc:dc + 1], LNB[:, dc:dc + 1],
                    op0=OP.mult, op1=OP.add), waits=[("v", t2)], inc="v")
                prev_h = op("v", lambda e, xv=xv, dc=dc: e.tensor_copy(
                    XSH[:, dc * 1024:(dc + 1) * 1024], xv), waits=[("v", t3)],
                    inc="v")
                prev_l = op("g", lambda e, xv=xv, dc=dc: e.tensor_tensor(
                    XSL[:, dc * 1024:(dc + 1) * 1024], xv,
                    XSH[:, dc * 1024:(dc + 1) * 1024], op=OP.subtract),
                    waits=[("v", prev_h)], inc="g")
                xt2_free[sl_] = ("g", prev_l)
            xs_hi_ev, xs_lo_ev = prev_h, prev_l

            wh_half_free = [("t", 0), ("t", 0)]
            wf_free = [("v", 0)] * 4

            def stream_half(csrc0, half, qscale):
                hs = ls = 0
                for i in range(NDC // 2):
                    slot = i % 4
                    dsem = f"dW{slot}"
                    ssrc = w_in[i * 256:(i + 1) * 256, csrc0:csrc0 + 256]
                    src3 = ssrc.rearrange("(t p) c -> p t c", p=128)
                    dst3 = WF[:, slot * 512:(slot + 1) * 512].rearrange(
                        "p (t c) -> p t c", t=2)
                    dl = dma_in(dst3, src3, dsem, waits=[wf_free[slot]])
                    src = WF[:, slot * 512:(slot + 1) * 512]
                    dsth = WH[:, half * 4096 + i * 512:half * 4096 + (i + 1) * 512]
                    dstl = WL[:, half * 4096 + i * 512:half * 4096 + (i + 1) * 512]
                    hw = [wh_half_free[half]] if i == 0 else []
                    if qscale:
                        sc = op("v", lambda e, s_=src: e.tensor_scalar(
                            s_, s_, CQ, None, op0=OP.mult), waits=[(dsem, dl)],
                            inc="v")
                        hs = op("s", lambda e, s_=src, d=dsth: e.copy(d, s_),
                                waits=[("v", sc)] + hw, inc="s")
                    else:
                        hs = op("s", lambda e, s_=src, d=dsth: e.copy(d, s_),
                                waits=[(dsem, dl)] + hw, inc="s")
                    ls = op("v", lambda e, s_=src, d=dstl, dh=dsth: e.tensor_tensor(
                        d, s_, dh, op=OP.subtract), waits=[("s", hs)], inc="v")
                    wf_free[slot] = ("v", ls)
                return hs, ls

            def rope_evac(pti, fo, tt, base_t, is_q, gev):
                ps = PT[pti]
                j, o32 = yb_slot()
                jw = yb_state["free"][j]
                j2, tmp = yb_slot()
                j2w = yb_state["free"][j2]
                if is_q:
                    CT, ST = CQD, SQD
                    tsl = slice(tt * 512, (tt + 1) * 512)
                else:
                    CT, ST = CKD, SKD
                    tsl = slice(base_t + tt * 512, base_t + (tt + 1) * 512)
                w0 = [("t", gev), jw, j2w]
                op("v", lambda e, o=o32, p=ps, c=CT, ts=tsl: e.tensor_tensor(
                    o[0:32, :], p[0:32, :], c[0:32, ts], op=OP.mult), waits=w0,
                    inc="v")
                a2 = op("v", lambda e, o=tmp, p=ps, s_=ST, ts=tsl: e.tensor_tensor(
                    o[0:32, :], p[32:64, :], s_[0:32, ts], op=OP.mult), inc="v")
                a3 = op("v", lambda e, o=o32, t_=tmp: e.tensor_tensor(
                    o[0:32, :], o[0:32, :], t_[0:32, :], op=OP.subtract),
                    waits=[("v", a2)], inc="v")
                a4 = op("v", lambda e, o=o32, p=ps, c=CT, ts=tsl: e.tensor_tensor(
                    o[32:64, :], p[32:64, :], c[32:64, ts], op=OP.mult), inc="v")
                a5 = op("v", lambda e, o=tmp, p=ps, s_=ST, ts=tsl: e.tensor_tensor(
                    o[32:64, :], p[0:32, :], s_[32:64, ts], op=OP.mult), inc="v")
                a6 = op("v", lambda e, o=o32, t_=tmp: e.tensor_tensor(
                    o[32:64, :], o[32:64, :], t_[32:64, :], op=OP.add),
                    waits=[("v", a5)], inc="v")
                if is_q:
                    pv = op("s", lambda e, o=o32, p=ps: e.copy(o[64:128, :],
                                                               p[64:128, :]),
                            waits=w0, inc="s")
                    pass_ev = [("s", pv)]
                else:
                    b1 = op("s", lambda e, o=o32, p=ps: e.activation(
                        o[64:128, :], p[64:128, :], AF.Copy, scale=CK1),
                        waits=w0, inc="s")
                    b2 = op("s", lambda e, o=tmp, p=ps: e.activation(
                        o[64:128, :], p[64:128, :], AF.Copy, scale=CK2), inc="s")
                    b3 = op("v", lambda e, o=o32, t_=tmp: e.tensor_tensor(
                        o[64:128, :], o[64:128, :], t_[64:128, :], op=OP.add),
                        waits=[("s", b2), ("s", b1)], inc="v")
                    pass_ev = [("v", b3)]
                je, ehs, els = eh_slot()
                ew = eh_state["free"][je]
                hh = op("v", lambda e, d=ehs, o=o32: e.tensor_copy(d, o[:]),
                        waits=[("v", a3), ("v", a6)] + pass_ev + [ew], inc="v")
                ll = op("g", lambda e, d=els, o=o32, dh=ehs: e.tensor_tensor(
                    d, o[:], dh, op=OP.subtract), waits=[("v", hh)], inc="g")
                yb_state["free"][j] = ("g", ll)
                yb_state["free"][j2] = ("g", ll)
                if is_q:
                    dsth = q_hi_s[fo][:, tt * 512:(tt + 1) * 512]
                    dstl = q_lo_s[fo][:, tt * 512:(tt + 1) * 512]
                else:
                    dsth = k_hi_s[fo][:, base_t + tt * 512:base_t + (tt + 1) * 512]
                    dstl = k_lo_s[fo][:, base_t + tt * 512:base_t + (tt + 1) * 512]
                esem = f"dE{je}"
                dma_out(dsth, ehs, esem, waits=[("v", hh)])
                d2_ = dma_out(dstl, els, esem, waits=[("g", ll)])
                eh_state["free"][je] = (esem, d2_)
                return ("v", hh)

            def v_evac(pti, fo, tt, base_t, gev):
                ps = PT[pti]
                je, ehs, els = eh_slot()
                ew = eh_state["free"][je]
                hh = op("v", lambda e, d=ehs, p=ps: e.tensor_copy(d, p[:]),
                        waits=[("t", gev), ew], inc="v")
                ll = op("v", lambda e, d=els, p=ps, dh=ehs: e.tensor_tensor(
                    d, p[:], dh, op=OP.subtract), waits=[("v", hh)], inc="v")
                esem = f"dE{je}"
                dma_out(v_hi_s[fo][:, base_t + tt * 512:base_t + (tt + 1) * 512],
                        ehs, esem, waits=[("v", hh)])
                d2_ = dma_out(v_lo_s[fo][:, base_t + tt * 512:base_t + (tt + 1) * 512],
                              els, esem, waits=[("v", ll)])
                eh_state["free"][je] = (esem, d2_)
                return ("v", ll)

            pti_i = [0]

            def projection(kind):
                col0 = {"q": 0, "k": D, "v": 2 * D}[kind]
                base_t = slab * 1024
                ev = {0: stream_half(col0, 0, kind == "q")}
                for hb in range(8):
                    half = hb % 2
                    if hb + 1 < 8:
                        ev[hb + 1] = stream_half(col0 + (hb + 1) * 256,
                                                 (hb + 1) % 2, kind == "q")
                    hs, ls = ev[hb]
                    last_g = 0
                    for fo_loc in range(2):
                        fo = hb * 2 + fo_loc
                        for tt in range(2):
                            pti = pti_i[0] % 3
                            pti_i[0] += 1
                            w = pt_acquire(pti, [("s", hs), ("v", ls),
                                                 ("v", xs_hi_ev), ("g", xs_lo_ev)])
                            passes = [(WH, XSH), (WH, XSL), (WL, XSH)]
                            gev = None
                            for pi, (LB, RB) in enumerate(passes):
                                for dc in range(NDC):
                                    first = (pi == 0 and dc == 0)
                                    last = (pi == 2 and dc == NDC - 1)
                                    lhsT = LB[:, half * 4096 + dc * 256 + fo_loc * 128:
                                              half * 4096 + dc * 256 + fo_loc * 128
                                              + 128]
                                    rhs = RB[:, dc * 1024 + tt * 512:
                                             dc * 1024 + (tt + 1) * 512]
                                    gev = op("t", lambda e, p=PT[pti], l_=lhsT, r=rhs,
                                             f=first, st=last: e.matmul(
                                                 p[:], l_, r, start=f, stop=st),
                                             waits=w if first else (),
                                             inc="t" if last else None)
                            last_g = gev
                            if kind == "v":
                                pt_free[pti] = v_evac(pti, fo, tt, base_t, gev)
                            else:
                                pt_free[pti] = rope_evac(pti, fo, tt, base_t,
                                                         kind == "q", gev)
                    wh_half_free[half] = ("t", last_g)

            if slab == 0:
                projection("q")
            projection("k")
            projection("v")
            barrier()

        p1s.close()
        barrier()

        p23 = ExitStack()
        CTXH = p23.enter_context(nc.sbuf_tensor("CTXH", [128, 16 * 1024], BF16))
        CTXL = p23.enter_context(nc.sbuf_tensor("CTXL", [128, 16 * 1024], BF16))
        p2s = ExitStack()
        PH = p2s.enter_context(nc.sbuf_tensor("PH", [128, 16 * 512], BF16))
        PL = p2s.enter_context(nc.sbuf_tensor("PL", [128, 16 * 512], BF16))
        KH = [p2s.enter_context(nc.sbuf_tensor(f"KH{i}", [128, SQ], BF16))
              for i in range(2)]
        KL = [p2s.enter_context(nc.sbuf_tensor(f"KL{i}", [128, SQ], BF16))
              for i in range(2)]
        VFH = p2s.enter_context(nc.sbuf_tensor("VFH", [128, SQ], BF16))
        VFL = p2s.enter_context(nc.sbuf_tensor("VFL", [128, SQ], BF16))
        QHB = [p2s.enter_context(nc.sbuf_tensor(f"QHB{i}", [128, TOWN], BF16))
               for i in range(2)]
        QLB = [p2s.enter_context(nc.sbuf_tensor(f"QLB{i}", [128, TOWN], BF16))
               for i in range(2)]
        VTH = [p2s.enter_context(nc.sbuf_tensor(f"VTH{i}", [128, SQ], BF16))
               for i in range(2)]
        VTL = [p2s.enter_context(nc.sbuf_tensor(f"VTL{i}", [128, SQ], BF16))
               for i in range(2)]
        Y2 = p2s.enter_context(nc.sbuf_tensor("Y2", [128, 4 * 512], F32))

        y2_state = {"i": 0, "free": [("v", 0)] * 4}

        def y2_slot():
            j = y2_state["i"] % 4
            y2_state["i"] += 1
            return j, Y2[:, j * 512:(j + 1) * 512]

        head_ld = [0, 0]
        head_cmp = [("t", 0), ("t", 0)]

        def load_head(h, vf_guard):
            p = h % 2
            hsem = f"dH{p}"
            dma_in(KH[p][:], k_hi_s[h][:, :], hsem, waits=[head_cmp[p]])
            dma_in(KL[p][:], k_lo_s[h][:, :], hsem)
            dma_in(VFH[:], v_hi_s[h][:, :], hsem, waits=[vf_guard])
            dma_in(VFL[:], v_lo_s[h][:, :], hsem)
            dma_in(QHB[p][:], q_hi_s[h][:, :], hsem)
            head_ld[p] = dma_in(QLB[p][:], q_lo_s[h][:, :], hsem)

        load_head(0, ("v", 0))
        ptb_free = ("v", 0)
        vt_ev = {}
        negb_ev = {}

        def prelude(h):
            nonlocal ptb_free
            par = h % 2
            ldw = [(f"dH{par}", head_ld[par])]
            for src_, dst in ((VFH, VTH[par]), (VFL, VTL[par])):
                for rnd in range(2):
                    tev = None
                    for tc8 in range(8):
                        tc = rnd * 8 + tc8
                        tev = op("t", lambda e, s_=src_, tc=tc, tc8=tc8: e.transpose(
                            PTB[:, tc8 * 128:(tc8 + 1) * 128],
                            s_[:, tc * 128:(tc + 1) * 128], EYE[:]),
                            waits=(ldw + [ptb_free]) if tc8 == 0 else (),
                            inc="t" if tc8 == 7 else None)
                    vv = op("v", lambda e, d=dst, rnd=rnd: e.tensor_copy(
                        d[:, rnd * 1024:(rnd + 1) * 1024], PTB[:]),
                        waits=[("t", tev)], inc="v")
                    ptb_free = ("v", vv)
            vt_ev[h] = cnt["v"]
            o = par * 12
            prev_m3 = 0
            nrm_evs = []
            for i, (src_, n) in enumerate(((KH[par], 4), (QHB[par], 2))):
                for j in range(n):
                    jj, ys = y2_slot()
                    yw = y2_state["free"][jj]
                    m1 = op("v", lambda e, s_=src_, j=j, ys=ys: e.tensor_tensor(
                        ys, s_[:, j * 512:(j + 1) * 512],
                        s_[:, j * 512:(j + 1) * 512], op=OP.mult),
                        waits=[yw] + ldw, inc="v")
                    m2 = op("v", lambda e, ys=ys: e.tensor_copy(SRH2[:], ys),
                            waits=[("v", m1), ("t", prev_m3)], inc="v")
                    y2_state["free"][jj] = ("v", m2)
                    m3 = op("t", lambda e: e.matmul(PR[3][:], ONESC[:], SRH2[:],
                                                    start=True, stop=True),
                            waits=[("v", m2), pr_free[3]], inc="t")
                    prev_m3 = m3
                    idx = o + i * 4 + j
                    m4 = op("v", lambda e, idx=idx: e.reduce_max(
                        RN[0:1, idx:idx + 1], PR[3][:], axis=AX.X),
                        waits=[("t", m3)], inc="v")
                    pr_free[3] = ("v", m4)
                    nrm_evs.append(m4)
            c1 = op("v", lambda e, o=o: e.tensor_max(RN[0:1, o:o + 1],
                                                     RN[0:1, o:o + 1],
                                                     RN[0:1, o + 1:o + 2]),
                    waits=[("v", max(nrm_evs))], inc="v")
            op("v", lambda e, o=o: e.tensor_max(RN[0:1, o + 2:o + 3],
                                                RN[0:1, o + 2:o + 3],
                                                RN[0:1, o + 3:o + 4]), inc="v")
            c3 = op("v", lambda e, o=o: e.tensor_max(RN[0:1, o:o + 1],
                                                     RN[0:1, o:o + 1],
                                                     RN[0:1, o + 2:o + 3]), inc="v")
            c4 = op("v", lambda e, o=o: e.tensor_max(RN[0:1, o + 4:o + 5],
                                                     RN[0:1, o + 4:o + 5],
                                                     RN[0:1, o + 5:o + 6]), inc="v")
            c5 = op("v", lambda e, o=o: e.tensor_tensor(
                RN[0:1, o + 8:o + 9], RN[0:1, o:o + 1], RN[0:1, o + 4:o + 5],
                op=OP.add), waits=[("v", c3), ("v", c4)], inc="v")
            c6 = op("v", lambda e, o=o: e.tensor_scalar(
                RN[0:1, o + 9:o + 10], RN[0:1, o + 8:o + 9], -0.51, -0.5,
                op0=OP.mult, op1=OP.add), waits=[("v", c5)], inc="v")
            c7 = op("v", lambda e, o=o, par=par: e.tensor_copy(
                RMH[0:1, 1024 + par:1025 + par], RN[0:1, o + 9:o + 10]),
                waits=[("v", c6)], inc="v")
            c8 = op("t", lambda e, par=par: e.matmul(
                PT[0][:, 0:1], ONES[:], RMH[0:1, 1024 + par:1025 + par],
                start=True, stop=True), waits=pt_acquire(0, [("v", c7)]), inc="t")
            c9 = op("v", lambda e, par=par: e.tensor_copy(NEGB[:, par:par + 1],
                                                          PT[0][:, 0:1]),
                    waits=[("t", c8)], inc="v")
            pt_free[0] = ("v", c9)
            negb_ev[h] = c9

        prelude(0)
        for h in range(H):
            par = h % 2
            if h + 1 < H:
                load_head(h + 1, ("v", vt_ev[h]))
            head_last = []
            for qt in range(2):
                if qt == 1 and h + 1 < H:
                    prelude(h + 1)
                ctxw = pt_acquire(2, [])
                prw0 = [pr_free[0]]
                sum_gev = ctx_gev = None
                ll_ev = {}

                def consume(kc):
                    pti = kc % 2
                    eh = op("s", lambda e, p=PT[pti], kc=kc, par=par: e.activation(
                        PH[:, kc * 512:(kc + 1) * 512], p[:], AF.Exp,
                        bias=NEGB[:, par:par + 1]), waits=[("t", sc_ev[kc])],
                        inc="s")
                    jj, ys = y2_slot()
                    yw = y2_state["free"][jj]
                    e32 = op("s", lambda e, p=PT[pti], ys=ys, par=par: e.activation(
                        ys, p[:], AF.Exp, bias=NEGB[:, par:par + 1]), waits=[yw],
                        inc="s")
                    pt_free[pti] = ("s", e32)
                    ll_ev[kc] = op("v", lambda e, ys=ys, kc=kc: e.tensor_tensor(
                        PL[:, kc * 512:(kc + 1) * 512], ys,
                        PH[:, kc * 512:(kc + 1) * 512], op=OP.subtract),
                        waits=[("s", e32)], inc="v")
                    y2_state["free"][jj] = ("v", ll_ev[kc])
                    op("t", lambda e, kc=kc, par=par: e.matmul(
                        PT[2][:], VTH[par][:, kc * 128:(kc + 1) * 128],
                        PH[:, kc * 512:(kc + 1) * 512], start=(kc == 0),
                        stop=False), waits=[("s", eh)] + (ctxw if kc == 0 else []))
                    op("t", lambda e, kc=kc, par=par: e.matmul(
                        PT[2][:], VTL[par][:, kc * 128:(kc + 1) * 128],
                        PH[:, kc * 512:(kc + 1) * 512], start=False, stop=False))
                    op("t", lambda e, kc=kc: e.matmul(
                        PR[0][:], ONESC[:], PH[:, kc * 512:(kc + 1) * 512],
                        start=(kc == 0), stop=False),
                        waits=(prw0 if kc == 0 else []))

                sc_ev = {}
                for kc in range(16):
                    pti = kc % 2
                    w = pt_acquire(pti, [("v", vt_ev[h]), ("v", negb_ev[h])]
                                   if kc < 2 else [])
                    op("t", lambda e, p=PT[pti], k_=KH[par], q_=QHB[par], kc=kc,
                       qt=qt: e.matmul(p[:], k_[:, kc * 128:(kc + 1) * 128],
                                       q_[:, qt * 512:(qt + 1) * 512], start=True,
                                       stop=False), waits=w)
                    op("t", lambda e, p=PT[pti], k_=KH[par], q_=QLB[par], kc=kc,
                       qt=qt: e.matmul(p[:], k_[:, kc * 128:(kc + 1) * 128],
                                       q_[:, qt * 512:(qt + 1) * 512], start=False,
                                       stop=False))
                    sc_ev[kc] = op("t", lambda e, p=PT[pti], k_=KL[par],
                                   q_=QHB[par], kc=kc, qt=qt: e.matmul(
                                       p[:], k_[:, kc * 128:(kc + 1) * 128],
                                       q_[:, qt * 512:(qt + 1) * 512], start=False,
                                       stop=True), inc="t")
                    if kc >= 1:
                        consume(kc - 1)
                consume(15)
                for kc in range(16):
                    ctx_gev = op("t", lambda e, kc=kc, par=par: e.matmul(
                        PT[2][:], VTH[par][:, kc * 128:(kc + 1) * 128],
                        PL[:, kc * 512:(kc + 1) * 512], start=False,
                        stop=(kc == 15)),
                        waits=[("v", ll_ev[15])] if kc == 0 else (),
                        inc="t" if kc == 15 else None) or ctx_gev
                    sum_gev = op("t", lambda e, kc=kc: e.matmul(
                        PR[0][:], ONESC[:], PL[:, kc * 512:(kc + 1) * 512],
                        start=False, stop=(kc == 15)),
                        inc="t" if kc == 15 else None) or sum_gev
                jj, c32 = y2_slot()
                yw = y2_state["free"][jj]
                ce = op("v", lambda e, c=c32: e.tensor_copy(c, PT[2][:]),
                        waits=[("t", ctx_gev), yw], inc="v")
                pt_free[2] = ("v", ce)
                s1 = op("v", lambda e: e.tensor_copy(rA[0:1, 0:512], PR[0][:]),
                        waits=[("t", sum_gev)], inc="v")
                pr_free[0] = ("v", s1)
                s2 = op("v", lambda e: e.reciprocal(rA[0:1, 512:1024],
                                                    rA[0:1, 0:512]),
                        waits=[("v", s1)], inc="v")
                _, rl = row_pair(RMH[0:1, 0:512], RML[0:1, 0:512],
                                 rA[0:1, 512:1024], [("v", s2)])
                bv = bcast512(Z[:, 0:512], RMH[0:1, 0:512], RML[0:1, 0:512], 0,
                              [("v", rl)])
                e1_ = op("v", lambda e, c=c32: e.tensor_tensor(
                    c, c, Z[:, 0:512], op=OP.mult), waits=[("v", bv)], inc="v")
                ch = op("v", lambda e, c=c32, h=h, qt=qt: e.tensor_copy(
                    CTXH[:, h * 1024 + qt * 512:h * 1024 + (qt + 1) * 512], c),
                    waits=[("v", e1_)], inc="v")
                cl = op("g", lambda e, c=c32, h=h, qt=qt: e.tensor_tensor(
                    CTXL[:, h * 1024 + qt * 512:h * 1024 + (qt + 1) * 512], c,
                    CTXH[:, h * 1024 + qt * 512:h * 1024 + (qt + 1) * 512],
                    op=OP.subtract), waits=[("v", ch)], inc="g")
                y2_state["free"][jj] = ("g", cl)
                head_last.append(cl)
            head_cmp[par] = ("g", max(head_last))

        barrier()
        p2s.close()
        barrier()

        p3s = ExitStack()
        WF3 = p3s.enter_context(nc.sbuf_tensor("WF3", [128, 4 * 512], F32))
        WH3 = p3s.enter_context(nc.sbuf_tensor("WH3", [128, 2 * 16 * 256], BF16))
        WL3 = p3s.enter_context(nc.sbuf_tensor("WL3", [128, 2 * 16 * 256], BF16))
        XT3 = p3s.enter_context(nc.sbuf_tensor("XT3", [128, 1024], F32))
        Y3 = p3s.enter_context(nc.sbuf_tensor("Y3", [128, 6 * 512], F32))
        SRH3 = p3s.enter_context(nc.sbuf_tensor("SRH3", [128, 3 * 512], BF16))
        SRL3 = p3s.enter_context(nc.sbuf_tensor("SRL3", [128, 3 * 512], BF16))
        SQH3 = p3s.enter_context(nc.sbuf_tensor("SQH3", [128, 3 * 512], BF16))
        SQL3 = p3s.enter_context(nc.sbuf_tensor("SQL3", [128, 3 * 512], BF16))

        y3_state = {"i": 0, "free": [("v", 0)] * 6}

        def y3_slot():
            j = y3_state["i"] % 6
            y3_state["i"] += 1
            return j, Y3[:, j * 512:(j + 1) * 512]

        class DeferStats:

            def __init__(self, SH, SL, QH_, QL_, slot_fn, state):
                self.SH, self.SL, self.QH, self.QL = SH, SL, QH_, QL_
                self.slot_fn, self.state = slot_fn, state
                self.pend = []
                self.idx = 0
                self.tev = {}

            def prep(self, src32, src_waits, pr_sum, pr_sq, first, last):
                i = self.idx
                self.idx += 1
                sl = slice((i % 3) * 512, (i % 3) * 512 + 512)
                guard = [("t", self.tev.get(i - 3, 0))]
                hh = op("v", lambda e, s_=src32, SH=self.SH, sl=sl:
                        e.tensor_copy(SH[:, sl], s_),
                        waits=list(src_waits) + guard, inc="v")
                ll = op("g", lambda e, s_=src32, SH=self.SH, SL=self.SL, sl=sl:
                        e.tensor_tensor(SL[:, sl], s_, SH[:, sl], op=OP.subtract),
                        waits=[("v", hh)], inc="g")
                jj, ys = self.slot_fn()
                yw = self.state["free"][jj]
                sq = op("v", lambda e, s_=src32, ys=ys: e.tensor_tensor(
                    ys, s_, s_, op=OP.mult), waits=[("g", ll), yw], inc="v")
                qh = op("v", lambda e, ys=ys, QH_=self.QH, sl=sl:
                        e.tensor_copy(QH_[:, sl], ys), waits=[("v", sq)], inc="v")
                ql = op("g", lambda e, ys=ys, QH_=self.QH, QL_=self.QL, sl=sl:
                        e.tensor_tensor(QL_[:, sl], ys, QH_[:, sl],
                                        op=OP.subtract), waits=[("v", qh)], inc="g")
                self.state["free"][jj] = ("g", ql)
                self.pend.append((i, sl, pr_sum, pr_sq, first, last, hh, ll, qh, ql))
                if len(self.pend) > 2:
                    self.flush_one()

            def flush_one(self):
                i, sl, pr_sum, pr_sq, first, last, hh, ll, qh, ql = self.pend.pop(0)
                w0 = [pr_free[pr_sum]] if first else []
                w1 = [pr_free[pr_sq]] if first else []
                op("t", lambda e, p=PR[pr_sum], f=first, SH=self.SH, sl=sl:
                   e.matmul(p[:], ONESC[:], SH[:, sl], start=f, stop=False),
                   waits=[("v", hh)] + w0)
                op("t", lambda e, p=PR[pr_sum], l_=last, SL=self.SL, sl=sl:
                   e.matmul(p[:], ONESC[:], SL[:, sl], start=False, stop=l_),
                   waits=[("g", ll)])
                op("t", lambda e, p=PR[pr_sq], f=first, QH_=self.QH, sl=sl:
                   e.matmul(p[:], ONESC[:], QH_[:, sl], start=f, stop=False),
                   waits=[("v", qh)] + w1)
                tev = op("t", lambda e, p=PR[pr_sq], l_=last, QL_=self.QL, sl=sl:
                         e.matmul(p[:], ONESC[:], QL_[:, sl], start=False, stop=l_),
                         waits=[("g", ql)], inc="t")
                self.tev[i] = tev
                return tev

            def flush(self):
                tev = 0
                while self.pend:
                    tev = self.flush_one()
                return tev

        _, l1 = row_pair(RMH[0:1, 0:1024], RML[0:1, 0:1024], rG[0:1, :], [])
        _, l2 = row_pair(RMH[0:1, 1024:2048], RML[0:1, 1024:2048], rH[0:1, :], [])
        bev3s = []
        for tt in range(2):
            bev3s.append(bcast512(Z[:, tt * 512:(tt + 1) * 512],
                                  RMH[0:1, tt * 512:(tt + 1) * 512],
                                  RML[0:1, tt * 512:(tt + 1) * 512], 0, [("v", l1)]))
            bev3s.append(bcast512(Z[:, 1024 + tt * 512:1024 + (tt + 1) * 512],
                                  RMH[0:1, 1024 + tt * 512:1024 + (tt + 1) * 512],
                                  RML[0:1, 1024 + tt * 512:1024 + (tt + 1) * 512],
                                  1, [("v", l2)]))
        bev3 = max(bev3s)

        wh3_half_free = [("t", 0), ("t", 0)]
        wf3_free = [("v", 0)] * 4
        xt3_free = ("v", 0)
        pti3 = [0]
        last3 = 0

        def stream3(hb):
            half = hb % 2
            hs = ls = 0
            for i in range(NDC // 2):
                slot = i % 4
                dsem = f"dW{slot}"
                ssrc = w_out[i * 256:(i + 1) * 256, hb * 256:hb * 256 + 256]
                src3 = ssrc.rearrange("(t p) c -> p t c", p=128)
                dst3 = WF3[:, slot * 512:(slot + 1) * 512].rearrange(
                    "p (t c) -> p t c", t=2)
                dl = dma_in(dst3, src3, dsem, waits=[wf3_free[slot]])
                src = WF3[:, slot * 512:(slot + 1) * 512]
                dsth = WH3[:, half * 4096 + i * 512:half * 4096 + (i + 1) * 512]
                dstl = WL3[:, half * 4096 + i * 512:half * 4096 + (i + 1) * 512]
                hw = [wh3_half_free[half]] if i == 0 else []
                hs = op("s", lambda e, s_=src, d=dsth: e.copy(d, s_),
                        waits=[(dsem, dl)] + hw, inc="s")
                ls = op("v", lambda e, s_=src, d=dstl, dh=dsth: e.tensor_tensor(
                    d, s_, dh, op=OP.subtract), waits=[("s", hs)], inc="v")
                wf3_free[slot] = ("v", ls)
            return hs, ls

        ds3 = DeferStats(SRH3, SRL3, SQH3, SQL3, y3_slot, y3_state)
        ev3 = {0: stream3(0)}
        for hb in range(8):
            half = hb % 2
            if hb + 1 < 8:
                ev3[hb + 1] = stream3(hb + 1)
            hs, ls = ev3[hb]
            last_g = 0
            for fo_loc in range(2):
                fo = hb * 2 + fo_loc
                dlx = dma_in(XT3[:], x_own[fo * 128:(fo + 1) * 128, :], "dX0",
                             waits=[xt3_free])
                ao_evs = []
                for tt in range(2):
                    pti = pti3[0] % 3
                    pti3[0] += 1
                    w = pt_acquire(pti, [("s", hs), ("v", ls)])
                    passes = [(WH3, CTXH), (WH3, CTXL), (WL3, CTXH)]
                    gev = None
                    for pi, (LB, RB) in enumerate(passes):
                        for dc in range(NDC):
                            first = (pi == 0 and dc == 0)
                            last = (pi == 2 and dc == NDC - 1)
                            lhsT = LB[:, half * 4096 + dc * 256 + fo_loc * 128:
                                      half * 4096 + dc * 256 + fo_loc * 128 + 128]
                            rhs = RB[:, dc * 1024 + tt * 512:dc * 1024 +
                                     (tt + 1) * 512]
                            gev = op("t", lambda e, p=PT[pti], l_=lhsT, r=rhs,
                                     f=first, st=last: e.matmul(p[:], l_, r,
                                                                start=f, stop=st),
                                     waits=w if first else (),
                                     inc="t" if last else None)
                    last_g = gev
                    jj, ys = y3_slot()
                    yw = y3_state["free"][jj]
                    a1 = op("v", lambda e, ys=ys, tt=tt: e.tensor_tensor(
                        ys, XT3[:, tt * 512:(tt + 1) * 512],
                        Z[:, tt * 512:(tt + 1) * 512], op=OP.mult),
                        waits=[("dX0", dlx), yw, ("v", bev3)], inc="v")
                    a2 = op("v", lambda e, ys=ys, tt=tt: e.tensor_tensor(
                        ys, ys, Z[:, 1024 + tt * 512:1024 + (tt + 1) * 512],
                        op=OP.subtract), waits=[("v", a1)], inc="v")
                    a3 = op("v", lambda e, ys=ys, p=PT[pti]: e.tensor_tensor(
                        ys, ys, p[:], op=OP.add), waits=[("v", a2), ("t", gev)],
                        inc="v")
                    pt_free[pti] = ("v", a3)
                    ds3.prep(ys, [("v", a3)], 2 * tt, 2 * tt + 1,
                             first=(fo == 0), last=(fo == 15))
                    ysem = f"dY{jj}"
                    de = dma_out(ao_s[fo * 128:(fo + 1) * 128,
                                      tt * 512:(tt + 1) * 512], ys, ysem,
                                 waits=[("v", cnt["v"]), ("g", cnt["g"])])
                    y3_state["free"][jj] = (ysem, de)
                    ao_evs.append(a1)
                xt3_free = ("v", max(ao_evs))
            wh3_half_free[half] = ("t", last_g)
        last3 = ds3.flush()
        ln_rows(0, 1, 2, [("t", last3)])
        op("v", lambda e: e.tensor_copy(rG[0:1, :], rC[0:1, :]), inc="v")
        op("v", lambda e: e.tensor_tensor(rH[0:1, :], rA[0:1, :], rC[0:1, :],
                                          op=OP.mult), inc="v")
        barrier()
        p3s.close()
        p23.close()
        barrier()

        p4s = ExitStack()
        YSH = p4s.enter_context(nc.sbuf_tensor("YSH", [128, 16 * 512], BF16))
        YSL = p4s.enter_context(nc.sbuf_tensor("YSL", [128, 16 * 512], BF16))
        HH = p4s.enter_context(nc.sbuf_tensor("HH", [128, 8 * 512], BF16))
        HL = p4s.enter_context(nc.sbuf_tensor("HL", [128, 8 * 512], BF16))
        DA = p4s.enter_context(nc.sbuf_tensor("DA", [128, 16 * 512], F32))
        WF4 = p4s.enter_context(nc.sbuf_tensor("WF4", [128, 4 * 512], F32))
        WH4 = p4s.enter_context(nc.sbuf_tensor("WH4", [128, 3 * 16 * 256], BF16))
        XT4 = p4s.enter_context(nc.sbuf_tensor("XT4", [128, 4 * 512], F32))
        Y4 = p4s.enter_context(nc.sbuf_tensor("Y4", [128, 6 * 512], F32))
        SRH4 = p4s.enter_context(nc.sbuf_tensor("SRH4", [128, 3 * 512], BF16))
        SRL4 = p4s.enter_context(nc.sbuf_tensor("SRL4", [128, 3 * 512], BF16))
        SQH4 = p4s.enter_context(nc.sbuf_tensor("SQH4", [128, 3 * 512], BF16))
        SQL4 = p4s.enter_context(nc.sbuf_tensor("SQL4", [128, 3 * 512], BF16))

        y4_state = {"i": 0, "free": [("v", 0)] * 6}

        def y4_slot():
            j = y4_state["i"] % 6
            y4_state["i"] += 1
            return j, Y4[:, j * 512:(j + 1) * 512]

        xt4_state = {"i": 0, "free": [("v", 0)] * 4}

        def xt4_slot():
            j = xt4_state["i"] % 4
            xt4_state["i"] += 1
            return j, XT4[:, j * 512:(j + 1) * 512]

        def make_y(fo, tt, bw):
            ja, at = xt4_slot()
            aw = xt4_state["free"][ja]
            jx, xt = xt4_slot()
            xw = xt4_state["free"][jx]
            d1 = dma_in(at, ao_s[fo * 128:(fo + 1) * 128, tt * 512:(tt + 1) * 512],
                        f"dX{ja}", waits=[aw], q="y")
            d2 = dma_in(xt, x_own[fo * 128:(fo + 1) * 128, tt * 512:(tt + 1) * 512],
                        f"dX{jx}", waits=[xw])
            jy, ys = y4_slot()
            yw = y4_state["free"][jy]
            a1 = op("v", lambda e, ys=ys, at=at: e.tensor_tensor(
                ys, at, Z[:, 1024:1536], op=OP.mult),
                waits=[(f"dX{ja}", d1), yw, bw], inc="v")
            a2 = op("v", lambda e, ys=ys: e.tensor_tensor(
                ys, ys, Z[:, 1536:2048], op=OP.subtract), waits=[("v", a1)], inc="v")
            a3 = op("v", lambda e, ys=ys, xt=xt: e.tensor_tensor(
                ys, ys, xt, op=OP.add), waits=[("v", a2), (f"dX{jx}", d2)], inc="v")
            xt4_state["free"][ja] = ("v", a1)
            xt4_state["free"][jx] = ("v", a3)
            return jy, ys, a3

        QPART = [(0, 8), (8, 8), (16, 8), (24, 8), (32, 8), (40, 4)]

        for tt in range(2):
            _, l1 = row_pair(RMH[0:1, 0:512], RML[0:1, 0:512],
                             rG[0:1, tt * 512:(tt + 1) * 512], [])
            _, l2 = row_pair(RMH[0:1, 512:1024], RML[0:1, 512:1024],
                             rH[0:1, tt * 512:(tt + 1) * 512], [])
            bcast512(Z[:, 1024:1536], RMH[0:1, 0:512], RML[0:1, 0:512], 0,
                     [("v", l1)])
            b2v = bcast512(Z[:, 1536:2048], RMH[0:1, 512:1024], RML[0:1, 512:1024],
                           1, [("v", l2)])
            bw = ("v", b2v)
            ds4 = DeferStats(SRH4, SRL4, SQH4, SQL4, y4_slot, y4_state)
            for fo in range(NDC):
                jy, ys, yv = make_y(fo, tt, bw)
                ds4.prep(ys, [("v", yv)], 0, 1, first=(fo == 0), last=(fo == 15))
                y4_state["free"][jy] = ("g", cnt["g"])
            last4 = ds4.flush()
            ln_rows(0, 1, 1, [("t", last4)])
            m3e = op("v", lambda e: e.tensor_tensor(rF[0:1, 0:512], rA[0:1, 0:512],
                                                    rC[0:1, 0:512], op=OP.mult),
                     inc="v")
            _, l1 = row_pair(RMH[0:1, 0:512], RML[0:1, 0:512], rC[0:1, 0:512],
                             [("v", m3e)])
            _, l2 = row_pair(RMH[0:1, 512:1024], RML[0:1, 512:1024], rF[0:1, 0:512],
                             [("v", m3e)])
            bcast512(Z[:, 0:512], RMH[0:1, 0:512], RML[0:1, 0:512], 0, [("v", l1)])
            b4 = bcast512(Z[:, 512:1024], RMH[0:1, 512:1024], RML[0:1, 512:1024], 1,
                          [("v", l2)])
            ysh_ev = ysl_ev = 0
            for fo in range(NDC):
                jy, ys, yv = make_y(fo, tt, bw)
                a1 = op("v", lambda e, ys=ys: e.tensor_tensor(
                    ys, ys, Z[:, 0:512], op=OP.mult),
                    waits=[("v", yv), ("v", b4)], inc="v")
                a2 = op("v", lambda e, ys=ys: e.tensor_tensor(
                    ys, ys, Z[:, 512:1024], op=OP.subtract), waits=[("v", a1)],
                    inc="v")
                ysh_ev = op("v", lambda e, ys=ys, fo=fo: e.tensor_copy(
                    YSH[:, fo * 512:(fo + 1) * 512], ys), waits=[("v", a2)],
                    inc="v")
                ysl_ev = op("g", lambda e, ys=ys, fo=fo: e.tensor_tensor(
                    YSL[:, fo * 512:(fo + 1) * 512], ys,
                    YSH[:, fo * 512:(fo + 1) * 512], op=OP.subtract),
                    waits=[("v", ysh_ev)], inc="g")
                y4_state["free"][jy] = ("g", ysl_ev)

            wf4_free = [("v", 0)] * 4
            wh4_half_free = [("t", 0), ("t", 0), ("t", 0)]
            pti4 = [0]
            da_ev = {}
            hfree = ("t", 0)

            def stream_half4(wsrc, row0, c0, ntiles, half):
                hs = 0
                for i in range(ntiles // 2):
                    slot = i % 4
                    dsem = f"dW{slot}"
                    ssrc = wsrc[(row0 + 2 * i) * 128:(row0 + 2 * i + 2) * 128,
                                c0:c0 + 256]
                    src3 = ssrc.rearrange("(t p) c -> p t c", p=128)
                    dst3 = WF4[:, slot * 512:(slot + 1) * 512].rearrange(
                        "p (t c) -> p t c", t=2)
                    dl = dma_in(dst3, src3, dsem, waits=[wf4_free[slot]])
                    src = WF4[:, slot * 512:(slot + 1) * 512]
                    dsth = WH4[:, half * 4096 + i * 512:half * 4096 + (i + 1) * 512]
                    hw = [wh4_half_free[half]] if i == 0 else []
                    hs = op("s", lambda e, s_=src, d=dsth: e.copy(d, s_),
                            waits=[(dsem, dl)] + hw, inc="s")
                    wf4_free[slot] = ("s", hs)
                return hs

            tasks = []
            for (ic0, nic) in QPART:
                for ib in range(nic // 2):
                    tasks.append(("g", ic0 + ib * 2))
                    tasks.append(("u", ic0 + ib * 2))
                for hob in range(8):
                    tasks.append(("d", ic0, nic, hob))

            def stream_task(k):
                t = tasks[k]
                half = k % 3
                if t[0] == "g":
                    return stream_half4(gate_wt, 0, t[1] * 128, NDC, half)
                if t[0] == "u":
                    return stream_half4(up_wt, 0, t[1] * 128, NDC, half)
                return stream_half4(down_wt, t[1], t[3] * 256, t[2], half)

            sev_t = {0: stream_task(0), 1: stream_task(1)}
            gate_res = None
            h_ev = {}
            hseen = {}
            for k, t in enumerate(tasks):
                if k + 2 < len(tasks):
                    sev_t[k + 2] = stream_task(k + 2)
                hs = sev_t.pop(k)
                half = k % 3
                if t[0] in ("g", "u"):
                    icb = t[1]
                    scl, fn = (GSC, AF.Silu) if t[0] == "g" else (USC, AF.Copy)
                    g_evs = []
                    gev = None
                    for io in range(2):
                        ic = icb + io
                        pti = pti4[0] % 3
                        pti4[0] += 1
                        w = pt_acquire(pti, [("s", hs), ("v", ysh_ev),
                                             ("g", ysl_ev)])
                        gev = None
                        for pi, RB in enumerate((YSH, YSL)):
                            for dc in range(NDC):
                                first = (pi == 0 and dc == 0)
                                last = (pi == 1 and dc == NDC - 1)
                                lhsT = WH4[:, half * 4096 + dc * 256 + io * 128:
                                           half * 4096 + dc * 256 + io * 128 + 128]
                                rhs = RB[:, dc * 512:(dc + 1) * 512]
                                gev = op("t", lambda e, p=PT[pti], l_=lhsT, r=rhs,
                                         f=first, st=last: e.matmul(
                                             p[:], l_, r, start=f, stop=st),
                                         waits=w if first else (),
                                         inc="t" if last else None)
                        jy, ys = y4_slot()
                        yw = y4_state["free"][jy]
                        se = op("s", lambda e, ys=ys, p=PT[pti], scl=scl, ic=ic,
                                fn=fn: e.activation(ys, p[:], fn,
                                                    scale=scl[:, ic:ic + 1]),
                                waits=[("t", gev), yw], inc="s")
                        pt_free[pti] = ("s", se)
                        g_evs.append((jy, ys, se))
                    wh4_half_free[half] = ("t", gev)
                    if t[0] == "g":
                        gate_res = g_evs
                    else:
                        ic0q = icb - (icb % 12) if icb < 36 else 36
                        for q0, qn in QPART:
                            if q0 <= icb < q0 + qn:
                                ic0q = q0
                        for io in range(2):
                            jg, gys, ge = gate_res[io]
                            ju, uys, ue = g_evs[io]
                            pos = icb + io - ic0q
                            m1 = op("v", lambda e, g_=gys, u_=uys:
                                    e.tensor_tensor(g_, g_, u_, op=OP.mult),
                                    waits=[("s", ge), ("s", ue)], inc="v")
                            h_ = op("v", lambda e, g_=gys, pos=pos:
                                    e.tensor_copy(
                                        HH[:, pos * 512:(pos + 1) * 512], g_),
                                    waits=[("v", m1), hfree], inc="v")
                            l_ = op("g", lambda e, g_=gys, pos=pos:
                                    e.tensor_tensor(
                                        HL[:, pos * 512:(pos + 1) * 512], g_,
                                        HH[:, pos * 512:(pos + 1) * 512],
                                        op=OP.subtract),
                                    waits=[("v", h_)], inc="g")
                            y4_state["free"][jg] = ("g", l_)
                            y4_state["free"][ju] = ("v", m1)
                            hseen[ic0q] = hseen.get(ic0q, 0) + 1
                            if hseen[ic0q] == dict(QPART)[ic0q]:
                                h_ev[ic0q] = (cnt["v"], cnt["g"])
                else:
                    _, ic0, nic, hob = t
                    h_hi_ev, h_lo_ev = h_ev[ic0]
                    gevd = None
                    for ho_loc in range(2):
                        ho = hob * 2 + ho_loc
                        pti = pti4[0] % 3
                        pti4[0] += 1
                        w = pt_acquire(pti, [("s", hs), ("v", h_hi_ev),
                                             ("g", h_lo_ev)])
                        gevd = None
                        for pi, RB in enumerate((HH, HL)):
                            for i in range(nic):
                                first = (pi == 0 and i == 0)
                                last = (pi == 1 and i == nic - 1)
                                lhsT = WH4[:, half * 4096 + i * 256 + ho_loc * 128:
                                           half * 4096 + i * 256 + ho_loc * 128
                                           + 128]
                                rhs = RB[:, i * 512:(i + 1) * 512]
                                gevd = op("t", lambda e, p=PT[pti], l_=lhsT, r=rhs,
                                          f=first, st=last: e.matmul(
                                              p[:], l_, r, start=f, stop=st),
                                          waits=w if first else (),
                                          inc="t" if last else None)
                        dsl = DA[:, ho * 512:(ho + 1) * 512]
                        daw = da_ev.get(ho, ("v", 0))
                        if ic0 == 0:
                            de = op("v", lambda e, d=dsl, p=PT[pti]: e.tensor_copy(
                                d, p[:]), waits=[("t", gevd), daw], inc="v")
                        else:
                            de = op("v", lambda e, d=dsl, p=PT[pti]:
                                    e.tensor_tensor(d, d, p[:], op=OP.add),
                                    waits=[("t", gevd), daw], inc="v")
                        if ic0 == QPART[-1][0]:
                            de = op("v", lambda e, d=dsl, ho=ho: e.tensor_scalar(
                                d, d, DSC[:, ho:ho + 1], None, op0=OP.mult),
                                waits=[("v", de)], inc="v")
                        da_ev[ho] = ("v", de)
                        pt_free[pti] = ("v", de)
                    wh4_half_free[half] = ("t", gevd)
                    if hob == 7:
                        hfree = ("t", gevd)

            ds4b = DeferStats(SRH4, SRL4, SQH4, SQL4, y4_slot, y4_state)
            for ho in range(NDC):
                ds4b.prep(DA[:, ho * 512:(ho + 1) * 512], [da_ev[ho]], 0, 1,
                          first=(ho == 0), last=(ho == 15))
            last4 = ds4b.flush()
            ln_rows(0, 1, 1, [("t", last4)])
            m4e = op("v", lambda e: e.tensor_tensor(rF[0:1, 0:512], rA[0:1, 0:512],
                                                    rC[0:1, 0:512], op=OP.mult),
                     inc="v")
            _, l1 = row_pair(RMH[0:1, 0:512], RML[0:1, 0:512], rC[0:1, 0:512],
                             [("v", m4e)])
            _, l2 = row_pair(RMH[0:1, 512:1024], RML[0:1, 512:1024], rF[0:1, 0:512],
                             [("v", m4e)])
            bcast512(Z[:, 0:512], RMH[0:1, 0:512], RML[0:1, 0:512], 0, [("v", l1)])
            b4 = bcast512(Z[:, 512:1024], RMH[0:1, 512:1024], RML[0:1, 512:1024], 1,
                          [("v", l2)])
            for fo in range(NDC):
                jy, ys, yv = make_y(fo, tt, bw)
                jd, dt_ = y4_slot()
                dw = y4_state["free"][jd]
                f1 = op("v", lambda e, d=dt_, fo=fo: e.tensor_tensor(
                    d, DA[:, fo * 512:(fo + 1) * 512], Z[:, 0:512], op=OP.mult),
                    waits=[dw, ("v", b4), da_ev[fo]], inc="v")
                f2 = op("v", lambda e, d=dt_: e.tensor_tensor(
                    d, d, Z[:, 512:1024], op=OP.subtract), waits=[("v", f1)],
                    inc="v")
                f3 = op("v", lambda e, d=dt_, ys=ys: e.tensor_tensor(
                    d, d, ys, op=OP.add), waits=[("v", f2), ("v", yv)], inc="v")
                ysem = f"dY{jd}"
                de = dma_out(out_fm[fo * 128:(fo + 1) * 128,
                                    tt * 512:(tt + 1) * 512], dt_, ysem,
                             waits=[("v", f3)])
                y4_state["free"][jd] = (ysem, de)
                y4_state["free"][jy] = ("v", f3)
            barrier()

        p4s.close()
        fin = [(nm, cnt[nm]) for nm in SEMH if nm.startswith(("dE", "dY"))
               and cnt[nm] > 0]
        def _final(e, fin=tuple(fin)):
            for nm, v in fin:
                e.wait_ge(SEMH[nm], v)
        prog["y"].append(_final)

        block = es.enter_context(nc.Block())
        block.tensor(lambda e: [f(e) for f in prog["t"]])
        block.vector(lambda e: [f(e) for f in prog["v"]])
        block.scalar(lambda e: [f(e) for f in prog["s"]])
        block.gpsimd(lambda e: [f(e) for f in prog["g"]])
        block.sync(lambda e: [f(e) for f in prog["y"]])

    return nc


def kernel(L0_attn_X, L0_attn_ln_w, L0_attn_ln_b, L0_attn_W_in, L0_attn_W_out,
           L0_ffn_gate_w, L0_ffn_up_w, L0_ffn_down_w, L0_ffn_gate_s, L0_ffn_up_s,
           L0_ffn_down_s):
    X = np.asarray(L0_attn_X, dtype=np.float32)
    W_in = np.asarray(L0_attn_W_in, dtype=np.float32)
    W_out = np.ascontiguousarray(np.asarray(L0_attn_W_out, dtype=np.float32))
    gate_w = np.asarray(L0_ffn_gate_w, dtype=np.float32)
    up_w = np.asarray(L0_ffn_up_w, dtype=np.float32)
    down_w = np.asarray(L0_ffn_down_w, dtype=np.float32)

    perm64 = np.concatenate([np.arange(0, 64, 2), np.arange(1, 64, 2),
                             np.arange(64, 128)])
    colp = np.empty(3 * D, dtype=np.int64)
    for h in range(H):
        colp[h * 128:(h + 1) * 128] = h * 128 + perm64
        colp[D + h * 128:D + (h + 1) * 128] = D + h * 128 + perm64
    colp[2 * D:] = np.arange(2 * D, 3 * D)
    w_in_p = np.ascontiguousarray(W_in[:, colp])
    gate_wt = np.ascontiguousarray(gate_w.T)
    up_wt = np.ascontiguousarray(up_w.T)
    down_wt = np.ascontiguousarray(down_w.T)
    lnw = np.ascontiguousarray(np.asarray(L0_attn_ln_w, np.float32).reshape(16, 128).T)
    lnb = np.ascontiguousarray(np.asarray(L0_attn_ln_b, np.float32).reshape(16, 128).T)
    gs = np.ascontiguousarray(np.asarray(L0_ffn_gate_s, np.float32).reshape(44, 128).T)
    us = np.ascontiguousarray(np.asarray(L0_ffn_up_s, np.float32).reshape(44, 128).T)
    ds = np.ascontiguousarray(np.asarray(L0_ffn_down_s, np.float32).reshape(16, 128).T)
    cos32, sin32 = _rope_tables()
    import ml_dtypes
    eye = np.eye(128, dtype=ml_dtypes.bfloat16)
    ones1 = np.ones((1, 128), dtype=ml_dtypes.bfloat16)
    onesc = np.ones((128, 1), dtype=ml_dtypes.bfloat16)

    Xfm = [np.ascontiguousarray(X[b].T) for b in range(4)]

    nc = _build()
    in_maps = []
    for c in range(8):
        b, q0 = c // 2, (c % 2) * 1024
        o0 = 1024 - q0
        cq_ = cos32[:, q0:q0 + 1024]
        sq_ = sin32[:, q0:q0 + 1024]
        ck_ = np.concatenate([cos32[:, q0:q0 + 1024], cos32[:, o0:o0 + 1024]], axis=1)
        sk_ = np.concatenate([sin32[:, q0:q0 + 1024], sin32[:, o0:o0 + 1024]], axis=1)
        in_maps.append({
            "x_own": np.ascontiguousarray(Xfm[b][:, q0:q0 + 1024]),
            "x_oth": np.ascontiguousarray(Xfm[b][:, o0:o0 + 1024]),
            "w_in": w_in_p, "w_out": W_out,
            "gate_wt": gate_wt, "up_wt": up_wt, "down_wt": down_wt,
            "lnw": lnw, "lnb": lnb, "gate_s": gs, "up_s": us, "down_s": ds,
            "cosqd": np.ascontiguousarray(np.concatenate([cq_, cq_], axis=0)),
            "sinqd": np.ascontiguousarray(np.concatenate([sq_, sq_], axis=0)),
            "coskd": np.ascontiguousarray(np.concatenate([ck_, ck_], axis=0)),
            "sinkd": np.ascontiguousarray(np.concatenate([sk_, sk_], axis=0)),
            "eye": eye, "ones1": ones1, "onesc": onesc,
        })
    res = run_bass_kernel_spmd(nc, in_maps, core_ids=list(range(8)))
    out = np.empty((4, 2048, 2048), dtype=np.float32)
    for c in range(8):
        b, q0 = c // 2, (c % 2) * 1024
        out[b, q0:q0 + 1024, :] = np.asarray(res.results[c]["out_fm"]).T
    return out
